# version 02
# baseline (speedup 1.0000x reference)
from contextlib import ExitStack
import numpy as np
import ml_dtypes
import concourse.bass as bass
import concourse.mybir as mybir
from concourse.bass_utils import run_bass_kernel_spmd

F32 = mybir.dt.float32
BF16 = mybir.dt.bfloat16
AF = mybir.ActivationFunctionType
ALU = mybir.AluOpType
AX = mybir.AxisListType

D = 2048
NCH = 16
IF = 5632
ICN = 44
T_OWN = 2048
HALO = 128
T_ALL = 2176
NBLK = 16
EPS = 1e-5
NEG = -1.0e30

GROUPS = [(0, 512), (512, 512), (1024, 512), (1536, 512), (2048, 128)]
LN_STEPS = [(i * 256, 256) for i in range(8)] + [(2048, 128)]
GROUPS_OWN = [(0, 512), (512, 512), (1024, 512), (1536, 512)]
LN_STEPS_OWN = [(i * 256, 256) for i in range(8)]



def _clear_sems(nc, sems):
    with nc.Block() as blk:
        @blk.gpsimd
        def _(g):
            for sem in sems:
                g.sem_clear(sem)


def _ln_phase(nc, block_tensors, xin_view3, xn_out_sb, t_total, steps, consts, tag,
              dram_out=None, sync_hook=None, pool_hook=None):
    ones128 = consts["ones128"]
    t = block_tensors
    ns = len(steps)
    with (
        nc.semaphore(f"{tag}_xld0") as s_xld0,
        nc.semaphore(f"{tag}_xld1") as s_xld1,
        nc.semaphore(f"{tag}_v2p") as s_v2p,
        nc.semaphore(f"{tag}_pst") as s_pst,
        nc.semaphore(f"{tag}_var") as s_var,
        nc.semaphore(f"{tag}_sd") as s_sd,
        nc.semaphore(f"{tag}_nrm") as s_nrm,
        nc.semaphore(f"{tag}_ost0") as s_ost0,
        nc.semaphore(f"{tag}_ost1") as s_ost1,
    ):
        _clear_sems(nc, [s_xld0, s_xld1, s_v2p, s_pst, s_var, s_sd, s_nrm, s_ost0, s_ost1])
        block = None
        block_cm = nc.Block()
        block = block_cm.__enter__()
        s_xld = [s_xld0, s_xld1]
        s_ost = [s_ost0, s_ost1]

        @block.sync
        def _(sync):
            for s, (t0, L) in enumerate(steps):
                if s >= 2:
                    sync.wait_ge(s_nrm, s - 1)
                xst = t["xst"][s % 2]
                v = xst.rearrange("p (c t) -> p c t", t=256)
                sync.dma_start(out=v[:, 0:8, 0:L], in_=xin_view3[:, 0:8, t0:t0 + L]).then_inc(s_xld[s % 2], 16)
                sync.dma_start(out=v[:, 8:16, 0:L], in_=xin_view3[:, 8:16, t0:t0 + L]).then_inc(s_xld[s % 2], 16)
                if sync_hook is not None:
                    sync_hook(sync, s)

        @block.vector
        def _(vector):
            for s, (t0, L) in enumerate(steps):
                xst = t["xst"][s % 2]
                xv = xst.rearrange("p (c t) -> p c t", t=256)[:, :, 0:L]
                xcv = t["xc"].ap().rearrange("p (c t) -> p c t", t=256)[:, :, 0:L]
                xqv = t["xsq"].ap().rearrange("p (c t) -> p c t", t=256)[:, :, 0:L]
                vector.wait_ge(s_xld[s % 2], 32 * (s // 2 + 1))
                vector.tensor_copy(xcv, xv)
                vector.tensor_tensor(xqv, xcv, xcv, ALU.mult).then_inc(s_v2p, 1)
                vector.wait_ge(s_pst, s + 1)
                rows = t["rows"]
                m = rows[:, 0:256]
                t1 = rows[:, 256:512]
                sd = rows[:, 512:768]
                r = rows[:, 768:1024]
                vector.tensor_scalar_mul(m[:, 0:L], t["pstat"][:, 0:L], 1.0 / D)
                vector.tensor_scalar_mul(t1[:, 0:L], t["pstat"][:, 256:256 + L], 1.0 / D).then_inc(s_var, 1)
                vector.tensor_tensor(sd[:, 0:L], m[:, 0:L], m[:, 0:L], ALU.mult)
                vector.tensor_sub(t1[:, 0:L], t1[:, 0:L], sd[:, 0:L]).then_inc(s_var, 1)
                vector.wait_ge(s_sd, s + 1)
                vector.reciprocal(r[:, 0:L], sd[:, 0:L])
                tmpv = t["tmp"].ap().rearrange("p (c t) -> p c t", t=256)[:, :, 0:L]
                mb = m[:, 0:L].unsqueeze(1).broadcast_to([128, 16, L])
                rb = r[:, 0:L].unsqueeze(1).broadcast_to([128, 16, L])
                vector.tensor_sub(tmpv, xv, mb)
                if dram_out is None:
                    xout = xn_out_sb.ap().rearrange("p (c t) -> p c t", t=t_total)[:, :, t0:t0 + L]
                else:
                    if s >= 2:
                        vector.wait_ge(s_ost[s % 2], 64 * ((s - 2) // 2 + 1))
                    xout = xn_out_sb.ap().rearrange("p (b c t) -> p b c t", b=2, c=NCH)[:, s % 2, :, 0:L]
                vector.tensor_tensor(xout, tmpv, rb, ALU.mult).then_inc(s_nrm, 1)

        @block.tensor
        def _(tensor):
            for s, (t0, L) in enumerate(steps):
                tensor.wait_ge(s_v2p, s + 1)
                if s >= 1:
                    tensor.wait_ge(s_var, 2 * s - 1)
                xcv = t["xc"].ap().rearrange("p (c t) -> p c t", t=256)
                xqv = t["xsq"].ap().rearrange("p (c t) -> p c t", t=256)
                for c in range(NCH):
                    tensor.matmul(t["pstat"][:, 0:L], ones128[:, :], xcv[:, c, 0:L],
                                  start=(c == 0), stop=(c == NCH - 1))
                for c in range(NCH):
                    mm = tensor.matmul(t["pstat"][:, 256:256 + L], ones128[:, :], xqv[:, c, 0:L],
                                       start=(c == 0), stop=(c == NCH - 1))
                mm.then_inc(s_pst, 1)

        @block.scalar
        def _(scalar):
            for s, (t0, L) in enumerate(steps):
                scalar.wait_ge(s_var, 2 * s + 2)
                rows = t["rows"]
                scalar.activation(rows[:, 512:512 + L], rows[:, 256:256 + L], AF.Sqrt,
                                  bias=consts["eps_col"][:, 0:1], scale=1.0).then_inc(s_sd, 1)

        if dram_out is not None or pool_hook is not None:
            @block.gpsimd
            def _(g_eng):
                o3 = None if dram_out is None else dram_out.ap().rearrange("(c p) t -> p c t", p=128)
                for s, (t0, L) in enumerate(steps):
                    if pool_hook is not None:
                        pool_hook(g_eng, s)
                    if dram_out is not None:
                        g_eng.wait_ge(s_nrm, s + 1)
                        v = xn_out_sb.ap().rearrange("p (b c t) -> p b c t", b=2, c=NCH)[:, s % 2]
                        for q in range(4):
                            g_eng.dma_start(o3[:, q * 4:(q + 1) * 4, t0:t0 + L],
                                            v[:, q * 4:(q + 1) * 4, 0:L]).then_inc(s_ost[s % 2], 16)
                if dram_out is not None:
                    g_eng.wait_ge(s_ost[0], 64 * ((ns + 1) // 2))
                    g_eng.wait_ge(s_ost[1], 64 * (ns // 2))
        block_cm.__exit__(None, None, None)


def _gateup_phase(nc, xn_sb, t_total, groups, gwT, uwT, h_dram, consts, tag,
                  col_off=0):
    gs_sb, us_sb = consts["gs_sb"], consts["us_sb"]
    NSUP = 22
    n_per_sup = 2 * len(groups)
    NIT = NSUP * n_per_sup
    with (
        nc.sbuf_tensor(f"{tag}_gwst", [128, 2 * 4096], F32) as gwst,
        nc.sbuf_tensor(f"{tag}_uwst", [128, 2 * 4096], F32) as uwst,
        nc.sbuf_tensor(f"{tag}_gwbf", [128, 2 * 4096], BF16) as gwbf,
        nc.sbuf_tensor(f"{tag}_uwbf", [128, 2 * 4096], BF16) as uwbf,
        nc.sbuf_tensor(f"{tag}_silu", [128, 2 * 512], F32) as silu_sb,
        nc.sbuf_tensor(f"{tag}_hsb", [128, 2 * 512], BF16) as h_sb,
        nc.psum_tensor(f"{tag}_pG", [128, 2 * 512], F32) as pG,
        nc.psum_tensor(f"{tag}_pU", [128, 2 * 512], F32) as pU,
        nc.semaphore(f"{tag}_wld0") as s_wld0,
        nc.semaphore(f"{tag}_wld1") as s_wld1,
        nc.semaphore(f"{tag}_wcv") as s_wcv,
        nc.semaphore(f"{tag}_pmm") as s_pmm,
        nc.semaphore(f"{tag}_sil") as s_sil,
        nc.semaphore(f"{tag}_hm") as s_hm,
        nc.semaphore(f"{tag}_hst0") as s_hst0,
        nc.semaphore(f"{tag}_hst1") as s_hst1,
    ):
        _clear_sems(nc, [s_wld0, s_wld1, s_wcv, s_pmm, s_sil, s_hm, s_hst0, s_hst1])
        block_cm = nc.Block()
        block = block_cm.__enter__()
        gw3 = gwT.ap().rearrange("(c p) i -> p c i", p=128)
        uw3 = uwT.ap().rearrange("(c p) i -> p c i", p=128)
        s_wld = [s_wld0, s_wld1]
        s_hst = [s_hst0, s_hst1]

        @block.sync
        def _(sync):
            for s2 in range(NSUP):
                if s2 >= 2:
                    sync.wait_ge(s_pmm, n_per_sup * (s2 - 1))
                    sync.wait_ge(s_wcv, 2 * (s2 - 1))
                i0 = s2 * 256
                par = s2 % 2
                for (st, w3) in ((gwst, gw3), (uwst, uw3)):
                    v = st.ap().rearrange("p (b c i) -> p b c i", b=2, c=16)[:, par]
                    sync.dma_start(out=v[:, 0:8, :], in_=w3[:, 0:8, i0:i0 + 256]).then_inc(s_wld[par], 16)
                    sync.dma_start(out=v[:, 8:16, :], in_=w3[:, 8:16, i0:i0 + 256]).then_inc(s_wld[par], 16)

        @block.vector
        def _(vector):
            n = 0
            for s2 in range(NSUP):
                for m in range(n_per_sup):
                    ic_loc = m // len(groups)
                    g = m % len(groups)
                    L = groups[g][1]
                    ic = s2 * 2 + ic_loc
                    p2 = n % 2
                    vector.wait_ge(s_sil, n + 1)
                    if n >= 2:
                        vector.wait_ge(s_hst[n % 2], 16 * ((n - 2) // 2 + 1))
                    vector.scalar_tensor_tensor(
                        h_sb[:, p2 * 512:p2 * 512 + L], pU[:, p2 * 512:p2 * 512 + L],
                        us_sb[:, ic:ic + 1], silu_sb[:, p2 * 512:p2 * 512 + L],
                        ALU.mult, ALU.mult).then_inc(s_hm, 1)
                    n += 1

        @block.tensor
        def _(tensor):
            n = 0
            for s2 in range(NSUP):
                par = s2 % 2
                tensor.wait_ge(s_wcv, 2 * (s2 + 1))
                for m in range(n_per_sup):
                    ic_loc = m // len(groups)
                    g = m % len(groups)
                    t0, L = groups[g]
                    p2 = n % 2
                    if n >= 2:
                        tensor.wait_ge(s_sil, n - 1)
                        tensor.wait_ge(s_hm, n - 1)
                    for c in range(NCH):
                        lhs = gwbf[:, par * 4096 + c * 256 + ic_loc * 128: par * 4096 + c * 256 + ic_loc * 128 + 128]
                        rhs = xn_sb[:, c * t_total + t0: c * t_total + t0 + L]
                        tensor.matmul(pG[:, p2 * 512:p2 * 512 + L], lhs, rhs,
                                      start=(c == 0), stop=(c == NCH - 1))
                    for c in range(NCH):
                        lhs = uwbf[:, par * 4096 + c * 256 + ic_loc * 128: par * 4096 + c * 256 + ic_loc * 128 + 128]
                        rhs = xn_sb[:, c * t_total + t0: c * t_total + t0 + L]
                        mm = tensor.matmul(pU[:, p2 * 512:p2 * 512 + L], lhs, rhs,
                                           start=(c == 0), stop=(c == NCH - 1))
                    mm.then_inc(s_pmm, 1)
                    n += 1

        @block.scalar
        def _(scalar):
            n = 0
            for s2 in range(NSUP):
                for m in range(n_per_sup):
                    ic_loc = m // len(groups)
                    g = m % len(groups)
                    L = groups[g][1]
                    ic = s2 * 2 + ic_loc
                    p2 = n % 2
                    scalar.wait_ge(s_pmm, n + 1)
                    if n >= 2:
                        scalar.wait_ge(s_hm, n - 1)
                    scalar.activation(silu_sb[:, p2 * 512:p2 * 512 + L], pG[:, p2 * 512:p2 * 512 + L],
                                      AF.Silu, bias=0.0, scale=gs_sb[:, ic:ic + 1]).then_inc(s_sil, 1)
                    n += 1

        @block.gpsimd
        def _(g_eng):
            def conv(s2):
                par = s2 % 2
                g_eng.wait_ge(s_wld[par], 64 * (s2 // 2 + 1))
                if s2 >= 2:
                    g_eng.wait_ge(s_pmm, n_per_sup * (s2 - 1))
                g_eng.tensor_copy(gwbf[:, par * 4096:(par + 1) * 4096], gwst[:, par * 4096:(par + 1) * 4096]).then_inc(s_wcv, 1)
                g_eng.tensor_copy(uwbf[:, par * 4096:(par + 1) * 4096], uwst[:, par * 4096:(par + 1) * 4096]).then_inc(s_wcv, 1)
            conv(0)
            conv(1)
            n = 0
            for s2 in range(NSUP):
                for m in range(n_per_sup):
                    ic_loc = m // len(groups)
                    g = m % len(groups)
                    t0, L = groups[g]
                    ic = s2 * 2 + ic_loc
                    p2 = n % 2
                    g_eng.wait_ge(s_hm, n + 1)
                    g_eng.dma_start(h_dram.ap()[ic * 128:(ic + 1) * 128, col_off + t0: col_off + t0 + L],
                                    h_sb[:, p2 * 512:p2 * 512 + L]).then_inc(s_hst[n % 2], 16)
                    if m == n_per_sup - 1 and s2 + 2 < NSUP:
                        conv(s2 + 2)
                    n += 1
            g_eng.wait_ge(s_hst[0], 16 * (NIT // 2))
            g_eng.wait_ge(s_hst[1], 16 * (NIT // 2))
        block_cm.__exit__(None, None, None)


def _down_phase(nc, dwT, h_dram, x_view3, xres_out, groups, consts, tag,
                h_col_off=0, out_col_off=0):
    ds2_sb = consts["ds2_sb"]
    ng = len(groups)
    NQ = 4
    with (
        nc.sbuf_tensor(f"{tag}_dwst", [128, 2 * 2048], F32) as dwst,
        nc.sbuf_tensor(f"{tag}_dwbf", [128, 2 * 44 * 512], BF16) as dwbf,
        nc.sbuf_tensor(f"{tag}_hch", [128, 4 * 512], BF16) as hch,
        nc.sbuf_tensor(f"{tag}_xst", [128, 2 * 4 * 512], F32) as xst,
        nc.sbuf_tensor(f"{tag}_x1o", [128, 2 * 512], F32) as x1o,
        nc.psum_tensor(f"{tag}_pD", [128, 4 * 512], F32) as pD,
        nc.semaphore(f"{tag}_dwld0") as s_dwld0,
        nc.semaphore(f"{tag}_dwld1") as s_dwld1,
        nc.semaphore(f"{tag}_dwcv") as s_dwcv,
        nc.semaphore(f"{tag}_xlda") as s_xlda,
        nc.semaphore(f"{tag}_xldb") as s_xldb,
        nc.semaphore(f"{tag}_hld0") as s_hld0,
        nc.semaphore(f"{tag}_hld1") as s_hld1,
        nc.semaphore(f"{tag}_hld2") as s_hld2,
        nc.semaphore(f"{tag}_hld3") as s_hld3,
        nc.semaphore(f"{tag}_pic") as s_pic,
        nc.semaphore(f"{tag}_ev") as s_ev,
        nc.semaphore(f"{tag}_x1st0") as s_x1st0,
        nc.semaphore(f"{tag}_x1st1") as s_x1st1,
    ):
        _clear_sems(nc, [s_dwld0, s_dwld1, s_dwcv, s_xlda, s_xldb, s_hld0, s_hld1,
                         s_hld2, s_hld3, s_pic, s_ev, s_x1st0, s_x1st1])
        block_cm = nc.Block()
        block = block_cm.__enter__()
        dw3 = dwT.ap().rearrange("(c p) d -> p c d", p=128)
        s_hld = [s_hld0, s_hld1, s_hld2, s_hld3]
        s_dwld = [s_dwld0, s_dwld1]
        s_xld = [s_xlda, s_xldb]
        s_x1st = [s_x1st0, s_x1st1]

        def _dwload(sync, Qt, j):
            jj = Qt * 11 + j
            if jj >= 2:
                sync.wait_ge(s_dwcv, jj - 1)
            sync.dma_start(
                out=dwst.ap().rearrange("p (b c d) -> p b c d", b=2, c=4)[:, jj % 2],
                in_=dw3[:, j * 4:(j + 1) * 4, Qt * 512:(Qt + 1) * 512],
            ).then_inc(s_dwld[jj % 2], 16)

        @block.scalar
        def _(scalar):
            for Qt in range(NQ):
                for j in range(11):
                    _dwload(scalar, Qt, j)

        @block.sync
        def _(sync):
            for Q in range(NQ):
                for grp in range(ng):
                    gi = Q * ng + grp
                    t0, L = groups[grp]
                    if gi >= 2:
                        sync.wait_ge(s_ev, 4 * (gi - 1))
                    xv = xst.ap().rearrange("p (b c t) -> p b c t", b=2, c=4)[:, gi % 2]
                    sync.dma_start(out=xv[:, :, 0:L],
                                   in_=x_view3[:, Q * 4:(Q + 1) * 4, t0:t0 + L]).then_inc(s_xld[gi % 2], 16)
                    for ic in range(ICN):
                        n_ic = gi * ICN + ic
                        if n_ic >= 4:
                            sync.wait_ge(s_pic, n_ic - 3)
                        sync.dma_start(out=hch[:, (ic % 4) * 512:(ic % 4) * 512 + L],
                                       in_=h_dram.ap()[ic * 128:(ic + 1) * 128,
                                                       h_col_off + t0:h_col_off + t0 + L]).then_inc(s_hld[ic % 4], 16)

        @block.vector
        def _(vector):
            for Q in range(NQ):
                for grp in range(ng):
                    gi = Q * ng + grp
                    t0, L = groups[grp]
                    vector.wait_ge(s_pic, ICN * (gi + 1))
                    vector.wait_ge(s_xld[gi % 2], 16 * (gi // 2 + 1))
                    for dc in range(4):
                        idx = 4 * gi + dc
                        if idx >= 2:
                            vector.wait_ge(s_x1st[idx % 2], 16 * ((idx - 2) // 2 + 1))
                        dcg = Q * 4 + dc
                        xv = xst.ap().rearrange("p (b c t) -> p b c t", b=2, c=4)[:, gi % 2, dc, 0:L]
                        vector.scalar_tensor_tensor(
                            x1o[:, (idx % 2) * 512:(idx % 2) * 512 + L],
                            pD[:, dc * 512:dc * 512 + L],
                            ds2_sb[:, dcg:dcg + 1], xv, ALU.mult, ALU.add).then_inc(s_ev, 1)

        @block.tensor
        def _(tensor):
            for Q in range(NQ):
                par = Q % 2
                for grp in range(ng):
                    gi = Q * ng + grp
                    t0, L = groups[grp]
                    for ic in range(ICN):
                        n_ic = gi * ICN + ic
                        tensor.wait_ge(s_hld[ic % 4], 16 * (n_ic // 4 + 1))
                        if ic == 0:
                            tensor.wait_ge(s_dwcv, 11 * (Q + 1))
                        for dc in range(4):
                            if ic == 0 and gi >= 1:
                                tensor.wait_ge(s_ev, 4 * (gi - 1) + dc + 1)
                            lhs = dwbf[:, par * 22528 + ic * 512 + dc * 128: par * 22528 + ic * 512 + dc * 128 + 128]
                            mm = tensor.matmul(pD[:, dc * 512:dc * 512 + L],
                                               lhs, hch[:, (ic % 4) * 512:(ic % 4) * 512 + L],
                                               start=(ic == 0), stop=(ic == ICN - 1))
                        mm.then_inc(s_pic, 1)

        @block.gpsimd
        def _(g_eng):
            def conv(Qt, j):
                jj = Qt * 11 + j
                g_eng.wait_ge(s_dwld[jj % 2], 16 * (jj // 2 + 1))
                if Qt >= 2 and j == 0:
                    g_eng.wait_ge(s_pic, ICN * ng * (Qt - 1))
                g_eng.tensor_copy(
                    dwbf.ap().rearrange("p (b c d) -> p b c d", b=2, c=44, d=512)[:, Qt % 2, j * 4:(j + 1) * 4, :],
                    dwst.ap().rearrange("p (b c d) -> p b c d", b=2, c=4)[:, jj % 2]).then_inc(s_dwcv, 1)

            for j in range(11):
                conv(0, j)
            for j in range(11):
                conv(1, j)
            for Q in range(NQ):
                for grp in range(ng):
                    gi = Q * ng + grp
                    t0, L = groups[grp]
                    for dc in range(4):
                        idx = 4 * gi + dc
                        dcg = Q * 4 + dc
                        g_eng.wait_ge(s_ev, idx + 1)
                        g_eng.dma_start(
                            xres_out.ap()[dcg * 128:(dcg + 1) * 128,
                                          out_col_off + t0:out_col_off + t0 + L],
                            x1o[:, (idx % 2) * 512:(idx % 2) * 512 + L]).then_inc(s_x1st[idx % 2], 16)
                if Q + 2 < NQ:
                    for j in range(11):
                        conv(Q + 2, j)
            g_eng.wait_ge(s_x1st[0], 16 * 2 * NQ * ng)
            g_eng.wait_ge(s_x1st[1], 16 * 2 * NQ * ng)
        block_cm.__exit__(None, None, None)


def _kv_phase(nc, wkT, wvT, xn2_dram, k_dram, ve_dram, vo_dram, consts, tag):
    wk3 = wkT.ap().rearrange("(c p) f -> p c f", p=128)
    wv3 = wvT.ap().rearrange("(c p) f -> p c f", p=128)
    xn3v = xn2_dram.ap().rearrange("(c p) t -> p c t", p=128)
    ng = len(GROUPS)
    vtiles = []
    for g, (t0, L) in enumerate(GROUPS):
        for i in range(L // 128):
            vtiles.append((g, t0, i))
    NVT = len(vtiles)
    with (
        nc.sbuf_tensor(f"{tag}_wk", [128, 16 * 2048], BF16) as wk_sb,
        nc.sbuf_tensor(f"{tag}_wv", [128, 16 * 2048], BF16) as wv_sb,
        nc.sbuf_tensor(f"{tag}_wwst", [128, 2 * 2048], F32) as wwst,
        nc.sbuf_tensor(f"{tag}_xng", [128, 2 * 16 * 512], BF16) as xng,
        nc.sbuf_tensor(f"{tag}_ko", [128, 2 * 512], BF16) as k_out,
        nc.sbuf_tensor(f"{tag}_ve", [128, 2 * 2048], BF16) as ve_out,
        nc.sbuf_tensor(f"{tag}_vo", [128, 2 * 2048], BF16) as vo_out,
        nc.psum_tensor(f"{tag}_pKV", [128, 4096], F32) as pKV,
        nc.semaphore(f"{tag}_xnld0") as s_xnld0,
        nc.semaphore(f"{tag}_xnld1") as s_xnld1,
        nc.semaphore(f"{tag}_kmm") as s_kmm,
        nc.semaphore(f"{tag}_kev") as s_kev,
        nc.semaphore(f"{tag}_kst0") as s_kst0,
        nc.semaphore(f"{tag}_kst1") as s_kst1,
        nc.semaphore(f"{tag}_vmm") as s_vmm,
        nc.semaphore(f"{tag}_vev") as s_vev,
        nc.semaphore(f"{tag}_vst0") as s_vst0,
        nc.semaphore(f"{tag}_vst1") as s_vst1,
        nc.semaphore(f"{tag}_wwld0") as s_wwld0,
        nc.semaphore(f"{tag}_wwld1") as s_wwld1,
        nc.semaphore(f"{tag}_wwcv") as s_wwcv,
    ):
        _clear_sems(nc, [s_xnld0, s_xnld1, s_kmm, s_kev, s_kst0, s_kst1, s_vmm,
                         s_vev, s_vst0, s_vst1, s_wwld0, s_wwld1, s_wwcv])
        block_cm = nc.Block()
        block = block_cm.__enter__()
        s_wwld = [s_wwld0, s_wwld1]
        s_xnld = [s_xnld0, s_xnld1]
        s_kst = [s_kst0, s_kst1]
        s_vst = [s_vst0, s_vst1]

        def xn_load(sync, li, t0, L):
            v = xng.ap().rearrange("p (b c t) -> p b c t", b=2, c=16)[:, li % 2]
            src = xn3v[:, :, t0:t0 + L]
            sync.dma_start(out=v[:, 0:8, 0:L], in_=src[:, 0:8, :]).then_inc(s_xnld[li % 2], 16)
            sync.dma_start(out=v[:, 8:16, 0:L], in_=src[:, 8:16, :]).then_inc(s_xnld[li % 2], 16)

        @block.sync
        def _(sync):
            for jj in range(16):
                if jj >= 2:
                    sync.wait_ge(s_wwcv, jj - 1)
                sync.dma_start(out=wwst.ap().rearrange("p (b f) -> p b f", b=2)[:, jj % 2],
                               in_=wk3[:, jj, :]).then_inc(s_wwld[jj % 2], 16)
            wvj = list(range(16, 32))
            for g, (t0, L) in enumerate(GROUPS):
                li = g
                if li >= 2:
                    sync.wait_ge(s_kmm, 16 * (li - 1))
                xn_load(sync, li, t0, L)
                nw = [4, 3, 3, 3, 3][g]
                for _ in range(nw):
                    jj = wvj.pop(0)
                    if jj >= 2:
                        sync.wait_ge(s_wwcv, jj - 1)
                    sync.dma_start(out=wwst.ap().rearrange("p (b f) -> p b f", b=2)[:, jj % 2],
                                   in_=wv3[:, jj - 16, :]).then_inc(s_wwld[jj % 2], 16)
            tiles_cum = [4, 8, 12, 16, 17]
            for g2, (t0, L) in enumerate(GROUPS):
                li = ng + g2
                if g2 < 2:
                    sync.wait_ge(s_kmm, 16 * (li - 1))
                else:
                    sync.wait_ge(s_vmm, tiles_cum[g2 - 2])
                xn_load(sync, li, t0, L)

        @block.tensor
        def _(tensor):
            for g, (t0, L) in enumerate(GROUPS):
                tensor.wait_ge(s_xnld[g % 2], 32 * (g // 2 + 1))
                if g == 0:
                    tensor.wait_ge(s_wwcv, 16)
                for fc in range(16):
                    kidx = g * 16 + fc
                    if kidx >= 2:
                        tensor.wait_ge(s_kev, kidx - 1)
                    for dc in range(NCH):
                        mm = tensor.matmul(
                            pKV[:, (kidx % 2) * 512:(kidx % 2) * 512 + L],
                            wk_sb[:, dc * 2048 + fc * 128: dc * 2048 + fc * 128 + 128],
                            xng.ap().rearrange("p (b c t) -> p b c t", b=2, c=16)[:, g % 2, dc, 0:L],
                            start=(dc == 0), stop=(dc == NCH - 1))
                    mm.then_inc(s_kmm, 1)
            tensor.wait_ge(s_kev, 80)
            tensor.wait_ge(s_wwcv, 32)
            vidx = 0
            for g2, (t0, L) in enumerate(GROUPS):
                li = ng + g2
                tensor.wait_ge(s_xnld[li % 2], 32 * (li // 2 + 1))
                for tloc in range(L // 128):
                    vpar = vidx % 2
                    if vidx >= 2:
                        tensor.wait_ge(s_vev, vidx - 1)
                    for dc in range(NCH):
                        for fg in range(4):
                            mm = tensor.matmul(
                                pKV[:, vpar * 2048 + fg * 512: vpar * 2048 + fg * 512 + 512],
                                xng.ap().rearrange("p (b c t) -> p b c t", b=2, c=16)[:, li % 2, dc, tloc * 128:(tloc + 1) * 128],
                                wv_sb[:, dc * 2048 + fg * 512: dc * 2048 + fg * 512 + 512],
                                start=(dc == 0), stop=(dc == NCH - 1))
                    mm.then_inc(s_vmm, 1)
                    vidx += 1

        @block.vector
        def _(vector):
            for kidx in range(80):
                vector.wait_ge(s_kmm, kidx + 1)
                L = GROUPS[kidx // 16][1]
                if kidx >= 2:
                    vector.wait_ge(s_kst[kidx % 2], 16 * ((kidx - 2) // 2 + 1))
                vector.tensor_copy(k_out[:, (kidx % 2) * 512:(kidx % 2) * 512 + L],
                                   pKV[:, (kidx % 2) * 512:(kidx % 2) * 512 + L]).then_inc(s_kev, 1)
            vector.memset(ve_out[:], 0.0)
            vector.memset(vo_out[:], 0.0)
            for vidx in range(NVT):
                vector.wait_ge(s_vmm, vidx + 1)
                if vidx >= 2:
                    vector.wait_ge(s_vst[vidx % 2], 32 * ((vidx - 2) // 2 + 1))
                pv3 = pKV.ap().rearrange("p (a c f) -> p a c f", a=2, c=16)[:, vidx % 2]
                ve3 = ve_out.ap().rearrange("p (a c f) -> p a c f", a=2, c=16)[:, vidx % 2]
                vo3 = vo_out.ap().rearrange("p (a c f) -> p a c f", a=2, c=16)[:, vidx % 2]
                vector.tensor_copy(ve3[:, :, 0:64], pv3[:, :, 0:64])
                vector.tensor_copy(vo3[:, :, 64:128], pv3[:, :, 64:128]).then_inc(s_vev, 1)

        @block.gpsimd
        def _(g_eng):
            for jj in range(16):
                g_eng.wait_ge(s_wwld[jj % 2], 16 * (jj // 2 + 1))
                g_eng.tensor_copy(wk_sb[:, jj * 2048:(jj + 1) * 2048],
                                  wwst.ap().rearrange("p (b f) -> p b f", b=2)[:, jj % 2]).then_inc(s_wwcv, 1)
            wvp = list(range(16, 32))
            for kidx in range(80):
                g, fc = kidx // 16, kidx % 16
                t0, L = GROUPS[g]
                g_eng.wait_ge(s_kev, kidx + 1)
                g_eng.dma_start(k_dram.ap()[fc * 128:(fc + 1) * 128, t0:t0 + L],
                                k_out[:, (kidx % 2) * 512:(kidx % 2) * 512 + L]).then_inc(s_kst[kidx % 2], 16)
                if wvp and kidx % 4 == 1:
                    jj = wvp.pop(0)
                    g_eng.wait_ge(s_wwld[jj % 2], 16 * (jj // 2 + 1))
                    g_eng.tensor_copy(wv_sb[:, (jj - 16) * 2048:(jj - 15) * 2048],
                                      wwst.ap().rearrange("p (b f) -> p b f", b=2)[:, jj % 2]).then_inc(s_wwcv, 1)
            for jj in wvp:
                g_eng.wait_ge(s_wwld[jj % 2], 16 * (jj // 2 + 1))
                g_eng.tensor_copy(wv_sb[:, (jj - 16) * 2048:(jj - 15) * 2048],
                                  wwst.ap().rearrange("p (b f) -> p b f", b=2)[:, jj % 2]).then_inc(s_wwcv, 1)
            vidx = 0
            for g2, (t0, L) in enumerate(GROUPS):
                for tloc in range(L // 128):
                    g_eng.wait_ge(s_vev, vidx + 1)
                    r0 = t0 + tloc * 128
                    g_eng.dma_start(ve_dram.ap()[r0:r0 + 128, :],
                                    ve_out[:, (vidx % 2) * 2048:(vidx % 2 + 1) * 2048]).then_inc(s_vst[vidx % 2], 16)
                    g_eng.dma_start(vo_dram.ap()[r0:r0 + 128, :],
                                    vo_out[:, (vidx % 2) * 2048:(vidx % 2 + 1) * 2048]).then_inc(s_vst[vidx % 2], 16)
                    vidx += 1
            g_eng.wait_ge(s_kst[0], 16 * 40)
            g_eng.wait_ge(s_kst[1], 16 * 40)
            g_eng.wait_ge(s_vst[0], 32 * ((NVT + 1) // 2))
            g_eng.wait_ge(s_vst[1], 32 * (NVT // 2))
        block_cm.__exit__(None, None, None)


def _attn_phase(nc, wqT, woT, xn2_dram, k_dram, ve_dram, vo_dram, x1_dram,
                x2_dram, consts, tag, dbg=None):
    maskA_sb, maskB_sb, ident_sb = consts["maskA_sb"], consts["maskB_sb"], consts["ident_sb"]
    wq3 = wqT.ap().rearrange("(c p) f -> p c f", p=128)
    wo3 = woT.ap().rearrange("(c p) f -> p c f", p=128)
    xnv = xn2_dram.ap().rearrange("(c p) t -> p c t", p=128)
    x1v = x1_dram.ap().rearrange("(c p) t -> p c t", p=128)
    x2v = x2_dram.ap().rearrange("(c p) t -> p c t", p=128)
    kv3 = k_dram.ap().rearrange("(c p) t -> p c t", p=128)
    with ExitStack() as ctx:
        ec = ctx.enter_context
        wq_sb = ec(nc.sbuf_tensor(f"{tag}_wq", [128, 16 * 2048], BF16))
        wo_sb = ec(nc.sbuf_tensor(f"{tag}_wo", [128, 16 * 2048], BF16))
        wst = ec(nc.sbuf_tensor(f"{tag}_wst", [128, 2 * 512], F32))
        kr = ec(nc.sbuf_tensor(f"{tag}_kr", [128, 2 * 2048], BF16))
        vre = ec(nc.sbuf_tensor(f"{tag}_vre", [128, 2 * 2048], BF16))
        vro = ec(nc.sbuf_tensor(f"{tag}_vro", [128, 2 * 2048], BF16))
        xn_sb = ec(nc.sbuf_tensor(f"{tag}_xn", [128, 2048], BF16))
        x1_sb = ec(nc.sbuf_tensor(f"{tag}_x1", [128, 2048], F32))
        qe_sb = ec(nc.sbuf_tensor(f"{tag}_qe", [128, 2048], BF16))
        qo_sb = ec(nc.sbuf_tensor(f"{tag}_qo", [128, 2048], BF16))
        s_sb = ec(nc.sbuf_tensor(f"{tag}_ssb", [128, 1024], F32))
        probs_sb = ec(nc.sbuf_tensor(f"{tag}_prb", [128, 2 * 1024], BF16))
        pt_sb = ec(nc.sbuf_tensor(f"{tag}_pt", [128, 4 * 128], BF16))
        ao_sb = ec(nc.sbuf_tensor(f"{tag}_ao", [128, 2048], BF16))
        x2_sb = ec(nc.sbuf_tensor(f"{tag}_x2", [128, 2048], F32))
        nm_sb = ec(nc.sbuf_tensor(f"{tag}_nm", [128, 8], F32))
        sum_sb = ec(nc.sbuf_tensor(f"{tag}_sum", [128, 8], F32))
        rec_sb = ec(nc.sbuf_tensor(f"{tag}_rec", [128, 8], F32))
        pQ = ec(nc.psum_tensor(f"{tag}_pQ", [128, 2048], F32))
        pS = ec(nc.psum_tensor(f"{tag}_pS", [128, 1024], F32))
        pT = ec(nc.psum_tensor(f"{tag}_pT", [128, 512], BF16))
        pV = ec(nc.psum_tensor(f"{tag}_pV", [128, 512], F32))
        names = ["wld0", "wld1", "wqcv", "kvl0", "kvl1", "xnl0", "xnl1",
                 "x1l0", "x1l1", "qmm", "qev", "smm", "sev", "max", "exp",
                 "nrm4", "tmm", "ptev", "pvp", "aoev", "omm", "x2ev", "x2st0",
                 "x2st1", "dbgst", "rcp"]
        sems = {n: ec(nc.semaphore(f"{tag}_{n}")) for n in names}
        _clear_sems(nc, list(sems.values()))
        (s_wld0, s_wld1, s_wqcv, s_kvl0, s_kvl1, s_xnl0, s_xnl1,
         s_x1l0, s_x1l1, s_qmm, s_qev, s_smm, s_sev, s_max, s_exp,
         s_nrm4, s_tmm, s_ptev, s_pvp, s_aoev, s_omm, s_x2ev, s_x2st0,
         s_x2st1, s_dbgst, s_rcp) = (sems[n] for n in names)
        block = ec(nc.Block())
        s_wld = [s_wld0, s_wld1]
        s_kvl = [s_kvl0, s_kvl1]
        s_xnl = [s_xnl0, s_xnl1]
        s_x1l = [s_x1l0, s_x1l1]
        s_x2st = [s_x2st0, s_x2st1]


        def kv_load(sync, b):
            sl = b % 2
            col = b * 128
            dst = kr.ap().rearrange("p (s c t) -> p s c t", s=2, c=16)[:, sl]
            src = kv3[:, :, col:col + 128]
            for q in range(4):
                sync.dma_start(dst[:, q * 4:(q + 1) * 4, :], src[:, q * 4:(q + 1) * 4, :]).then_inc(s_kvl[sl], 16)
            sync.dma_start(vre[:, sl * 2048:(sl + 1) * 2048], ve_dram.ap()[col:col + 128, :]).then_inc(s_kvl[sl], 16)
            sync.dma_start(vro[:, sl * 2048:(sl + 1) * 2048], vo_dram.ap()[col:col + 128, :]).then_inc(s_kvl[sl], 16)

        def xn_load(sync, b):
            col = b * 128
            dst = xn_sb.ap().rearrange("p (c t) -> p c t", c=16)
            for q in range(4):
                sync.dma_start(dst[:, q * 4:(q + 1) * 4, :], xnv[:, q * 4:(q + 1) * 4, col:col + 128]).then_inc(s_xnl0, 16)

        def x1_load(sync, b):
            col = b * 128
            dst1 = x1_sb.ap().rearrange("p (c t) -> p c t", c=16)
            for q in range(4):
                sync.dma_start(dst1[:, q * 4:(q + 1) * 4, :], x1v[:, q * 4:(q + 1) * 4, col:col + 128]).then_inc(s_x1l0, 16)

        @block.sync
        def _(sync):
            for w, (w3,) in enumerate(((wq3,), (wo3,))):
                for c in range(16):
                    for hf in range(4):
                        jj = w * 64 + c * 4 + hf
                        if jj >= 2:
                            sync.wait_ge(s_wqcv, jj - 1)
                        sync.dma_start(out=wst.ap().rearrange("p (b f) -> p b f", b=2)[:, jj % 2],
                                       in_=w3[:, c, hf * 512:(hf + 1) * 512]).then_inc(s_wld[jj % 2], 16)
            kv_load(sync, 0)
            kv_load(sync, 1)
            xn_load(sync, 1)
            x1_load(sync, 1)
            for b in range(1, NBLK + 1):
                bi = b - 1
                if b + 1 <= NBLK:
                    sync.wait_ge(s_pvp, 16 * b)
                    kv_load(sync, b + 1)
                    sync.wait_ge(s_qmm, b)
                    xn_load(sync, b + 1)
                    sync.wait_ge(s_x2ev, b)
                    x1_load(sync, b + 1)

        @block.tensor
        def _(tensor):
            for b in range(1, NBLK + 1):
                bi = b - 1
                p = b % 2
                tensor.wait_ge(s_xnl0, 64 * b)
                if b == 1:
                    tensor.wait_ge(s_wqcv, 64)
                if bi >= 1:
                    tensor.wait_ge(s_x2ev, bi)
                for fc in range(16):
                    for dc in range(16):
                        mm = tensor.matmul(
                            pQ[:, fc * 128:(fc + 1) * 128],
                            wq_sb[:, dc * 2048 + fc * 128: dc * 2048 + fc * 128 + 128],
                            xn_sb[:, dc * 128:(dc + 1) * 128],
                            start=(dc == 0), stop=(dc == 15))
                mm.then_inc(s_qmm, 1)
                slp, slc = (b - 1) % 2, b % 2
                tensor.wait_ge(s_kvl[slp], 96 * ((b - 1) // 2 + 1))
                tensor.wait_ge(s_kvl[slc], 96 * (b // 2 + 1))
                if dbg is not None and bi >= 1:
                    tensor.wait_ge(s_dbgst, 144 * bi)
                tensor.wait_ge(s_qev, bi + 1)
                for hb in range(8):
                    B = 8 * bi + hb
                    tensor.wait_ge(s_sev, B)
                    for j in range(4):
                        h = hb * 4 + j
                        qz = qe_sb if h % 2 == 0 else qo_sb
                        qs = qz[:, (h // 2) * 128:(h // 2) * 128 + 128]
                        for half, sl in ((0, slp), (1, slc)):
                            ks = kr[:, sl * 2048 + (h // 2) * 128: sl * 2048 + (h // 2) * 128 + 128]
                            mm = tensor.matmul(pS[:, j * 256 + half * 128: j * 256 + half * 128 + 128],
                                               qs, ks, start=True, stop=True)
                    mm.then_inc(s_smm, 1)
                    tensor.wait_ge(s_nrm4, B + 1)
                    for j in range(4):
                        h = hb * 4 + j
                        hh = 32 * bi + h
                        if hh >= 2:
                            tensor.wait_ge(s_ptev, 2 * (hh - 2) + 2)
                        for half in (0, 1):
                            sc = ((h % 2) * 2 + half) * 128
                            mm = tensor.transpose(
                                pT[:, sc:sc + 128],
                                probs_sb[:, (hb % 2) * 1024 + j * 256 + half * 128: (hb % 2) * 1024 + j * 256 + half * 128 + 128],
                                ident_sb[:, :])
                        mm.then_inc(s_tmm, 1)
                        hp = hh // 2
                        tensor.wait_ge(s_ptev, 2 * hh + 2)
                        if h % 2 == 0 and hp >= 4:
                            tensor.wait_ge(s_aoev, hp - 3)
                        vz = vre if h % 2 == 0 else vro
                        for half, sl in ((0, slp), (1, slc)):
                            sc = ((h % 2) * 2 + half) * 128
                            mm = tensor.matmul(
                                pV[:, (hp % 4) * 128:(hp % 4) * 128 + 128],
                                vz[:, sl * 2048 + (h // 2) * 128: sl * 2048 + (h // 2) * 128 + 128],
                                pt_sb[:, sc:sc + 128],
                                start=(h % 2 == 0 and half == 0),
                                stop=(h % 2 == 1 and half == 1))
                        if h % 2 == 1:
                            mm.then_inc(s_pvp, 1)
                tensor.wait_ge(s_aoev, 16 * bi + 16)
                if b == 1:
                    tensor.wait_ge(s_wqcv, 64)
                for dc in range(16):
                    for fc in range(16):
                        mm = tensor.matmul(
                            pQ[:, dc * 128:(dc + 1) * 128],
                            wo_sb[:, fc * 2048 + dc * 128: fc * 2048 + dc * 128 + 128],
                            ao_sb[:, fc * 128:(fc + 1) * 128],
                            start=(fc == 0), stop=(fc == 15))
                mm.then_inc(s_omm, 1)

        @block.vector
        def _(vector):
            for b in range(1, NBLK + 1):
                bi = b - 1
                p = b % 2
                if b == 1:
                    vector.memset(qe_sb[64:128, :], 0.0)
                    vector.memset(qo_sb[0:64, :], 0.0)
                vector.wait_ge(s_qmm, bi + 1)
                if dbg is not None and bi >= 1:
                    vector.wait_ge(s_dbgst, 144 * (bi - 1) + 64)
                vector.tensor_copy(qe_sb[0:64, :], pQ[0:64, :])
                vector.tensor_copy(qo_sb[64:128, :], pQ[64:128, :]).then_inc(s_qev, 1)
                for hb in range(8):
                    B = 8 * bi + hb
                    vector.wait_ge(s_smm, B + 1)
                    if B >= 1:
                        vector.wait_ge(s_exp, 4 * B)
                    msk = maskA_sb if b == 1 else maskB_sb
                    for j in range(4):
                        op = vector.scalar_tensor_tensor(
                            s_sb[:, j * 256:(j + 1) * 256],
                            pS[:, j * 256:(j + 1) * 256],
                            0.125, msk[:], ALU.mult, ALU.add)
                    op.then_inc(s_sev, 1)
                    vector.tensor_reduce(nm_sb[:, (hb % 2) * 4:(hb % 2) * 4 + 4],
                                         s_sb.ap().rearrange("p (j c) -> p j c", c=256),
                                         AX.X, ALU.max, negate=True).then_inc(s_max, 1)
                    vector.wait_ge(s_exp, 4 * B + 4)
                    vector.tensor_reduce(sum_sb[:, (hb % 2) * 4:(hb % 2) * 4 + 4],
                                         probs_sb.ap().rearrange("p (a j c) -> p a j c", a=2, c=256)[:, hb % 2],
                                         AX.X, ALU.add).then_inc(s_rcp, 1)
                    vector.wait_ge(s_rcp, 2 * B + 1)
                    vector.reciprocal(rec_sb[:, (hb % 2) * 4:(hb % 2) * 4 + 4],
                                      sum_sb[:, (hb % 2) * 4:(hb % 2) * 4 + 4]).then_inc(s_rcp, 1)
                    vector.wait_ge(s_rcp, 2 * B + 2)
                    for j in range(4):
                        op = vector.tensor_scalar_mul(
                            probs_sb[:, (hb % 2) * 1024 + j * 256:(hb % 2) * 1024 + j * 256 + 256],
                            probs_sb[:, (hb % 2) * 1024 + j * 256:(hb % 2) * 1024 + j * 256 + 256],
                            rec_sb[:, (hb % 2) * 4 + j:(hb % 2) * 4 + j + 1])
                    op.then_inc(s_nrm4, 1)
                vector.wait_ge(s_omm, bi + 1)
                vector.wait_ge(s_x1l0, 64 * (bi + 1))
                if bi >= 1:
                    vector.wait_ge(s_x2st0, 64 * bi)
                vector.tensor_add(x2_sb[:], pQ[:], x1_sb[:]).then_inc(s_x2ev, 1)

        @block.scalar
        def _(scalar):
            for b in range(1, NBLK + 1):
                bi = b - 1
                for hb in range(8):
                    B = 8 * bi + hb
                    scalar.wait_ge(s_max, B + 1)
                    if B >= 2:
                        scalar.wait_ge(s_tmm, 4 * (B - 2) + 4)
                    if dbg is not None and hb == 2:
                        scalar.wait_ge(s_dbgst, 144 * bi + 80)
                    for j in range(4):
                        scalar.activation(
                            probs_sb[:, (hb % 2) * 1024 + j * 256:(hb % 2) * 1024 + j * 256 + 256],
                            s_sb[:, j * 256:(j + 1) * 256],
                            AF.Exp, bias=nm_sb[:, (hb % 2) * 4 + j:(hb % 2) * 4 + j + 1],
                            scale=1.0,
                            accum_out=sum_sb[:, (hb % 2) * 4 + j:(hb % 2) * 4 + j + 1]).then_inc(s_exp, 1)
                    for j in range(4):
                        h = hb * 4 + j
                        hh = 32 * bi + h
                        scalar.wait_ge(s_tmm, hh + 1)
                        if hh >= 2:
                            scalar.wait_ge(s_pvp, (hh - 2) // 2 + 1)
                        for half in (0, 1):
                            sc = ((h % 2) * 2 + half) * 128
                            scalar.copy(pt_sb[:, sc:sc + 128], pT[:, sc:sc + 128]).then_inc(s_ptev, 1)
                        if h % 2 == 1:
                            hp = hh // 2
                            scalar.wait_ge(s_pvp, 16 * bi + (h - 1) // 2 + 1)
                            pl = (h - 1) // 2
                            scalar.copy(ao_sb[:, pl * 128:(pl + 1) * 128],
                                        pV[:, (hp % 4) * 128:(hp % 4) * 128 + 128]).then_inc(s_aoev, 1)

        @block.gpsimd
        def _(g_eng):
            for w in range(2):
                for c in range(16):
                    for hf in range(4):
                        jj = w * 64 + c * 4 + hf
                        g_eng.wait_ge(s_wld[jj % 2], 16 * (jj // 2 + 1))
                        dst = wq_sb if w == 0 else wo_sb
                        g_eng.tensor_copy(dst[:, c * 2048 + hf * 512: c * 2048 + (hf + 1) * 512],
                                          wst.ap().rearrange("p (b f) -> p b f", b=2)[:, jj % 2]).then_inc(s_wqcv, 1)
            for b in range(1, NBLK + 1):
                bi = b - 1
                if dbg is not None:
                    qv = dbg["q_dump"].ap().rearrange("(c p) t -> p c t", p=128)
                    av = dbg["ao_dump"].ap().rearrange("(c p) t -> p c t", p=128)
                    g_eng.wait_ge(s_qev, bi + 1)
                    for qq in range(2):
                        g_eng.dma_start(qv[0:64, qq * 8:(qq + 1) * 8, bi * 128:(bi + 1) * 128],
                                        qe_sb.ap().rearrange("p (c t) -> p c t", c=16)[0:64, qq * 8:(qq + 1) * 8, :]).then_inc(s_dbgst, 16)
                    for qq in range(2):
                        g_eng.dma_start(qv[64:128, qq * 8:(qq + 1) * 8, bi * 128:(bi + 1) * 128],
                                        qo_sb.ap().rearrange("p (c t) -> p c t", c=16)[64:128, qq * 8:(qq + 1) * 8, :]).then_inc(s_dbgst, 16)
                    g_eng.wait_ge(s_nrm4, 8 * bi + 1)
                    g_eng.dma_start(dbg["pr_dump"].ap()[:, bi * 1024:(bi + 1) * 1024],
                                    probs_sb[:, 0:1024]).then_inc(s_dbgst, 16)
                    g_eng.wait_ge(s_aoev, 16 * bi + 16)
                    asv = ao_sb.ap().rearrange("p (c t) -> p c t", c=16)
                    for qq in range(4):
                        g_eng.dma_start(av[:, qq * 4:(qq + 1) * 4, bi * 128:(bi + 1) * 128],
                                        asv[:, qq * 4:(qq + 1) * 4, :]).then_inc(s_dbgst, 16)
                g_eng.wait_ge(s_x2ev, bi + 1)
                dst = x2v[:, :, bi * 128:(bi + 1) * 128]
                src = x2_sb.ap().rearrange("p (c t) -> p c t", c=16)
                for q in range(4):
                    g_eng.dma_start(dst[:, q * 4:(q + 1) * 4, :], src[:, q * 4:(q + 1) * 4, :]).then_inc(s_x2st0, 16)
            g_eng.wait_ge(s_x2st0, 64 * NBLK)
            if dbg is not None:
                g_eng.wait_ge(s_dbgst, 144 * NBLK)


def _build(taps=(), upto=99, race_detect=True):
    nc = bass.Bass("TRN2", target_bir_lowering=False, detect_race_conditions=race_detect)
    xT = nc.dram_tensor("xT", [D, T_ALL], F32, kind="ExternalInput")
    gwT = nc.dram_tensor("gwT", [D, IF], F32, kind="ExternalInput")
    uwT = nc.dram_tensor("uwT", [D, IF], F32, kind="ExternalInput")
    dwT = nc.dram_tensor("dwT", [IF, D], F32, kind="ExternalInput")
    gs = nc.dram_tensor("gs", [128, 44], F32, kind="ExternalInput")
    us = nc.dram_tensor("us", [128, 44], F32, kind="ExternalInput")
    ds = nc.dram_tensor("ds", [128, 16], F32, kind="ExternalInput")
    wqT = nc.dram_tensor("wqT", [D, D], F32, kind="ExternalInput")
    wkT = nc.dram_tensor("wkT", [D, D], F32, kind="ExternalInput")
    wvT = nc.dram_tensor("wvT", [D, D], F32, kind="ExternalInput")
    woT = nc.dram_tensor("woT", [D, D], F32, kind="ExternalInput")
    maskA = nc.dram_tensor("maskA", [128, 256], F32, kind="ExternalInput")
    maskB = nc.dram_tensor("maskB", [128, 256], F32, kind="ExternalInput")
    ident = nc.dram_tensor("ident", [128, 128], BF16, kind="ExternalInput")

    def scratch(name, shape, dt):
        kind = "ExternalOutput" if name in taps else "Internal"
        return nc.dram_tensor(name, shape, dt, kind=kind)

    h_dram = scratch("h_dram", [IF, T_ALL], BF16)
    x1_dram = scratch("x1_dram", [D, T_ALL], F32)
    k_dram = scratch("k_dram", [D, T_ALL], BF16)
    ve_dram = scratch("ve_dram", [T_ALL, D], BF16)
    vo_dram = scratch("vo_dram", [T_ALL, D], BF16)
    xn2_dram = scratch("xn2_dram", [D, T_ALL], BF16)
    x2_dram = scratch("x2_dram", [D, T_OWN], F32)
    outT = nc.dram_tensor("outT", [D, T_OWN], F32, kind="ExternalOutput")

    x_view3 = xT.ap().rearrange("(c p) t -> p c t", p=128)
    x1_view3 = x1_dram.ap().rearrange("(c p) t -> p c t", p=128)
    x2_view3 = x2_dram.ap().rearrange("(c p) t -> p c t", p=128)

    with (
        nc.sbuf_tensor("gs_sb", [128, 44], F32) as gs_sb,
        nc.sbuf_tensor("us_sb", [128, 44], F32) as us_sb,
        nc.sbuf_tensor("ds2_sb", [128, 16], F32) as ds2_sb,
        nc.sbuf_tensor("maskA_sb", [128, 256], F32) as maskA_sb,
        nc.sbuf_tensor("maskB_sb", [128, 256], F32) as maskB_sb,
        nc.sbuf_tensor("ident_sb", [128, 128], BF16) as ident_sb,
        nc.sbuf_tensor("ones128", [128, 128], BF16) as ones128,
        nc.sbuf_tensor("eps_col", [128, 1], F32) as eps_col,
    ):
        consts = dict(gs_sb=gs_sb, us_sb=us_sb, ds2_sb=ds2_sb, maskA_sb=maskA_sb,
                      maskB_sb=maskB_sb, ident_sb=ident_sb, ones128=ones128,
                      eps_col=eps_col)
        with (
            nc.semaphore("p0ld") as p0ld,
            nc.Block() as block,
        ):
            @block.sync
            def _(sync):
                sync.dma_start(out=gs_sb[:], in_=gs.ap()).then_inc(p0ld, 16)
                sync.dma_start(out=us_sb[:], in_=us.ap()).then_inc(p0ld, 16)
                sync.dma_start(out=ds2_sb[:], in_=ds.ap()).then_inc(p0ld, 16)
                sync.dma_start(out=maskA_sb[:], in_=maskA.ap()).then_inc(p0ld, 16)
                sync.dma_start(out=maskB_sb[:], in_=maskB.ap()).then_inc(p0ld, 16)
                sync.dma_start(out=ident_sb[:], in_=ident.ap()).then_inc(p0ld, 16)

            @block.vector
            def _(vector):
                vector.wait_ge(p0ld, 96)
                vector.tensor_scalar_mul(ds2_sb[:], ds2_sb[:], 0.5)
                vector.memset(ones128[:], 1.0)
                vector.memset(eps_col[:], EPS)

        if upto < 1:
            return nc
        with nc.sbuf_tensor("xn1", [128, NCH * T_ALL], BF16) as xn1_sb:
            with (
                nc.sbuf_tensor("p1_xst", [128, 2 * 4096], F32) as p1_xst,
                nc.sbuf_tensor("p1_tmp", [128, 4096], F32) as p1_tmp,
                nc.sbuf_tensor("p1_xc", [128, 4096], BF16) as p1_xc,
                nc.sbuf_tensor("p1_xsq", [128, 4096], BF16) as p1_xsq,
                nc.sbuf_tensor("p1_rows", [128, 1024], F32) as p1_rows,
                nc.psum_tensor("p1_pstat", [128, 512], F32) as p1_pstat,
            ):
                bt = dict(xst=[p1_xst.ap()[:, 0:4096], p1_xst.ap()[:, 4096:8192]],
                          tmp=p1_tmp, xc=p1_xc, xsq=p1_xsq, rows=p1_rows,
                          pstat=p1_pstat)
                _ln_phase(nc, bt, x_view3, xn1_sb, T_ALL, LN_STEPS, consts, "ln1")
            if "xn1_dump" in taps:
                xn1_dump = nc.dram_tensor("xn1_dump", [128, NCH * T_ALL], BF16, kind="ExternalOutput")
                with nc.semaphore("dmp") as sdmp:
                    _clear_sems(nc, [sdmp])
                with nc.semaphore("dmp2") as sdmp, nc.Block() as block:
                    @block.gpsimd
                    def _(g_eng):
                        for c in range(NCH):
                            g_eng.dma_start(xn1_dump.ap()[:, c * T_ALL:(c + 1) * T_ALL],
                                            xn1_sb[:, c * T_ALL:(c + 1) * T_ALL]).then_inc(sdmp, 16)
                        g_eng.wait_ge(sdmp, 16 * NCH)
            if upto < 2:
                return nc
            _gateup_phase(nc, xn1_sb, T_ALL, GROUPS, gwT, uwT, h_dram, consts, "g1")

        if upto < 3:
            return nc
        _down_phase(nc, dwT, h_dram, x_view3, x1_dram, GROUPS, consts, "d1")

        if upto < 4:
            return nc
        with (
            nc.sbuf_tensor("p3_xst", [128, 2 * 4096], F32) as p3_xst,
            nc.sbuf_tensor("p3_tmp", [128, 4096], F32) as p3_tmp,
            nc.sbuf_tensor("p3_xc", [128, 4096], BF16) as p3_xc,
            nc.sbuf_tensor("p3_xsq", [128, 4096], BF16) as p3_xsq,
            nc.sbuf_tensor("p3_rows", [128, 1024], F32) as p3_rows,
            nc.sbuf_tensor("p3_xout", [128, 2 * 4096], BF16) as p3_xout,
            nc.psum_tensor("p3_pstat", [128, 512], F32) as p3_pstat,
        ):
            bt = dict(xst=[p3_xst.ap()[:, 0:4096], p3_xst.ap()[:, 4096:8192]],
                      tmp=p3_tmp, xc=p3_xc, xsq=p3_xsq, rows=p3_rows,
                      pstat=p3_pstat)
            _ln_phase(nc, bt, x1_view3, p3_xout, T_ALL, LN_STEPS, consts, "ln2",
                      dram_out=xn2_dram)
        if upto < 5:
            return nc
        _kv_phase(nc, wkT, wvT, xn2_dram, k_dram, ve_dram, vo_dram, consts, "kv")

        if upto < 6:
            return nc
        dbg = None
        if "q_dump" in taps:
            dbg = {"q_dump": nc.dram_tensor("q_dump", [D, T_OWN], BF16, kind="ExternalOutput"),
                   "ao_dump": nc.dram_tensor("ao_dump", [D, T_OWN], BF16, kind="ExternalOutput"),
                   "pr_dump": nc.dram_tensor("pr_dump", [128, NBLK * 1024], BF16, kind="ExternalOutput")}
        _attn_phase(nc, wqT, woT, xn2_dram, k_dram, ve_dram, vo_dram, x1_dram,
                    x2_dram, consts, "at", dbg=dbg)

        if upto < 7:
            return nc
        with nc.sbuf_tensor("xn3", [128, NCH * T_OWN], BF16) as xn3_sb:
            with (
                nc.sbuf_tensor("p5_xst", [128, 2 * 4096], F32) as p5_xst,
                nc.sbuf_tensor("p5_tmp", [128, 4096], F32) as p5_tmp,
                nc.sbuf_tensor("p5_xc", [128, 4096], BF16) as p5_xc,
                nc.sbuf_tensor("p5_xsq", [128, 4096], BF16) as p5_xsq,
                nc.sbuf_tensor("p5_rows", [128, 1024], F32) as p5_rows,
                nc.psum_tensor("p5_pstat", [128, 512], F32) as p5_pstat,
            ):
                bt = dict(xst=[p5_xst.ap()[:, 0:4096], p5_xst.ap()[:, 4096:8192]],
                          tmp=p5_tmp, xc=p5_xc, xsq=p5_xsq, rows=p5_rows,
                          pstat=p5_pstat)
                _ln_phase(nc, bt, x2_view3, xn3_sb, T_OWN, LN_STEPS_OWN, consts, "ln3")
            _gateup_phase(nc, xn3_sb, T_OWN, GROUPS_OWN, gwT, uwT, h_dram, consts,
                          "g2", col_off=HALO)

        if upto < 8:
            return nc
        _down_phase(nc, dwT, h_dram, x2_view3, outT, GROUPS_OWN, consts, "d2",
                    h_col_off=HALO, out_col_off=0)

    return nc



def _prep_maps(L0_attn_x, L0_attn_Wq, L0_attn_Wk, L0_attn_Wv, L0_attn_Wo,
               L0_ffn_gate_w, L0_ffn_up_w, L0_ffn_down_w,
               L0_ffn_gate_s, L0_ffn_up_s, L0_ffn_down_s):
    x = np.asarray(L0_attn_x, np.float32)
    common = {
        "gwT": np.ascontiguousarray(np.asarray(L0_ffn_gate_w, np.float32).T),
        "uwT": np.ascontiguousarray(np.asarray(L0_ffn_up_w, np.float32).T),
        "dwT": np.ascontiguousarray(np.asarray(L0_ffn_down_w, np.float32).T),
        "gs": np.ascontiguousarray(np.asarray(L0_ffn_gate_s, np.float32).reshape(44, 128).T),
        "us": np.ascontiguousarray(np.asarray(L0_ffn_up_s, np.float32).reshape(44, 128).T),
        "ds": np.ascontiguousarray(np.asarray(L0_ffn_down_s, np.float32).reshape(16, 128).T),
        "wqT": np.ascontiguousarray(np.asarray(L0_attn_Wq, np.float32).T),
        "wkT": np.ascontiguousarray(np.asarray(L0_attn_Wk, np.float32).T),
        "wvT": np.ascontiguousarray(np.asarray(L0_attn_Wv, np.float32).T),
        "woT": np.ascontiguousarray(np.asarray(L0_attn_Wo, np.float32).T),
        "ident": np.eye(128, dtype=ml_dtypes.bfloat16),
    }
    r = np.arange(128)[:, None]
    c = np.arange(256)[None, :]
    validB = (c >= r) & (c <= r + 128)
    maskB = np.where(validB, np.float32(0.0), np.float32(NEG))
    maskA0 = np.where(validB & (c >= 128), np.float32(0.0), np.float32(NEG))
    in_maps = []
    for core in range(8):
        b, hf = core // 2, core % 2
        t0 = hf * T_OWN
        own = x[b, t0:t0 + T_OWN]
        halo = x[b, t0 - HALO:t0] if hf == 1 else np.zeros((HALO, D), np.float32)
        xT_c = np.ascontiguousarray(np.concatenate([halo, own], 0).T)
        m = dict(common)
        m["xT"] = xT_c
        m["maskA"] = maskB if hf == 1 else maskA0
        m["maskB"] = maskB
        in_maps.append(m)
    return in_maps


def kernel(**inputs):
    in_maps = _prep_maps(**inputs)
    nc = _build()
    res = run_bass_kernel_spmd(nc, in_maps, core_ids=list(range(8)))
    out = np.empty((4, 4096, D), np.float32)
    for core in range(8):
        b, hf = core // 2, core % 2
        out[b, hf * T_OWN:(hf + 1) * T_OWN] = res.results[core]["outT"].T
    return out


# version 03
# speedup vs baseline: 1.0566x; 1.0566x over previous
from contextlib import ExitStack
import numpy as np
import ml_dtypes
import concourse.bass as bass
import concourse.mybir as mybir
from concourse.bass_utils import run_bass_kernel_spmd

F32 = mybir.dt.float32
BF16 = mybir.dt.bfloat16
AF = mybir.ActivationFunctionType
ALU = mybir.AluOpType
AX = mybir.AxisListType

D = 2048
NCH = 16
IF = 5632
ICN = 44
T_OWN = 2048
HALO = 128
T_ALL = 2176
NBLK = 16
EPS = 1e-5
NEG = -1.0e30

GROUPS = [(0, 512), (512, 512), (1024, 512), (1536, 512), (2048, 128)]
LN_STEPS = [(i * 256, 256) for i in range(8)] + [(2048, 128)]
GROUPS_OWN = [(0, 512), (512, 512), (1024, 512), (1536, 512)]
LN_STEPS_OWN = [(i * 256, 256) for i in range(8)]



def _clear_sems(nc, sems):
    with nc.Block() as blk:
        @blk.gpsimd
        def _(g):
            for sem in sems:
                g.sem_clear(sem)


def _ln_phase(nc, block_tensors, xin_view3, xn_out_sb, t_total, steps, consts, tag,
              dram_out=None, sync_hook=None, pool_hook=None):
    ones128 = consts["ones128"]
    t = block_tensors
    ns = len(steps)
    with (
        nc.semaphore(f"{tag}_xld0") as s_xld0,
        nc.semaphore(f"{tag}_xld1") as s_xld1,
        nc.semaphore(f"{tag}_v2p") as s_v2p,
        nc.semaphore(f"{tag}_pst") as s_pst,
        nc.semaphore(f"{tag}_var") as s_var,
        nc.semaphore(f"{tag}_sd") as s_sd,
        nc.semaphore(f"{tag}_nrm") as s_nrm,
        nc.semaphore(f"{tag}_ost0") as s_ost0,
        nc.semaphore(f"{tag}_ost1") as s_ost1,
    ):
        _clear_sems(nc, [s_xld0, s_xld1, s_v2p, s_pst, s_var, s_sd, s_nrm, s_ost0, s_ost1])
        block = None
        block_cm = nc.Block()
        block = block_cm.__enter__()
        s_xld = [s_xld0, s_xld1]
        s_ost = [s_ost0, s_ost1]

        @block.sync
        def _(sync):
            for s, (t0, L) in enumerate(steps):
                if s >= 2:
                    sync.wait_ge(s_nrm, s - 1)
                xst = t["xst"][s % 2]
                v = xst.rearrange("p (c t) -> p c t", t=256)
                sync.dma_start(out=v[:, 0:8, 0:L], in_=xin_view3[:, 0:8, t0:t0 + L]).then_inc(s_xld[s % 2], 16)
                sync.dma_start(out=v[:, 8:16, 0:L], in_=xin_view3[:, 8:16, t0:t0 + L]).then_inc(s_xld[s % 2], 16)
                if sync_hook is not None:
                    sync_hook(sync, s)

        @block.vector
        def _(vector):
            for s, (t0, L) in enumerate(steps):
                xst = t["xst"][s % 2]
                xv = xst.rearrange("p (c t) -> p c t", t=256)[:, :, 0:L]
                xcv = t["xc"].ap().rearrange("p (c t) -> p c t", t=256)[:, :, 0:L]
                xqv = t["xsq"].ap().rearrange("p (c t) -> p c t", t=256)[:, :, 0:L]
                vector.wait_ge(s_xld[s % 2], 32 * (s // 2 + 1))
                vector.tensor_copy(xcv, xv)
                vector.tensor_tensor(xqv, xcv, xcv, ALU.mult).then_inc(s_v2p, 1)
                vector.wait_ge(s_pst, s + 1)
                rows = t["rows"]
                m = rows[:, 0:256]
                t1 = rows[:, 256:512]
                sd = rows[:, 512:768]
                r = rows[:, 768:1024]
                vector.tensor_scalar_mul(m[:, 0:L], t["pstat"][:, 0:L], 1.0 / D)
                vector.tensor_scalar_mul(t1[:, 0:L], t["pstat"][:, 256:256 + L], 1.0 / D).then_inc(s_var, 1)
                vector.tensor_tensor(sd[:, 0:L], m[:, 0:L], m[:, 0:L], ALU.mult)
                vector.tensor_sub(t1[:, 0:L], t1[:, 0:L], sd[:, 0:L]).then_inc(s_var, 1)
                vector.wait_ge(s_sd, s + 1)
                vector.reciprocal(r[:, 0:L], sd[:, 0:L])
                tmpv = t["tmp"].ap().rearrange("p (c t) -> p c t", t=256)[:, :, 0:L]
                mb = m[:, 0:L].unsqueeze(1).broadcast_to([128, 16, L])
                rb = r[:, 0:L].unsqueeze(1).broadcast_to([128, 16, L])
                vector.tensor_sub(tmpv, xv, mb)
                if dram_out is None:
                    xout = xn_out_sb.ap().rearrange("p (c t) -> p c t", t=t_total)[:, :, t0:t0 + L]
                else:
                    if s >= 2:
                        vector.wait_ge(s_ost[s % 2], 64 * ((s - 2) // 2 + 1))
                    xout = xn_out_sb.ap().rearrange("p (b c t) -> p b c t", b=2, c=NCH)[:, s % 2, :, 0:L]
                vector.tensor_tensor(xout, tmpv, rb, ALU.mult).then_inc(s_nrm, 1)

        @block.tensor
        def _(tensor):
            for s, (t0, L) in enumerate(steps):
                tensor.wait_ge(s_v2p, s + 1)
                if s >= 1:
                    tensor.wait_ge(s_var, 2 * s - 1)
                xcv = t["xc"].ap().rearrange("p (c t) -> p c t", t=256)
                xqv = t["xsq"].ap().rearrange("p (c t) -> p c t", t=256)
                for c in range(NCH):
                    tensor.matmul(t["pstat"][:, 0:L], ones128[:, :], xcv[:, c, 0:L],
                                  start=(c == 0), stop=(c == NCH - 1))
                for c in range(NCH):
                    mm = tensor.matmul(t["pstat"][:, 256:256 + L], ones128[:, :], xqv[:, c, 0:L],
                                       start=(c == 0), stop=(c == NCH - 1))
                mm.then_inc(s_pst, 1)

        @block.scalar
        def _(scalar):
            for s, (t0, L) in enumerate(steps):
                scalar.wait_ge(s_var, 2 * s + 2)
                rows = t["rows"]
                scalar.activation(rows[:, 512:512 + L], rows[:, 256:256 + L], AF.Sqrt,
                                  bias=consts["eps_col"][:, 0:1], scale=1.0).then_inc(s_sd, 1)

        if dram_out is not None or pool_hook is not None:
            @block.gpsimd
            def _(g_eng):
                o3 = None if dram_out is None else dram_out.ap().rearrange("(c p) t -> p c t", p=128)
                for s, (t0, L) in enumerate(steps):
                    if pool_hook is not None:
                        pool_hook(g_eng, s)
                    if dram_out is not None:
                        g_eng.wait_ge(s_nrm, s + 1)
                        v = xn_out_sb.ap().rearrange("p (b c t) -> p b c t", b=2, c=NCH)[:, s % 2]
                        for q in range(4):
                            g_eng.dma_start(o3[:, q * 4:(q + 1) * 4, t0:t0 + L],
                                            v[:, q * 4:(q + 1) * 4, 0:L]).then_inc(s_ost[s % 2], 16)
                if dram_out is not None:
                    g_eng.wait_ge(s_ost[0], 64 * ((ns + 1) // 2))
                    g_eng.wait_ge(s_ost[1], 64 * (ns // 2))
        block_cm.__exit__(None, None, None)


def _gateup_phase(nc, xn_sb, t_total, groups, gwT, uwT, h_dram, consts, tag,
                  col_off=0):
    gs_sb, us_sb = consts["gs_sb"], consts["us_sb"]
    NSUP = 22
    n_per_sup = 2 * len(groups)
    NIT = NSUP * n_per_sup
    with (
        nc.sbuf_tensor(f"{tag}_gwst", [128, 2 * 4096], F32) as gwst,
        nc.sbuf_tensor(f"{tag}_uwst", [128, 2 * 4096], F32) as uwst,
        nc.sbuf_tensor(f"{tag}_gwbf", [128, 2 * 4096], BF16) as gwbf,
        nc.sbuf_tensor(f"{tag}_uwbf", [128, 2 * 4096], BF16) as uwbf,
        nc.sbuf_tensor(f"{tag}_silu", [128, 3 * 512], F32) as silu_sb,
        nc.sbuf_tensor(f"{tag}_hsb", [128, 3 * 512], BF16) as h_sb,
        nc.psum_tensor(f"{tag}_pG", [128, 3 * 512], F32) as pG,
        nc.psum_tensor(f"{tag}_pU", [128, 3 * 512], F32) as pU,
        nc.semaphore(f"{tag}_wld0") as s_wld0,
        nc.semaphore(f"{tag}_wld1") as s_wld1,
        nc.semaphore(f"{tag}_wcv") as s_wcv,
        nc.semaphore(f"{tag}_pmm") as s_pmm,
        nc.semaphore(f"{tag}_sil") as s_sil,
        nc.semaphore(f"{tag}_hm") as s_hm,
        nc.semaphore(f"{tag}_hst0") as s_hst0,
        nc.semaphore(f"{tag}_hst1") as s_hst1,
        nc.semaphore(f"{tag}_hst2") as s_hst2,
    ):
        _clear_sems(nc, [s_wld0, s_wld1, s_wcv, s_pmm, s_sil, s_hm, s_hst0, s_hst1, s_hst2])
        block_cm = nc.Block()
        block = block_cm.__enter__()
        gw3 = gwT.ap().rearrange("(c p) i -> p c i", p=128)
        uw3 = uwT.ap().rearrange("(c p) i -> p c i", p=128)
        s_wld = [s_wld0, s_wld1]
        s_hst = [s_hst0, s_hst1, s_hst2]

        @block.sync
        def _(sync):
            for s2 in range(NSUP):
                if s2 >= 2:
                    sync.wait_ge(s_pmm, n_per_sup * (s2 - 1))
                    sync.wait_ge(s_wcv, 2 * (s2 - 1))
                i0 = s2 * 256
                par = s2 % 2
                for (st, w3) in ((gwst, gw3), (uwst, uw3)):
                    v = st.ap().rearrange("p (b c i) -> p b c i", b=2, c=16)[:, par]
                    sync.dma_start(out=v[:, 0:8, :], in_=w3[:, 0:8, i0:i0 + 256]).then_inc(s_wld[par], 16)
                    sync.dma_start(out=v[:, 8:16, :], in_=w3[:, 8:16, i0:i0 + 256]).then_inc(s_wld[par], 16)

        @block.vector
        def _(vector):
            n = 0
            for s2 in range(NSUP):
                for m in range(n_per_sup):
                    ic_loc = m // len(groups)
                    g = m % len(groups)
                    L = groups[g][1]
                    ic = s2 * 2 + ic_loc
                    p2 = n % 3
                    vector.wait_ge(s_sil, n + 1)
                    if n >= 3:
                        vector.wait_ge(s_hst[n % 3], 16 * ((n - 3) // 3 + 1))
                    vector.scalar_tensor_tensor(
                        h_sb[:, p2 * 512:p2 * 512 + L], pU[:, p2 * 512:p2 * 512 + L],
                        us_sb[:, ic:ic + 1], silu_sb[:, p2 * 512:p2 * 512 + L],
                        ALU.mult, ALU.mult).then_inc(s_hm, 1)
                    n += 1

        @block.tensor
        def _(tensor):
            n = 0
            for s2 in range(NSUP):
                par = s2 % 2
                tensor.wait_ge(s_wcv, 2 * (s2 + 1))
                for m in range(n_per_sup):
                    ic_loc = m // len(groups)
                    g = m % len(groups)
                    t0, L = groups[g]
                    p2 = n % 3
                    if n >= 3:
                        tensor.wait_ge(s_sil, n - 2)
                        tensor.wait_ge(s_hm, n - 2)
                    for c in range(NCH):
                        lhs = gwbf[:, par * 4096 + c * 256 + ic_loc * 128: par * 4096 + c * 256 + ic_loc * 128 + 128]
                        rhs = xn_sb[:, c * t_total + t0: c * t_total + t0 + L]
                        tensor.matmul(pG[:, p2 * 512:p2 * 512 + L], lhs, rhs,
                                      start=(c == 0), stop=(c == NCH - 1))
                    for c in range(NCH):
                        lhs = uwbf[:, par * 4096 + c * 256 + ic_loc * 128: par * 4096 + c * 256 + ic_loc * 128 + 128]
                        rhs = xn_sb[:, c * t_total + t0: c * t_total + t0 + L]
                        mm = tensor.matmul(pU[:, p2 * 512:p2 * 512 + L], lhs, rhs,
                                           start=(c == 0), stop=(c == NCH - 1))
                    mm.then_inc(s_pmm, 1)
                    n += 1

        @block.scalar
        def _(scalar):
            n = 0
            for s2 in range(NSUP):
                for m in range(n_per_sup):
                    ic_loc = m // len(groups)
                    g = m % len(groups)
                    L = groups[g][1]
                    ic = s2 * 2 + ic_loc
                    p2 = n % 3
                    scalar.wait_ge(s_pmm, n + 1)
                    if n >= 3:
                        scalar.wait_ge(s_hm, n - 2)
                    scalar.activation(silu_sb[:, p2 * 512:p2 * 512 + L], pG[:, p2 * 512:p2 * 512 + L],
                                      AF.Silu, bias=0.0, scale=gs_sb[:, ic:ic + 1]).then_inc(s_sil, 1)
                    n += 1

        @block.gpsimd
        def _(g_eng):
            def conv(s2):
                par = s2 % 2
                g_eng.wait_ge(s_wld[par], 64 * (s2 // 2 + 1))
                if s2 >= 2:
                    g_eng.wait_ge(s_pmm, n_per_sup * (s2 - 1))
                g_eng.tensor_copy(gwbf[:, par * 4096:(par + 1) * 4096], gwst[:, par * 4096:(par + 1) * 4096]).then_inc(s_wcv, 1)
                g_eng.tensor_copy(uwbf[:, par * 4096:(par + 1) * 4096], uwst[:, par * 4096:(par + 1) * 4096]).then_inc(s_wcv, 1)
            conv(0)
            conv(1)
            n = 0
            for s2 in range(NSUP):
                for m in range(n_per_sup):
                    ic_loc = m // len(groups)
                    g = m % len(groups)
                    t0, L = groups[g]
                    ic = s2 * 2 + ic_loc
                    p2 = n % 3
                    g_eng.wait_ge(s_hm, n + 1)
                    g_eng.dma_start(h_dram.ap()[ic * 128:(ic + 1) * 128, col_off + t0: col_off + t0 + L],
                                    h_sb[:, p2 * 512:p2 * 512 + L]).then_inc(s_hst[n % 3], 16)
                    if m == n_per_sup - 1 and s2 + 2 < NSUP:
                        conv(s2 + 2)
                    n += 1
            for sl in range(3):
                g_eng.wait_ge(s_hst[sl], 16 * len([x for x in range(NIT) if x % 3 == sl]))
        block_cm.__exit__(None, None, None)


def _down_phase(nc, dwT, h_dram, x_view3, xres_out, groups, consts, tag,
                h_col_off=0, out_col_off=0):
    ds2_sb = consts["ds2_sb"]
    ng = len(groups)
    NQ = 4
    with (
        nc.sbuf_tensor(f"{tag}_dwst", [128, 2 * 2048], F32) as dwst,
        nc.sbuf_tensor(f"{tag}_dwbf", [128, 2 * 44 * 512], BF16) as dwbf,
        nc.sbuf_tensor(f"{tag}_hch", [128, 8 * 512], BF16) as hch,
        nc.sbuf_tensor(f"{tag}_xst", [128, 2 * 4 * 512], F32) as xst,
        nc.sbuf_tensor(f"{tag}_x1o", [128, 2 * 512], F32) as x1o,
        nc.psum_tensor(f"{tag}_pD", [128, 4 * 512], F32) as pD,
        nc.semaphore(f"{tag}_dwld0") as s_dwld0,
        nc.semaphore(f"{tag}_dwld1") as s_dwld1,
        nc.semaphore(f"{tag}_dwcv") as s_dwcv,
        nc.semaphore(f"{tag}_xlda") as s_xlda,
        nc.semaphore(f"{tag}_xldb") as s_xldb,
        nc.semaphore(f"{tag}_hld0") as s_hld0,
        nc.semaphore(f"{tag}_hld1") as s_hld1,
        nc.semaphore(f"{tag}_pic") as s_pic,
        nc.semaphore(f"{tag}_ev") as s_ev,
        nc.semaphore(f"{tag}_x1st0") as s_x1st0,
        nc.semaphore(f"{tag}_x1st1") as s_x1st1,
    ):
        _clear_sems(nc, [s_dwld0, s_dwld1, s_dwcv, s_xlda, s_xldb, s_hld0, s_hld1,
                         s_pic, s_ev, s_x1st0, s_x1st1])
        block_cm = nc.Block()
        block = block_cm.__enter__()
        dw3 = dwT.ap().rearrange("(c p) d -> p c d", p=128)
        s_hld = [s_hld0, s_hld1]
        s_dwld = [s_dwld0, s_dwld1]
        s_xld = [s_xlda, s_xldb]
        s_x1st = [s_x1st0, s_x1st1]

        def _dwload(sync, Qt, j):
            jj = Qt * 11 + j
            if jj >= 2:
                sync.wait_ge(s_dwcv, jj - 1)
            sync.dma_start(
                out=dwst.ap().rearrange("p (b c d) -> p b c d", b=2, c=4)[:, jj % 2],
                in_=dw3[:, j * 4:(j + 1) * 4, Qt * 512:(Qt + 1) * 512],
            ).then_inc(s_dwld[jj % 2], 16)

        @block.scalar
        def _(scalar):
            for Qt in range(NQ):
                for j in range(11):
                    _dwload(scalar, Qt, j)

        @block.sync
        def _(sync):
            for Q in range(NQ):
                for grp in range(ng):
                    gi = Q * ng + grp
                    t0, L = groups[grp]
                    if gi >= 2:
                        sync.wait_ge(s_ev, 4 * (gi - 1))
                    xv = xst.ap().rearrange("p (b c t) -> p b c t", b=2, c=4)[:, gi % 2]
                    sync.dma_start(out=xv[:, :, 0:L],
                                   in_=x_view3[:, Q * 4:(Q + 1) * 4, t0:t0 + L]).then_inc(s_xld[gi % 2], 16)
                    h3 = h_dram.ap().rearrange("(c p) t -> p c t", p=128)
                    hv = hch.ap().rearrange("p (s c t) -> p s c t", s=2, c=4)
                    for qd in range(11):
                        gq = gi * 11 + qd
                        if gq >= 2:
                            sync.wait_ge(s_pic, 4 * (gq - 2) + 4)
                        sync.dma_start(out=hv[:, gq % 2, :, 0:L],
                                       in_=h3[:, qd * 4:(qd + 1) * 4,
                                              h_col_off + t0:h_col_off + t0 + L]).then_inc(s_hld[gq % 2], 16)

        @block.vector
        def _(vector):
            for Q in range(NQ):
                for grp in range(ng):
                    gi = Q * ng + grp
                    t0, L = groups[grp]
                    vector.wait_ge(s_pic, ICN * (gi + 1))
                    vector.wait_ge(s_xld[gi % 2], 16 * (gi // 2 + 1))
                    for dc in range(4):
                        idx = 4 * gi + dc
                        if idx >= 2:
                            vector.wait_ge(s_x1st[idx % 2], 16 * ((idx - 2) // 2 + 1))
                        dcg = Q * 4 + dc
                        xv = xst.ap().rearrange("p (b c t) -> p b c t", b=2, c=4)[:, gi % 2, dc, 0:L]
                        vector.scalar_tensor_tensor(
                            x1o[:, (idx % 2) * 512:(idx % 2) * 512 + L],
                            pD[:, dc * 512:dc * 512 + L],
                            ds2_sb[:, dcg:dcg + 1], xv, ALU.mult, ALU.add).then_inc(s_ev, 1)

        @block.tensor
        def _(tensor):
            for Q in range(NQ):
                par = Q % 2
                for grp in range(ng):
                    gi = Q * ng + grp
                    t0, L = groups[grp]
                    for ic in range(ICN):
                        n_ic = gi * ICN + ic
                        if ic % 4 == 0:
                            gq = gi * 11 + ic // 4
                            tensor.wait_ge(s_hld[gq % 2], 16 * (gq // 2 + 1))
                        if ic == 0:
                            tensor.wait_ge(s_dwcv, 11 * (Q + 1))
                        sl = ((gi * 11 + ic // 4) % 2) * 4 + (ic % 4)
                        for dc in range(4):
                            if ic == 0 and gi >= 1:
                                tensor.wait_ge(s_ev, 4 * (gi - 1) + dc + 1)
                            lhs = dwbf[:, par * 22528 + ic * 512 + dc * 128: par * 22528 + ic * 512 + dc * 128 + 128]
                            mm = tensor.matmul(pD[:, dc * 512:dc * 512 + L],
                                               lhs, hch[:, sl * 512:sl * 512 + L],
                                               start=(ic == 0), stop=(ic == ICN - 1))
                        mm.then_inc(s_pic, 1)

        @block.gpsimd
        def _(g_eng):
            def conv(Qt, j):
                jj = Qt * 11 + j
                g_eng.wait_ge(s_dwld[jj % 2], 16 * (jj // 2 + 1))
                if Qt >= 2 and j == 0:
                    g_eng.wait_ge(s_pic, ICN * ng * (Qt - 1))
                g_eng.tensor_copy(
                    dwbf.ap().rearrange("p (b c d) -> p b c d", b=2, c=44, d=512)[:, Qt % 2, j * 4:(j + 1) * 4, :],
                    dwst.ap().rearrange("p (b c d) -> p b c d", b=2, c=4)[:, jj % 2]).then_inc(s_dwcv, 1)

            for j in range(11):
                conv(0, j)
            for j in range(11):
                conv(1, j)
            for Q in range(NQ):
                for grp in range(ng):
                    gi = Q * ng + grp
                    t0, L = groups[grp]
                    for dc in range(4):
                        idx = 4 * gi + dc
                        dcg = Q * 4 + dc
                        g_eng.wait_ge(s_ev, idx + 1)
                        g_eng.dma_start(
                            xres_out.ap()[dcg * 128:(dcg + 1) * 128,
                                          out_col_off + t0:out_col_off + t0 + L],
                            x1o[:, (idx % 2) * 512:(idx % 2) * 512 + L]).then_inc(s_x1st[idx % 2], 16)
                if Q + 2 < NQ:
                    for j in range(11):
                        conv(Q + 2, j)
            g_eng.wait_ge(s_x1st[0], 16 * 2 * NQ * ng)
            g_eng.wait_ge(s_x1st[1], 16 * 2 * NQ * ng)
        block_cm.__exit__(None, None, None)


def _kv_phase(nc, wkT, wvT, xn2_dram, k_dram, ve_dram, vo_dram, consts, tag):
    wk3 = wkT.ap().rearrange("(c p) f -> p c f", p=128)
    wv3 = wvT.ap().rearrange("(c p) f -> p c f", p=128)
    xn3v = xn2_dram.ap().rearrange("(c p) t -> p c t", p=128)
    ng = len(GROUPS)
    vtiles = []
    for g, (t0, L) in enumerate(GROUPS):
        for i in range(L // 128):
            vtiles.append((g, t0, i))
    NVT = len(vtiles)
    with (
        nc.sbuf_tensor(f"{tag}_wk", [128, 16 * 2048], BF16) as wk_sb,
        nc.sbuf_tensor(f"{tag}_wv", [128, 16 * 2048], BF16) as wv_sb,
        nc.sbuf_tensor(f"{tag}_wwst", [128, 2 * 2048], F32) as wwst,
        nc.sbuf_tensor(f"{tag}_xng", [128, 2 * 16 * 512], BF16) as xng,
        nc.sbuf_tensor(f"{tag}_ko", [128, 2 * 512], BF16) as k_out,
        nc.sbuf_tensor(f"{tag}_ve", [128, 2 * 2048], BF16) as ve_out,
        nc.sbuf_tensor(f"{tag}_vo", [128, 2 * 2048], BF16) as vo_out,
        nc.psum_tensor(f"{tag}_pKV", [128, 4096], F32) as pKV,
        nc.semaphore(f"{tag}_xnld0") as s_xnld0,
        nc.semaphore(f"{tag}_xnld1") as s_xnld1,
        nc.semaphore(f"{tag}_kmm") as s_kmm,
        nc.semaphore(f"{tag}_kev") as s_kev,
        nc.semaphore(f"{tag}_kst0") as s_kst0,
        nc.semaphore(f"{tag}_kst1") as s_kst1,
        nc.semaphore(f"{tag}_vmm") as s_vmm,
        nc.semaphore(f"{tag}_vev") as s_vev,
        nc.semaphore(f"{tag}_vst0") as s_vst0,
        nc.semaphore(f"{tag}_vst1") as s_vst1,
        nc.semaphore(f"{tag}_wwld0") as s_wwld0,
        nc.semaphore(f"{tag}_wwld1") as s_wwld1,
        nc.semaphore(f"{tag}_wwcv") as s_wwcv,
    ):
        _clear_sems(nc, [s_xnld0, s_xnld1, s_kmm, s_kev, s_kst0, s_kst1, s_vmm,
                         s_vev, s_vst0, s_vst1, s_wwld0, s_wwld1, s_wwcv])
        block_cm = nc.Block()
        block = block_cm.__enter__()
        s_wwld = [s_wwld0, s_wwld1]
        s_xnld = [s_xnld0, s_xnld1]
        s_kst = [s_kst0, s_kst1]
        s_vst = [s_vst0, s_vst1]

        def xn_load(sync, li, t0, L):
            v = xng.ap().rearrange("p (b c t) -> p b c t", b=2, c=16)[:, li % 2]
            src = xn3v[:, :, t0:t0 + L]
            sync.dma_start(out=v[:, 0:8, 0:L], in_=src[:, 0:8, :]).then_inc(s_xnld[li % 2], 16)
            sync.dma_start(out=v[:, 8:16, 0:L], in_=src[:, 8:16, :]).then_inc(s_xnld[li % 2], 16)

        @block.sync
        def _(sync):
            for jj in range(16):
                if jj >= 2:
                    sync.wait_ge(s_wwcv, jj - 1)
                sync.dma_start(out=wwst.ap().rearrange("p (b f) -> p b f", b=2)[:, jj % 2],
                               in_=wk3[:, jj, :]).then_inc(s_wwld[jj % 2], 16)
            wvj = list(range(16, 32))
            for g, (t0, L) in enumerate(GROUPS):
                li = g
                if li >= 2:
                    sync.wait_ge(s_kmm, 16 * (li - 1))
                xn_load(sync, li, t0, L)
                nw = [4, 3, 3, 3, 3][g]
                for _ in range(nw):
                    jj = wvj.pop(0)
                    if jj >= 2:
                        sync.wait_ge(s_wwcv, jj - 1)
                    sync.dma_start(out=wwst.ap().rearrange("p (b f) -> p b f", b=2)[:, jj % 2],
                                   in_=wv3[:, jj - 16, :]).then_inc(s_wwld[jj % 2], 16)
            tiles_cum = [4, 8, 12, 16, 17]
            for g2, (t0, L) in enumerate(GROUPS):
                li = ng + g2
                if g2 < 2:
                    sync.wait_ge(s_kmm, 16 * (li - 1))
                else:
                    sync.wait_ge(s_vmm, tiles_cum[g2 - 2])
                xn_load(sync, li, t0, L)

        @block.tensor
        def _(tensor):
            for g, (t0, L) in enumerate(GROUPS):
                tensor.wait_ge(s_xnld[g % 2], 32 * (g // 2 + 1))
                if g == 0:
                    tensor.wait_ge(s_wwcv, 16)
                for fc in range(16):
                    kidx = g * 16 + fc
                    if kidx >= 2:
                        tensor.wait_ge(s_kev, kidx - 1)
                    for dc in range(NCH):
                        mm = tensor.matmul(
                            pKV[:, (kidx % 2) * 512:(kidx % 2) * 512 + L],
                            wk_sb[:, dc * 2048 + fc * 128: dc * 2048 + fc * 128 + 128],
                            xng.ap().rearrange("p (b c t) -> p b c t", b=2, c=16)[:, g % 2, dc, 0:L],
                            start=(dc == 0), stop=(dc == NCH - 1))
                    mm.then_inc(s_kmm, 1)
            tensor.wait_ge(s_kev, 80)
            tensor.wait_ge(s_wwcv, 32)
            vidx = 0
            for g2, (t0, L) in enumerate(GROUPS):
                li = ng + g2
                tensor.wait_ge(s_xnld[li % 2], 32 * (li // 2 + 1))
                for tloc in range(L // 128):
                    vpar = vidx % 2
                    if vidx >= 2:
                        tensor.wait_ge(s_vev, vidx - 1)
                    for dc in range(NCH):
                        for fg in range(4):
                            mm = tensor.matmul(
                                pKV[:, vpar * 2048 + fg * 512: vpar * 2048 + fg * 512 + 512],
                                xng.ap().rearrange("p (b c t) -> p b c t", b=2, c=16)[:, li % 2, dc, tloc * 128:(tloc + 1) * 128],
                                wv_sb[:, dc * 2048 + fg * 512: dc * 2048 + fg * 512 + 512],
                                start=(dc == 0), stop=(dc == NCH - 1))
                    mm.then_inc(s_vmm, 1)
                    vidx += 1

        @block.vector
        def _(vector):
            for kidx in range(80):
                vector.wait_ge(s_kmm, kidx + 1)
                L = GROUPS[kidx // 16][1]
                if kidx >= 2:
                    vector.wait_ge(s_kst[kidx % 2], 16 * ((kidx - 2) // 2 + 1))
                vector.tensor_copy(k_out[:, (kidx % 2) * 512:(kidx % 2) * 512 + L],
                                   pKV[:, (kidx % 2) * 512:(kidx % 2) * 512 + L]).then_inc(s_kev, 1)
            vector.memset(ve_out[:], 0.0)
            vector.memset(vo_out[:], 0.0)
            for vidx in range(NVT):
                vector.wait_ge(s_vmm, vidx + 1)
                if vidx >= 2:
                    vector.wait_ge(s_vst[vidx % 2], 32 * ((vidx - 2) // 2 + 1))
                pv3 = pKV.ap().rearrange("p (a c f) -> p a c f", a=2, c=16)[:, vidx % 2]
                ve3 = ve_out.ap().rearrange("p (a c f) -> p a c f", a=2, c=16)[:, vidx % 2]
                vo3 = vo_out.ap().rearrange("p (a c f) -> p a c f", a=2, c=16)[:, vidx % 2]
                vector.tensor_copy(ve3[:, :, 0:64], pv3[:, :, 0:64])
                vector.tensor_copy(vo3[:, :, 64:128], pv3[:, :, 64:128]).then_inc(s_vev, 1)

        @block.gpsimd
        def _(g_eng):
            for jj in range(16):
                g_eng.wait_ge(s_wwld[jj % 2], 16 * (jj // 2 + 1))
                g_eng.tensor_copy(wk_sb[:, jj * 2048:(jj + 1) * 2048],
                                  wwst.ap().rearrange("p (b f) -> p b f", b=2)[:, jj % 2]).then_inc(s_wwcv, 1)
            wvp = list(range(16, 32))
            for kidx in range(80):
                g, fc = kidx // 16, kidx % 16
                t0, L = GROUPS[g]
                g_eng.wait_ge(s_kev, kidx + 1)
                g_eng.dma_start(k_dram.ap()[fc * 128:(fc + 1) * 128, t0:t0 + L],
                                k_out[:, (kidx % 2) * 512:(kidx % 2) * 512 + L]).then_inc(s_kst[kidx % 2], 16)
                if wvp and kidx % 4 == 1:
                    jj = wvp.pop(0)
                    g_eng.wait_ge(s_wwld[jj % 2], 16 * (jj // 2 + 1))
                    g_eng.tensor_copy(wv_sb[:, (jj - 16) * 2048:(jj - 15) * 2048],
                                      wwst.ap().rearrange("p (b f) -> p b f", b=2)[:, jj % 2]).then_inc(s_wwcv, 1)
            for jj in wvp:
                g_eng.wait_ge(s_wwld[jj % 2], 16 * (jj // 2 + 1))
                g_eng.tensor_copy(wv_sb[:, (jj - 16) * 2048:(jj - 15) * 2048],
                                  wwst.ap().rearrange("p (b f) -> p b f", b=2)[:, jj % 2]).then_inc(s_wwcv, 1)
            vidx = 0
            for g2, (t0, L) in enumerate(GROUPS):
                for tloc in range(L // 128):
                    g_eng.wait_ge(s_vev, vidx + 1)
                    r0 = t0 + tloc * 128
                    g_eng.dma_start(ve_dram.ap()[r0:r0 + 128, :],
                                    ve_out[:, (vidx % 2) * 2048:(vidx % 2 + 1) * 2048]).then_inc(s_vst[vidx % 2], 16)
                    g_eng.dma_start(vo_dram.ap()[r0:r0 + 128, :],
                                    vo_out[:, (vidx % 2) * 2048:(vidx % 2 + 1) * 2048]).then_inc(s_vst[vidx % 2], 16)
                    vidx += 1
            g_eng.wait_ge(s_kst[0], 16 * 40)
            g_eng.wait_ge(s_kst[1], 16 * 40)
            g_eng.wait_ge(s_vst[0], 32 * ((NVT + 1) // 2))
            g_eng.wait_ge(s_vst[1], 32 * (NVT // 2))
        block_cm.__exit__(None, None, None)


def _attn_phase(nc, wqT, woT, xn2_dram, k_dram, ve_dram, vo_dram, x1_dram,
                x2_dram, consts, tag, dbg=None):
    maskA_sb, maskB_sb, ident_sb = consts["maskA_sb"], consts["maskB_sb"], consts["ident_sb"]
    wq3 = wqT.ap().rearrange("(c p) f -> p c f", p=128)
    wo3 = woT.ap().rearrange("(c p) f -> p c f", p=128)
    xnv = xn2_dram.ap().rearrange("(c p) t -> p c t", p=128)
    x1v = x1_dram.ap().rearrange("(c p) t -> p c t", p=128)
    x2v = x2_dram.ap().rearrange("(c p) t -> p c t", p=128)
    kv3 = k_dram.ap().rearrange("(c p) t -> p c t", p=128)
    with ExitStack() as ctx:
        ec = ctx.enter_context
        wq_sb = ec(nc.sbuf_tensor(f"{tag}_wq", [128, 16 * 2048], BF16))
        wo_sb = ec(nc.sbuf_tensor(f"{tag}_wo", [128, 16 * 2048], BF16))
        wst = ec(nc.sbuf_tensor(f"{tag}_wst", [128, 2 * 512], F32))
        kr = ec(nc.sbuf_tensor(f"{tag}_kr", [128, 2 * 2048], BF16))
        vre = ec(nc.sbuf_tensor(f"{tag}_vre", [128, 2 * 2048], BF16))
        vro = ec(nc.sbuf_tensor(f"{tag}_vro", [128, 2 * 2048], BF16))
        xn_sb = ec(nc.sbuf_tensor(f"{tag}_xn", [128, 2048], BF16))
        x1_sb = ec(nc.sbuf_tensor(f"{tag}_x1", [128, 2048], F32))
        qe_sb = ec(nc.sbuf_tensor(f"{tag}_qe", [128, 2048], BF16))
        qo_sb = ec(nc.sbuf_tensor(f"{tag}_qo", [128, 2048], BF16))
        s_sb = ec(nc.sbuf_tensor(f"{tag}_ssb", [128, 1024], F32))
        probs_sb = ec(nc.sbuf_tensor(f"{tag}_prb", [128, 2 * 1024], BF16))
        pt_sb = ec(nc.sbuf_tensor(f"{tag}_pt", [128, 4 * 128], BF16))
        ao_sb = ec(nc.sbuf_tensor(f"{tag}_ao", [128, 2048], BF16))
        x2_sb = ec(nc.sbuf_tensor(f"{tag}_x2", [128, 2048], F32))
        nm_sb = ec(nc.sbuf_tensor(f"{tag}_nm", [128, 8], F32))
        sum_sb = ec(nc.sbuf_tensor(f"{tag}_sum", [128, 8], F32))
        rec_sb = ec(nc.sbuf_tensor(f"{tag}_rec", [128, 8], F32))
        pQ = ec(nc.psum_tensor(f"{tag}_pQ", [128, 2048], F32))
        pS = ec(nc.psum_tensor(f"{tag}_pS", [128, 1024], F32))
        pT = ec(nc.psum_tensor(f"{tag}_pT", [128, 512], BF16))
        pV = ec(nc.psum_tensor(f"{tag}_pV", [128, 512], F32))
        names = ["wld0", "wld1", "wqcv", "kvl0", "kvl1", "xnl0", "xnl1",
                 "x1l0", "x1l1", "qmm", "qev", "smm", "sev", "max", "exp",
                 "nrm4", "tmm", "ptev", "pvp", "aoev", "omm", "x2ev", "x2st0",
                 "x2st1", "dbgst", "rcp"]
        sems = {n: ec(nc.semaphore(f"{tag}_{n}")) for n in names}
        _clear_sems(nc, list(sems.values()))
        (s_wld0, s_wld1, s_wqcv, s_kvl0, s_kvl1, s_xnl0, s_xnl1,
         s_x1l0, s_x1l1, s_qmm, s_qev, s_smm, s_sev, s_max, s_exp,
         s_nrm4, s_tmm, s_ptev, s_pvp, s_aoev, s_omm, s_x2ev, s_x2st0,
         s_x2st1, s_dbgst, s_rcp) = (sems[n] for n in names)
        block = ec(nc.Block())
        s_wld = [s_wld0, s_wld1]
        s_kvl = [s_kvl0, s_kvl1]
        s_xnl = [s_xnl0, s_xnl1]
        s_x1l = [s_x1l0, s_x1l1]
        s_x2st = [s_x2st0, s_x2st1]


        def kv_load(sync, b):
            sl = b % 2
            col = b * 128
            dst = kr.ap().rearrange("p (s c t) -> p s c t", s=2, c=16)[:, sl]
            src = kv3[:, :, col:col + 128]
            for q in range(4):
                sync.dma_start(dst[:, q * 4:(q + 1) * 4, :], src[:, q * 4:(q + 1) * 4, :]).then_inc(s_kvl[sl], 16)
            sync.dma_start(vre[:, sl * 2048:(sl + 1) * 2048], ve_dram.ap()[col:col + 128, :]).then_inc(s_kvl[sl], 16)
            sync.dma_start(vro[:, sl * 2048:(sl + 1) * 2048], vo_dram.ap()[col:col + 128, :]).then_inc(s_kvl[sl], 16)

        def xn_load(sync, b):
            col = b * 128
            dst = xn_sb.ap().rearrange("p (c t) -> p c t", c=16)
            for q in range(4):
                sync.dma_start(dst[:, q * 4:(q + 1) * 4, :], xnv[:, q * 4:(q + 1) * 4, col:col + 128]).then_inc(s_xnl0, 16)

        def x1_load(sync, b):
            col = b * 128
            dst1 = x1_sb.ap().rearrange("p (c t) -> p c t", c=16)
            for q in range(4):
                sync.dma_start(dst1[:, q * 4:(q + 1) * 4, :], x1v[:, q * 4:(q + 1) * 4, col:col + 128]).then_inc(s_x1l0, 16)

        @block.sync
        def _(sync):
            for w, (w3,) in enumerate(((wq3,), (wo3,))):
                for c in range(16):
                    for hf in range(4):
                        jj = w * 64 + c * 4 + hf
                        if jj >= 2:
                            sync.wait_ge(s_wqcv, jj - 1)
                        sync.dma_start(out=wst.ap().rearrange("p (b f) -> p b f", b=2)[:, jj % 2],
                                       in_=w3[:, c, hf * 512:(hf + 1) * 512]).then_inc(s_wld[jj % 2], 16)
            kv_load(sync, 0)
            kv_load(sync, 1)
            xn_load(sync, 1)
            x1_load(sync, 1)
            for b in range(1, NBLK + 1):
                bi = b - 1
                if b + 1 <= NBLK:
                    sync.wait_ge(s_pvp, 16 * b)
                    kv_load(sync, b + 1)
                    sync.wait_ge(s_qmm, b)
                    xn_load(sync, b + 1)
                    sync.wait_ge(s_x2ev, b)
                    x1_load(sync, b + 1)

        @block.tensor
        def _(tensor):
            for b in range(1, NBLK + 1):
                bi = b - 1
                p = b % 2
                tensor.wait_ge(s_xnl0, 64 * b)
                if b == 1:
                    tensor.wait_ge(s_wqcv, 64)
                if bi >= 1:
                    tensor.wait_ge(s_x2ev, bi)
                for fc in range(16):
                    for dc in range(16):
                        mm = tensor.matmul(
                            pQ[:, fc * 128:(fc + 1) * 128],
                            wq_sb[:, dc * 2048 + fc * 128: dc * 2048 + fc * 128 + 128],
                            xn_sb[:, dc * 128:(dc + 1) * 128],
                            start=(dc == 0), stop=(dc == 15))
                mm.then_inc(s_qmm, 1)
                slp, slc = (b - 1) % 2, b % 2
                tensor.wait_ge(s_kvl[slp], 96 * ((b - 1) // 2 + 1))
                tensor.wait_ge(s_kvl[slc], 96 * (b // 2 + 1))
                if dbg is not None and bi >= 1:
                    tensor.wait_ge(s_dbgst, 144 * bi)
                tensor.wait_ge(s_qev, bi + 1)
                for hb in range(8):
                    B = 8 * bi + hb
                    tensor.wait_ge(s_sev, B)
                    for j in range(4):
                        h = hb * 4 + j
                        qz = qe_sb if h % 2 == 0 else qo_sb
                        qs = qz[:, (h // 2) * 128:(h // 2) * 128 + 128]
                        for half, sl in ((0, slp), (1, slc)):
                            ks = kr[:, sl * 2048 + (h // 2) * 128: sl * 2048 + (h // 2) * 128 + 128]
                            mm = tensor.matmul(pS[:, j * 256 + half * 128: j * 256 + half * 128 + 128],
                                               qs, ks, start=True, stop=True)
                    mm.then_inc(s_smm, 1)
                    tensor.wait_ge(s_nrm4, B + 1)
                    for j in range(4):
                        h = hb * 4 + j
                        hh = 32 * bi + h
                        if hh >= 2:
                            tensor.wait_ge(s_ptev, 2 * (hh - 2) + 2)
                        for half in (0, 1):
                            sc = ((h % 2) * 2 + half) * 128
                            mm = tensor.transpose(
                                pT[:, sc:sc + 128],
                                probs_sb[:, (hb % 2) * 1024 + j * 256 + half * 128: (hb % 2) * 1024 + j * 256 + half * 128 + 128],
                                ident_sb[:, :])
                        mm.then_inc(s_tmm, 1)
                        hp = hh // 2
                        tensor.wait_ge(s_ptev, 2 * hh + 2)
                        if h % 2 == 0 and hp >= 4:
                            tensor.wait_ge(s_aoev, hp - 3)
                        vz = vre if h % 2 == 0 else vro
                        for half, sl in ((0, slp), (1, slc)):
                            sc = ((h % 2) * 2 + half) * 128
                            mm = tensor.matmul(
                                pV[:, (hp % 4) * 128:(hp % 4) * 128 + 128],
                                vz[:, sl * 2048 + (h // 2) * 128: sl * 2048 + (h // 2) * 128 + 128],
                                pt_sb[:, sc:sc + 128],
                                start=(h % 2 == 0 and half == 0),
                                stop=(h % 2 == 1 and half == 1))
                        if h % 2 == 1:
                            mm.then_inc(s_pvp, 1)
                tensor.wait_ge(s_aoev, 16 * bi + 16)
                if b == 1:
                    tensor.wait_ge(s_wqcv, 64)
                for dc in range(16):
                    for fc in range(16):
                        mm = tensor.matmul(
                            pQ[:, dc * 128:(dc + 1) * 128],
                            wo_sb[:, fc * 2048 + dc * 128: fc * 2048 + dc * 128 + 128],
                            ao_sb[:, fc * 128:(fc + 1) * 128],
                            start=(fc == 0), stop=(fc == 15))
                mm.then_inc(s_omm, 1)

        @block.vector
        def _(vector):
            for b in range(1, NBLK + 1):
                bi = b - 1
                p = b % 2
                if b == 1:
                    vector.memset(qe_sb[64:128, :], 0.0)
                    vector.memset(qo_sb[0:64, :], 0.0)
                vector.wait_ge(s_qmm, bi + 1)
                if dbg is not None and bi >= 1:
                    vector.wait_ge(s_dbgst, 144 * (bi - 1) + 64)
                vector.tensor_copy(qe_sb[0:64, :], pQ[0:64, :])
                vector.tensor_copy(qo_sb[64:128, :], pQ[64:128, :]).then_inc(s_qev, 1)
                for hb in range(8):
                    B = 8 * bi + hb
                    vector.wait_ge(s_smm, B + 1)
                    if B >= 1:
                        vector.wait_ge(s_exp, 4 * B)
                    msk = maskA_sb if b == 1 else maskB_sb
                    for j in range(4):
                        op = vector.scalar_tensor_tensor(
                            s_sb[:, j * 256:(j + 1) * 256],
                            pS[:, j * 256:(j + 1) * 256],
                            0.125, msk[:], ALU.mult, ALU.add)
                    op.then_inc(s_sev, 1)
                    vector.tensor_reduce(nm_sb[:, (hb % 2) * 4:(hb % 2) * 4 + 4],
                                         s_sb.ap().rearrange("p (j c) -> p j c", c=256),
                                         AX.X, ALU.max, negate=True).then_inc(s_max, 1)
                    vector.wait_ge(s_exp, 4 * B + 4)
                    vector.tensor_reduce(sum_sb[:, (hb % 2) * 4:(hb % 2) * 4 + 4],
                                         probs_sb.ap().rearrange("p (a j c) -> p a j c", a=2, c=256)[:, hb % 2],
                                         AX.X, ALU.add).then_inc(s_rcp, 1)
                    vector.wait_ge(s_rcp, 2 * B + 1)
                    vector.reciprocal(rec_sb[:, (hb % 2) * 4:(hb % 2) * 4 + 4],
                                      sum_sb[:, (hb % 2) * 4:(hb % 2) * 4 + 4]).then_inc(s_rcp, 1)
                    vector.wait_ge(s_rcp, 2 * B + 2)
                    for j in range(4):
                        op = vector.tensor_scalar_mul(
                            probs_sb[:, (hb % 2) * 1024 + j * 256:(hb % 2) * 1024 + j * 256 + 256],
                            probs_sb[:, (hb % 2) * 1024 + j * 256:(hb % 2) * 1024 + j * 256 + 256],
                            rec_sb[:, (hb % 2) * 4 + j:(hb % 2) * 4 + j + 1])
                    op.then_inc(s_nrm4, 1)
                vector.wait_ge(s_omm, bi + 1)
                vector.wait_ge(s_x1l0, 64 * (bi + 1))
                if bi >= 1:
                    vector.wait_ge(s_x2st0, 64 * bi)
                vector.tensor_add(x2_sb[:], pQ[:], x1_sb[:]).then_inc(s_x2ev, 1)

        @block.scalar
        def _(scalar):
            for b in range(1, NBLK + 1):
                bi = b - 1
                for hb in range(8):
                    B = 8 * bi + hb
                    scalar.wait_ge(s_max, B + 1)
                    if B >= 2:
                        scalar.wait_ge(s_tmm, 4 * (B - 2) + 4)
                    if dbg is not None and hb == 2:
                        scalar.wait_ge(s_dbgst, 144 * bi + 80)
                    for j in range(4):
                        scalar.activation(
                            probs_sb[:, (hb % 2) * 1024 + j * 256:(hb % 2) * 1024 + j * 256 + 256],
                            s_sb[:, j * 256:(j + 1) * 256],
                            AF.Exp, bias=nm_sb[:, (hb % 2) * 4 + j:(hb % 2) * 4 + j + 1],
                            scale=1.0,
                            accum_out=sum_sb[:, (hb % 2) * 4 + j:(hb % 2) * 4 + j + 1]).then_inc(s_exp, 1)
                    for j in range(4):
                        h = hb * 4 + j
                        hh = 32 * bi + h
                        scalar.wait_ge(s_tmm, hh + 1)
                        if hh >= 2:
                            scalar.wait_ge(s_pvp, (hh - 2) // 2 + 1)
                        for half in (0, 1):
                            sc = ((h % 2) * 2 + half) * 128
                            scalar.copy(pt_sb[:, sc:sc + 128], pT[:, sc:sc + 128]).then_inc(s_ptev, 1)
                        if h % 2 == 1:
                            hp = hh // 2
                            scalar.wait_ge(s_pvp, 16 * bi + (h - 1) // 2 + 1)
                            pl = (h - 1) // 2
                            scalar.copy(ao_sb[:, pl * 128:(pl + 1) * 128],
                                        pV[:, (hp % 4) * 128:(hp % 4) * 128 + 128]).then_inc(s_aoev, 1)

        @block.gpsimd
        def _(g_eng):
            for w in range(2):
                for c in range(16):
                    for hf in range(4):
                        jj = w * 64 + c * 4 + hf
                        g_eng.wait_ge(s_wld[jj % 2], 16 * (jj // 2 + 1))
                        dst = wq_sb if w == 0 else wo_sb
                        g_eng.tensor_copy(dst[:, c * 2048 + hf * 512: c * 2048 + (hf + 1) * 512],
                                          wst.ap().rearrange("p (b f) -> p b f", b=2)[:, jj % 2]).then_inc(s_wqcv, 1)
            for b in range(1, NBLK + 1):
                bi = b - 1
                if dbg is not None:
                    qv = dbg["q_dump"].ap().rearrange("(c p) t -> p c t", p=128)
                    av = dbg["ao_dump"].ap().rearrange("(c p) t -> p c t", p=128)
                    g_eng.wait_ge(s_qev, bi + 1)
                    for qq in range(2):
                        g_eng.dma_start(qv[0:64, qq * 8:(qq + 1) * 8, bi * 128:(bi + 1) * 128],
                                        qe_sb.ap().rearrange("p (c t) -> p c t", c=16)[0:64, qq * 8:(qq + 1) * 8, :]).then_inc(s_dbgst, 16)
                    for qq in range(2):
                        g_eng.dma_start(qv[64:128, qq * 8:(qq + 1) * 8, bi * 128:(bi + 1) * 128],
                                        qo_sb.ap().rearrange("p (c t) -> p c t", c=16)[64:128, qq * 8:(qq + 1) * 8, :]).then_inc(s_dbgst, 16)
                    g_eng.wait_ge(s_nrm4, 8 * bi + 1)
                    g_eng.dma_start(dbg["pr_dump"].ap()[:, bi * 1024:(bi + 1) * 1024],
                                    probs_sb[:, 0:1024]).then_inc(s_dbgst, 16)
                    g_eng.wait_ge(s_aoev, 16 * bi + 16)
                    asv = ao_sb.ap().rearrange("p (c t) -> p c t", c=16)
                    for qq in range(4):
                        g_eng.dma_start(av[:, qq * 4:(qq + 1) * 4, bi * 128:(bi + 1) * 128],
                                        asv[:, qq * 4:(qq + 1) * 4, :]).then_inc(s_dbgst, 16)
                g_eng.wait_ge(s_x2ev, bi + 1)
                dst = x2v[:, :, bi * 128:(bi + 1) * 128]
                src = x2_sb.ap().rearrange("p (c t) -> p c t", c=16)
                for q in range(4):
                    g_eng.dma_start(dst[:, q * 4:(q + 1) * 4, :], src[:, q * 4:(q + 1) * 4, :]).then_inc(s_x2st0, 16)
            g_eng.wait_ge(s_x2st0, 64 * NBLK)
            if dbg is not None:
                g_eng.wait_ge(s_dbgst, 144 * NBLK)


def _build(taps=(), upto=99, race_detect=True):
    nc = bass.Bass("TRN2", target_bir_lowering=False, detect_race_conditions=race_detect)
    xT = nc.dram_tensor("xT", [D, T_ALL], F32, kind="ExternalInput")
    gwT = nc.dram_tensor("gwT", [D, IF], F32, kind="ExternalInput")
    uwT = nc.dram_tensor("uwT", [D, IF], F32, kind="ExternalInput")
    dwT = nc.dram_tensor("dwT", [IF, D], F32, kind="ExternalInput")
    gs = nc.dram_tensor("gs", [128, 44], F32, kind="ExternalInput")
    us = nc.dram_tensor("us", [128, 44], F32, kind="ExternalInput")
    ds = nc.dram_tensor("ds", [128, 16], F32, kind="ExternalInput")
    wqT = nc.dram_tensor("wqT", [D, D], F32, kind="ExternalInput")
    wkT = nc.dram_tensor("wkT", [D, D], F32, kind="ExternalInput")
    wvT = nc.dram_tensor("wvT", [D, D], F32, kind="ExternalInput")
    woT = nc.dram_tensor("woT", [D, D], F32, kind="ExternalInput")
    maskA = nc.dram_tensor("maskA", [128, 256], F32, kind="ExternalInput")
    maskB = nc.dram_tensor("maskB", [128, 256], F32, kind="ExternalInput")
    ident = nc.dram_tensor("ident", [128, 128], BF16, kind="ExternalInput")

    def scratch(name, shape, dt):
        kind = "ExternalOutput" if name in taps else "Internal"
        return nc.dram_tensor(name, shape, dt, kind=kind)

    h_dram = scratch("h_dram", [IF, T_ALL], BF16)
    x1_dram = scratch("x1_dram", [D, T_ALL], F32)
    k_dram = scratch("k_dram", [D, T_ALL], BF16)
    ve_dram = scratch("ve_dram", [T_ALL, D], BF16)
    vo_dram = scratch("vo_dram", [T_ALL, D], BF16)
    xn2_dram = scratch("xn2_dram", [D, T_ALL], BF16)
    x2_dram = scratch("x2_dram", [D, T_OWN], F32)
    outT = nc.dram_tensor("outT", [D, T_OWN], F32, kind="ExternalOutput")

    x_view3 = xT.ap().rearrange("(c p) t -> p c t", p=128)
    x1_view3 = x1_dram.ap().rearrange("(c p) t -> p c t", p=128)
    x2_view3 = x2_dram.ap().rearrange("(c p) t -> p c t", p=128)

    with (
        nc.sbuf_tensor("gs_sb", [128, 44], F32) as gs_sb,
        nc.sbuf_tensor("us_sb", [128, 44], F32) as us_sb,
        nc.sbuf_tensor("ds2_sb", [128, 16], F32) as ds2_sb,
        nc.sbuf_tensor("maskA_sb", [128, 256], F32) as maskA_sb,
        nc.sbuf_tensor("maskB_sb", [128, 256], F32) as maskB_sb,
        nc.sbuf_tensor("ident_sb", [128, 128], BF16) as ident_sb,
        nc.sbuf_tensor("ones128", [128, 128], BF16) as ones128,
        nc.sbuf_tensor("eps_col", [128, 1], F32) as eps_col,
    ):
        consts = dict(gs_sb=gs_sb, us_sb=us_sb, ds2_sb=ds2_sb, maskA_sb=maskA_sb,
                      maskB_sb=maskB_sb, ident_sb=ident_sb, ones128=ones128,
                      eps_col=eps_col)
        with (
            nc.semaphore("p0ld") as p0ld,
            nc.Block() as block,
        ):
            @block.sync
            def _(sync):
                sync.dma_start(out=gs_sb[:], in_=gs.ap()).then_inc(p0ld, 16)
                sync.dma_start(out=us_sb[:], in_=us.ap()).then_inc(p0ld, 16)
                sync.dma_start(out=ds2_sb[:], in_=ds.ap()).then_inc(p0ld, 16)
                sync.dma_start(out=maskA_sb[:], in_=maskA.ap()).then_inc(p0ld, 16)
                sync.dma_start(out=maskB_sb[:], in_=maskB.ap()).then_inc(p0ld, 16)
                sync.dma_start(out=ident_sb[:], in_=ident.ap()).then_inc(p0ld, 16)

            @block.vector
            def _(vector):
                vector.wait_ge(p0ld, 96)
                vector.tensor_scalar_mul(ds2_sb[:], ds2_sb[:], 0.5)
                vector.memset(ones128[:], 1.0)
                vector.memset(eps_col[:], EPS)

        if upto < 1:
            return nc
        with nc.sbuf_tensor("xn1", [128, NCH * T_ALL], BF16) as xn1_sb:
            with (
                nc.sbuf_tensor("p1_xst", [128, 2 * 4096], F32) as p1_xst,
                nc.sbuf_tensor("p1_tmp", [128, 4096], F32) as p1_tmp,
                nc.sbuf_tensor("p1_xc", [128, 4096], BF16) as p1_xc,
                nc.sbuf_tensor("p1_xsq", [128, 4096], BF16) as p1_xsq,
                nc.sbuf_tensor("p1_rows", [128, 1024], F32) as p1_rows,
                nc.psum_tensor("p1_pstat", [128, 512], F32) as p1_pstat,
            ):
                bt = dict(xst=[p1_xst.ap()[:, 0:4096], p1_xst.ap()[:, 4096:8192]],
                          tmp=p1_tmp, xc=p1_xc, xsq=p1_xsq, rows=p1_rows,
                          pstat=p1_pstat)
                _ln_phase(nc, bt, x_view3, xn1_sb, T_ALL, LN_STEPS, consts, "ln1")
            if "xn1_dump" in taps:
                xn1_dump = nc.dram_tensor("xn1_dump", [128, NCH * T_ALL], BF16, kind="ExternalOutput")
                with nc.semaphore("dmp") as sdmp:
                    _clear_sems(nc, [sdmp])
                with nc.semaphore("dmp2") as sdmp, nc.Block() as block:
                    @block.gpsimd
                    def _(g_eng):
                        for c in range(NCH):
                            g_eng.dma_start(xn1_dump.ap()[:, c * T_ALL:(c + 1) * T_ALL],
                                            xn1_sb[:, c * T_ALL:(c + 1) * T_ALL]).then_inc(sdmp, 16)
                        g_eng.wait_ge(sdmp, 16 * NCH)
            if upto < 2:
                return nc
            _gateup_phase(nc, xn1_sb, T_ALL, GROUPS, gwT, uwT, h_dram, consts, "g1")

        if upto < 3:
            return nc
        _down_phase(nc, dwT, h_dram, x_view3, x1_dram, GROUPS, consts, "d1")

        if upto < 4:
            return nc
        with (
            nc.sbuf_tensor("p3_xst", [128, 2 * 4096], F32) as p3_xst,
            nc.sbuf_tensor("p3_tmp", [128, 4096], F32) as p3_tmp,
            nc.sbuf_tensor("p3_xc", [128, 4096], BF16) as p3_xc,
            nc.sbuf_tensor("p3_xsq", [128, 4096], BF16) as p3_xsq,
            nc.sbuf_tensor("p3_rows", [128, 1024], F32) as p3_rows,
            nc.sbuf_tensor("p3_xout", [128, 2 * 4096], BF16) as p3_xout,
            nc.psum_tensor("p3_pstat", [128, 512], F32) as p3_pstat,
        ):
            bt = dict(xst=[p3_xst.ap()[:, 0:4096], p3_xst.ap()[:, 4096:8192]],
                      tmp=p3_tmp, xc=p3_xc, xsq=p3_xsq, rows=p3_rows,
                      pstat=p3_pstat)
            _ln_phase(nc, bt, x1_view3, p3_xout, T_ALL, LN_STEPS, consts, "ln2",
                      dram_out=xn2_dram)
        if upto < 5:
            return nc
        _kv_phase(nc, wkT, wvT, xn2_dram, k_dram, ve_dram, vo_dram, consts, "kv")

        if upto < 6:
            return nc
        dbg = None
        if "q_dump" in taps:
            dbg = {"q_dump": nc.dram_tensor("q_dump", [D, T_OWN], BF16, kind="ExternalOutput"),
                   "ao_dump": nc.dram_tensor("ao_dump", [D, T_OWN], BF16, kind="ExternalOutput"),
                   "pr_dump": nc.dram_tensor("pr_dump", [128, NBLK * 1024], BF16, kind="ExternalOutput")}
        _attn_phase(nc, wqT, woT, xn2_dram, k_dram, ve_dram, vo_dram, x1_dram,
                    x2_dram, consts, "at", dbg=dbg)

        if upto < 7:
            return nc
        with nc.sbuf_tensor("xn3", [128, NCH * T_OWN], BF16) as xn3_sb:
            with (
                nc.sbuf_tensor("p5_xst", [128, 2 * 4096], F32) as p5_xst,
                nc.sbuf_tensor("p5_tmp", [128, 4096], F32) as p5_tmp,
                nc.sbuf_tensor("p5_xc", [128, 4096], BF16) as p5_xc,
                nc.sbuf_tensor("p5_xsq", [128, 4096], BF16) as p5_xsq,
                nc.sbuf_tensor("p5_rows", [128, 1024], F32) as p5_rows,
                nc.psum_tensor("p5_pstat", [128, 512], F32) as p5_pstat,
            ):
                bt = dict(xst=[p5_xst.ap()[:, 0:4096], p5_xst.ap()[:, 4096:8192]],
                          tmp=p5_tmp, xc=p5_xc, xsq=p5_xsq, rows=p5_rows,
                          pstat=p5_pstat)
                _ln_phase(nc, bt, x2_view3, xn3_sb, T_OWN, LN_STEPS_OWN, consts, "ln3")
            _gateup_phase(nc, xn3_sb, T_OWN, GROUPS_OWN, gwT, uwT, h_dram, consts,
                          "g2", col_off=HALO)

        if upto < 8:
            return nc
        _down_phase(nc, dwT, h_dram, x2_view3, outT, GROUPS_OWN, consts, "d2",
                    h_col_off=HALO, out_col_off=0)

    return nc



def _prep_maps(L0_attn_x, L0_attn_Wq, L0_attn_Wk, L0_attn_Wv, L0_attn_Wo,
               L0_ffn_gate_w, L0_ffn_up_w, L0_ffn_down_w,
               L0_ffn_gate_s, L0_ffn_up_s, L0_ffn_down_s):
    x = np.asarray(L0_attn_x, np.float32)
    common = {
        "gwT": np.ascontiguousarray(np.asarray(L0_ffn_gate_w, np.float32).T),
        "uwT": np.ascontiguousarray(np.asarray(L0_ffn_up_w, np.float32).T),
        "dwT": np.ascontiguousarray(np.asarray(L0_ffn_down_w, np.float32).T),
        "gs": np.ascontiguousarray(np.asarray(L0_ffn_gate_s, np.float32).reshape(44, 128).T),
        "us": np.ascontiguousarray(np.asarray(L0_ffn_up_s, np.float32).reshape(44, 128).T),
        "ds": np.ascontiguousarray(np.asarray(L0_ffn_down_s, np.float32).reshape(16, 128).T),
        "wqT": np.ascontiguousarray(np.asarray(L0_attn_Wq, np.float32).T),
        "wkT": np.ascontiguousarray(np.asarray(L0_attn_Wk, np.float32).T),
        "wvT": np.ascontiguousarray(np.asarray(L0_attn_Wv, np.float32).T),
        "woT": np.ascontiguousarray(np.asarray(L0_attn_Wo, np.float32).T),
        "ident": np.eye(128, dtype=ml_dtypes.bfloat16),
    }
    r = np.arange(128)[:, None]
    c = np.arange(256)[None, :]
    validB = (c >= r) & (c <= r + 128)
    maskB = np.where(validB, np.float32(0.0), np.float32(NEG))
    maskA0 = np.where(validB & (c >= 128), np.float32(0.0), np.float32(NEG))
    in_maps = []
    for core in range(8):
        b, hf = core // 2, core % 2
        t0 = hf * T_OWN
        own = x[b, t0:t0 + T_OWN]
        halo = x[b, t0 - HALO:t0] if hf == 1 else np.zeros((HALO, D), np.float32)
        xT_c = np.ascontiguousarray(np.concatenate([halo, own], 0).T)
        m = dict(common)
        m["xT"] = xT_c
        m["maskA"] = maskB if hf == 1 else maskA0
        m["maskB"] = maskB
        in_maps.append(m)
    return in_maps


def kernel(**inputs):
    in_maps = _prep_maps(**inputs)
    nc = _build()
    res = run_bass_kernel_spmd(nc, in_maps, core_ids=list(range(8)))
    out = np.empty((4, 4096, D), np.float32)
    for core in range(8):
        b, hf = core // 2, core % 2
        out[b, hf * T_OWN:(hf + 1) * T_OWN] = res.results[core]["outT"].T
    return out


# version 04
# speedup vs baseline: 1.0644x; 1.0074x over previous
from contextlib import ExitStack
import numpy as np
import ml_dtypes
import concourse.bass as bass
import concourse.mybir as mybir
from concourse.bass_utils import run_bass_kernel_spmd

F32 = mybir.dt.float32
BF16 = mybir.dt.bfloat16
AF = mybir.ActivationFunctionType
ALU = mybir.AluOpType
AX = mybir.AxisListType

D = 2048
NCH = 16
IF = 5632
ICN = 44
T_OWN = 2048
HALO = 128
T_ALL = 2176
NBLK = 16
EPS = 1e-5
NEG = -1.0e30

GROUPS = [(0, 512), (512, 512), (1024, 512), (1536, 512), (2048, 128)]
LN_STEPS = [(i * 256, 256) for i in range(8)] + [(2048, 128)]
GROUPS_OWN = [(0, 512), (512, 512), (1024, 512), (1536, 512)]
LN_STEPS_OWN = [(i * 256, 256) for i in range(8)]



def _clear_sems(nc, sems):
    with nc.Block() as blk:
        @blk.gpsimd
        def _(g):
            for sem in sems:
                g.sem_clear(sem)


def _ln_phase(nc, block_tensors, xin_view3, xn_out_sb, t_total, steps, consts, tag,
              dram_out=None, sync_hook=None, pool_hook=None):
    ones128 = consts["ones128"]
    t = block_tensors
    ns = len(steps)
    with (
        nc.semaphore(f"{tag}_xld0") as s_xld0,
        nc.semaphore(f"{tag}_xld1") as s_xld1,
        nc.semaphore(f"{tag}_v2p") as s_v2p,
        nc.semaphore(f"{tag}_pst") as s_pst,
        nc.semaphore(f"{tag}_var") as s_var,
        nc.semaphore(f"{tag}_sd") as s_sd,
        nc.semaphore(f"{tag}_nrm") as s_nrm,
        nc.semaphore(f"{tag}_ost0") as s_ost0,
        nc.semaphore(f"{tag}_ost1") as s_ost1,
    ):
        _clear_sems(nc, [s_xld0, s_xld1, s_v2p, s_pst, s_var, s_sd, s_nrm, s_ost0, s_ost1])
        block = None
        block_cm = nc.Block()
        block = block_cm.__enter__()
        s_xld = [s_xld0, s_xld1]
        s_ost = [s_ost0, s_ost1]

        @block.sync
        def _(sync):
            for s, (t0, L) in enumerate(steps):
                if s >= 2:
                    sync.wait_ge(s_nrm, s - 1)
                xst = t["xst"][s % 2]
                v = xst.rearrange("p (c t) -> p c t", t=256)
                sync.dma_start(out=v[:, 0:8, 0:L], in_=xin_view3[:, 0:8, t0:t0 + L]).then_inc(s_xld[s % 2], 16)
                sync.dma_start(out=v[:, 8:16, 0:L], in_=xin_view3[:, 8:16, t0:t0 + L]).then_inc(s_xld[s % 2], 16)
                if sync_hook is not None:
                    sync_hook(sync, s)

        @block.vector
        def _(vector):
            for s, (t0, L) in enumerate(steps):
                xst = t["xst"][s % 2]
                xv = xst.rearrange("p (c t) -> p c t", t=256)[:, :, 0:L]
                xcv = t["xc"].ap().rearrange("p (c t) -> p c t", t=256)[:, :, 0:L]
                xqv = t["xsq"].ap().rearrange("p (c t) -> p c t", t=256)[:, :, 0:L]
                vector.wait_ge(s_xld[s % 2], 32 * (s // 2 + 1))
                vector.tensor_copy(xcv, xv)
                vector.tensor_tensor(xqv, xcv, xcv, ALU.mult).then_inc(s_v2p, 1)
                vector.wait_ge(s_pst, s + 1)
                rows = t["rows"]
                m = rows[:, 0:256]
                t1 = rows[:, 256:512]
                sd = rows[:, 512:768]
                r = rows[:, 768:1024]
                vector.tensor_scalar_mul(m[:, 0:L], t["pstat"][:, 0:L], 1.0 / D)
                vector.tensor_scalar_mul(t1[:, 0:L], t["pstat"][:, 256:256 + L], 1.0 / D).then_inc(s_var, 1)
                vector.tensor_tensor(sd[:, 0:L], m[:, 0:L], m[:, 0:L], ALU.mult)
                vector.tensor_sub(t1[:, 0:L], t1[:, 0:L], sd[:, 0:L]).then_inc(s_var, 1)
                vector.wait_ge(s_sd, s + 1)
                vector.reciprocal(r[:, 0:L], sd[:, 0:L])
                tmpv = t["tmp"].ap().rearrange("p (c t) -> p c t", t=256)[:, :, 0:L]
                mb = m[:, 0:L].unsqueeze(1).broadcast_to([128, 16, L])
                rb = r[:, 0:L].unsqueeze(1).broadcast_to([128, 16, L])
                vector.tensor_sub(tmpv, xv, mb)
                if dram_out is None:
                    xout = xn_out_sb.ap().rearrange("p (c t) -> p c t", t=t_total)[:, :, t0:t0 + L]
                else:
                    if s >= 2:
                        vector.wait_ge(s_ost[s % 2], 64 * ((s - 2) // 2 + 1))
                    xout = xn_out_sb.ap().rearrange("p (b c t) -> p b c t", b=2, c=NCH)[:, s % 2, :, 0:L]
                vector.tensor_tensor(xout, tmpv, rb, ALU.mult).then_inc(s_nrm, 1)

        @block.tensor
        def _(tensor):
            for s, (t0, L) in enumerate(steps):
                tensor.wait_ge(s_v2p, s + 1)
                if s >= 1:
                    tensor.wait_ge(s_var, 2 * s - 1)
                xcv = t["xc"].ap().rearrange("p (c t) -> p c t", t=256)
                xqv = t["xsq"].ap().rearrange("p (c t) -> p c t", t=256)
                for c in range(NCH):
                    tensor.matmul(t["pstat"][:, 0:L], ones128[:, :], xcv[:, c, 0:L],
                                  start=(c == 0), stop=(c == NCH - 1))
                for c in range(NCH):
                    mm = tensor.matmul(t["pstat"][:, 256:256 + L], ones128[:, :], xqv[:, c, 0:L],
                                       start=(c == 0), stop=(c == NCH - 1))
                mm.then_inc(s_pst, 1)

        @block.scalar
        def _(scalar):
            for s, (t0, L) in enumerate(steps):
                scalar.wait_ge(s_var, 2 * s + 2)
                rows = t["rows"]
                scalar.activation(rows[:, 512:512 + L], rows[:, 256:256 + L], AF.Sqrt,
                                  bias=consts["eps_col"][:, 0:1], scale=1.0).then_inc(s_sd, 1)

        if dram_out is not None or pool_hook is not None:
            @block.gpsimd
            def _(g_eng):
                o3 = None if dram_out is None else dram_out.ap().rearrange("(c p) t -> p c t", p=128)
                for s, (t0, L) in enumerate(steps):
                    if pool_hook is not None:
                        pool_hook(g_eng, s)
                    if dram_out is not None:
                        g_eng.wait_ge(s_nrm, s + 1)
                        v = xn_out_sb.ap().rearrange("p (b c t) -> p b c t", b=2, c=NCH)[:, s % 2]
                        for q in range(4):
                            g_eng.dma_start(o3[:, q * 4:(q + 1) * 4, t0:t0 + L],
                                            v[:, q * 4:(q + 1) * 4, 0:L]).then_inc(s_ost[s % 2], 16)
                if dram_out is not None:
                    g_eng.wait_ge(s_ost[0], 64 * ((ns + 1) // 2))
                    g_eng.wait_ge(s_ost[1], 64 * (ns // 2))
        block_cm.__exit__(None, None, None)


def _gateup_phase(nc, xn_sb, t_total, groups, gwT, uwT, h_dram, consts, tag,
                  col_off=0):
    gs_sb, us_sb = consts["gs_sb"], consts["us_sb"]
    NSUP = 22
    n_per_sup = 2 * len(groups)
    NIT = NSUP * n_per_sup
    with (
        nc.sbuf_tensor(f"{tag}_gwst", [128, 2 * 4096], F32) as gwst,
        nc.sbuf_tensor(f"{tag}_uwst", [128, 2 * 4096], F32) as uwst,
        nc.sbuf_tensor(f"{tag}_gwbf", [128, 2 * 4096], BF16) as gwbf,
        nc.sbuf_tensor(f"{tag}_uwbf", [128, 2 * 4096], BF16) as uwbf,
        nc.sbuf_tensor(f"{tag}_silu", [128, 3 * 512], F32) as silu_sb,
        nc.sbuf_tensor(f"{tag}_hsb", [128, 3 * 512], BF16) as h_sb,
        nc.psum_tensor(f"{tag}_pG", [128, 3 * 512], F32) as pG,
        nc.psum_tensor(f"{tag}_pU", [128, 3 * 512], F32) as pU,
        nc.semaphore(f"{tag}_wld0") as s_wld0,
        nc.semaphore(f"{tag}_wld1") as s_wld1,
        nc.semaphore(f"{tag}_wcv") as s_wcv,
        nc.semaphore(f"{tag}_pmm") as s_pmm,
        nc.semaphore(f"{tag}_sil") as s_sil,
        nc.semaphore(f"{tag}_hm") as s_hm,
        nc.semaphore(f"{tag}_hst0") as s_hst0,
        nc.semaphore(f"{tag}_hst1") as s_hst1,
        nc.semaphore(f"{tag}_hst2") as s_hst2,
    ):
        _clear_sems(nc, [s_wld0, s_wld1, s_wcv, s_pmm, s_sil, s_hm, s_hst0, s_hst1, s_hst2])
        block_cm = nc.Block()
        block = block_cm.__enter__()
        gw3 = gwT.ap().rearrange("(c p) i -> p c i", p=128)
        uw3 = uwT.ap().rearrange("(c p) i -> p c i", p=128)
        s_wld = [s_wld0, s_wld1]
        s_hst = [s_hst0, s_hst1, s_hst2]

        @block.sync
        def _(sync):
            for s2 in range(NSUP):
                if s2 >= 2:
                    sync.wait_ge(s_pmm, n_per_sup * (s2 - 1))
                    sync.wait_ge(s_wcv, 2 * (s2 - 1))
                i0 = s2 * 256
                par = s2 % 2
                for (st, w3) in ((gwst, gw3), (uwst, uw3)):
                    v = st.ap().rearrange("p (b c i) -> p b c i", b=2, c=16)[:, par]
                    sync.dma_start(out=v[:, 0:8, :], in_=w3[:, 0:8, i0:i0 + 256]).then_inc(s_wld[par], 16)
                    sync.dma_start(out=v[:, 8:16, :], in_=w3[:, 8:16, i0:i0 + 256]).then_inc(s_wld[par], 16)

        @block.vector
        def _(vector):
            n = 0
            for s2 in range(NSUP):
                for m in range(n_per_sup):
                    ic_loc = m // len(groups)
                    g = m % len(groups)
                    L = groups[g][1]
                    ic = s2 * 2 + ic_loc
                    p2 = n % 3
                    vector.wait_ge(s_sil, n + 1)
                    if n >= 3:
                        vector.wait_ge(s_hst[n % 3], 16 * ((n - 3) // 3 + 1))
                    vector.scalar_tensor_tensor(
                        h_sb[:, p2 * 512:p2 * 512 + L], pU[:, p2 * 512:p2 * 512 + L],
                        us_sb[:, ic:ic + 1], silu_sb[:, p2 * 512:p2 * 512 + L],
                        ALU.mult, ALU.mult).then_inc(s_hm, 1)
                    n += 1

        @block.tensor
        def _(tensor):
            n = 0
            for s2 in range(NSUP):
                par = s2 % 2
                tensor.wait_ge(s_wcv, 2 * (s2 + 1))
                for m in range(n_per_sup):
                    ic_loc = m // len(groups)
                    g = m % len(groups)
                    t0, L = groups[g]
                    p2 = n % 3
                    if n >= 3:
                        tensor.wait_ge(s_sil, n - 2)
                        tensor.wait_ge(s_hm, n - 2)
                    for c in range(NCH):
                        lhs = gwbf[:, par * 4096 + c * 256 + ic_loc * 128: par * 4096 + c * 256 + ic_loc * 128 + 128]
                        rhs = xn_sb[:, c * t_total + t0: c * t_total + t0 + L]
                        tensor.matmul(pG[:, p2 * 512:p2 * 512 + L], lhs, rhs,
                                      start=(c == 0), stop=(c == NCH - 1))
                    for c in range(NCH):
                        lhs = uwbf[:, par * 4096 + c * 256 + ic_loc * 128: par * 4096 + c * 256 + ic_loc * 128 + 128]
                        rhs = xn_sb[:, c * t_total + t0: c * t_total + t0 + L]
                        mm = tensor.matmul(pU[:, p2 * 512:p2 * 512 + L], lhs, rhs,
                                           start=(c == 0), stop=(c == NCH - 1))
                    mm.then_inc(s_pmm, 1)
                    n += 1

        @block.scalar
        def _(scalar):
            n = 0
            for s2 in range(NSUP):
                for m in range(n_per_sup):
                    ic_loc = m // len(groups)
                    g = m % len(groups)
                    L = groups[g][1]
                    ic = s2 * 2 + ic_loc
                    p2 = n % 3
                    scalar.wait_ge(s_pmm, n + 1)
                    if n >= 3:
                        scalar.wait_ge(s_hm, n - 2)
                    scalar.activation(silu_sb[:, p2 * 512:p2 * 512 + L], pG[:, p2 * 512:p2 * 512 + L],
                                      AF.Silu, bias=0.0, scale=gs_sb[:, ic:ic + 1]).then_inc(s_sil, 1)
                    n += 1

        @block.gpsimd
        def _(g_eng):
            def conv(s2):
                par = s2 % 2
                g_eng.wait_ge(s_wld[par], 64 * (s2 // 2 + 1))
                if s2 >= 2:
                    g_eng.wait_ge(s_pmm, n_per_sup * (s2 - 1))
                g_eng.tensor_copy(gwbf[:, par * 4096:(par + 1) * 4096], gwst[:, par * 4096:(par + 1) * 4096]).then_inc(s_wcv, 1)
                g_eng.tensor_copy(uwbf[:, par * 4096:(par + 1) * 4096], uwst[:, par * 4096:(par + 1) * 4096]).then_inc(s_wcv, 1)
            conv(0)
            conv(1)
            n = 0
            for s2 in range(NSUP):
                for m in range(n_per_sup):
                    ic_loc = m // len(groups)
                    g = m % len(groups)
                    t0, L = groups[g]
                    ic = s2 * 2 + ic_loc
                    p2 = n % 3
                    g_eng.wait_ge(s_hm, n + 1)
                    g_eng.dma_start(h_dram.ap()[ic * 128:(ic + 1) * 128, col_off + t0: col_off + t0 + L],
                                    h_sb[:, p2 * 512:p2 * 512 + L]).then_inc(s_hst[n % 3], 16)
                    if m == n_per_sup - 1 and s2 + 2 < NSUP:
                        conv(s2 + 2)
                    n += 1
            for sl in range(3):
                g_eng.wait_ge(s_hst[sl], 16 * len([x for x in range(NIT) if x % 3 == sl]))
        block_cm.__exit__(None, None, None)


def _down_phase(nc, dwT, h_dram, x_view3, xres_out, groups, consts, tag,
                h_col_off=0, out_col_off=0):
    ds2_sb = consts["ds2_sb"]
    ng = len(groups)
    NQ = 4
    with (
        nc.sbuf_tensor(f"{tag}_dwst", [128, 2 * 2048], F32) as dwst,
        nc.sbuf_tensor(f"{tag}_dwbf", [128, 2 * 44 * 512], BF16) as dwbf,
        nc.sbuf_tensor(f"{tag}_hch", [128, 8 * 512], BF16) as hch,
        nc.sbuf_tensor(f"{tag}_xst", [128, 2 * 4 * 512], F32) as xst,
        nc.sbuf_tensor(f"{tag}_x1o", [128, 2 * 512], F32) as x1o,
        nc.psum_tensor(f"{tag}_pD", [128, 4 * 512], F32) as pD,
        nc.semaphore(f"{tag}_dwld0") as s_dwld0,
        nc.semaphore(f"{tag}_dwld1") as s_dwld1,
        nc.semaphore(f"{tag}_dwcv") as s_dwcv,
        nc.semaphore(f"{tag}_xlda") as s_xlda,
        nc.semaphore(f"{tag}_xldb") as s_xldb,
        nc.semaphore(f"{tag}_hld0") as s_hld0,
        nc.semaphore(f"{tag}_hld1") as s_hld1,
        nc.semaphore(f"{tag}_pic") as s_pic,
        nc.semaphore(f"{tag}_ev") as s_ev,
        nc.semaphore(f"{tag}_x1st0") as s_x1st0,
        nc.semaphore(f"{tag}_x1st1") as s_x1st1,
    ):
        _clear_sems(nc, [s_dwld0, s_dwld1, s_dwcv, s_xlda, s_xldb, s_hld0, s_hld1,
                         s_pic, s_ev, s_x1st0, s_x1st1])
        block_cm = nc.Block()
        block = block_cm.__enter__()
        dw3 = dwT.ap().rearrange("(c p) d -> p c d", p=128)
        s_hld = [s_hld0, s_hld1]
        s_dwld = [s_dwld0, s_dwld1]
        s_xld = [s_xlda, s_xldb]
        s_x1st = [s_x1st0, s_x1st1]

        def _dwload(sync, Qt, j):
            jj = Qt * 11 + j
            if jj >= 2:
                sync.wait_ge(s_dwcv, jj - 1)
            sync.dma_start(
                out=dwst.ap().rearrange("p (b c d) -> p b c d", b=2, c=4)[:, jj % 2],
                in_=dw3[:, j * 4:(j + 1) * 4, Qt * 512:(Qt + 1) * 512],
            ).then_inc(s_dwld[jj % 2], 16)

        @block.scalar
        def _(scalar):
            for Qt in range(NQ):
                for j in range(11):
                    _dwload(scalar, Qt, j)

        @block.sync
        def _(sync):
            for Q in range(NQ):
                for grp in range(ng):
                    gi = Q * ng + grp
                    t0, L = groups[grp]
                    if gi >= 2:
                        sync.wait_ge(s_ev, 4 * (gi - 1))
                    xv = xst.ap().rearrange("p (b c t) -> p b c t", b=2, c=4)[:, gi % 2]
                    sync.dma_start(out=xv[:, :, 0:L],
                                   in_=x_view3[:, Q * 4:(Q + 1) * 4, t0:t0 + L]).then_inc(s_xld[gi % 2], 16)
                    h3 = h_dram.ap().rearrange("(c p) t -> p c t", p=128)
                    hv = hch.ap().rearrange("p (s c t) -> p s c t", s=2, c=4)
                    for qd in range(11):
                        gq = gi * 11 + qd
                        if gq >= 2:
                            sync.wait_ge(s_pic, 4 * (gq - 2) + 4)
                        sync.dma_start(out=hv[:, gq % 2, :, 0:L],
                                       in_=h3[:, qd * 4:(qd + 1) * 4,
                                              h_col_off + t0:h_col_off + t0 + L]).then_inc(s_hld[gq % 2], 16)

        @block.vector
        def _(vector):
            for Q in range(NQ):
                for grp in range(ng):
                    gi = Q * ng + grp
                    t0, L = groups[grp]
                    vector.wait_ge(s_pic, ICN * (gi + 1))
                    vector.wait_ge(s_xld[gi % 2], 16 * (gi // 2 + 1))
                    for dc in range(4):
                        idx = 4 * gi + dc
                        if idx >= 2:
                            vector.wait_ge(s_x1st[idx % 2], 16 * ((idx - 2) // 2 + 1))
                        dcg = Q * 4 + dc
                        xv = xst.ap().rearrange("p (b c t) -> p b c t", b=2, c=4)[:, gi % 2, dc, 0:L]
                        vector.scalar_tensor_tensor(
                            x1o[:, (idx % 2) * 512:(idx % 2) * 512 + L],
                            pD[:, dc * 512:dc * 512 + L],
                            ds2_sb[:, dcg:dcg + 1], xv, ALU.mult, ALU.add).then_inc(s_ev, 1)

        @block.tensor
        def _(tensor):
            for Q in range(NQ):
                par = Q % 2
                for grp in range(ng):
                    gi = Q * ng + grp
                    t0, L = groups[grp]
                    for ic in range(ICN):
                        n_ic = gi * ICN + ic
                        if ic % 4 == 0:
                            gq = gi * 11 + ic // 4
                            tensor.wait_ge(s_hld[gq % 2], 16 * (gq // 2 + 1))
                        if ic == 0:
                            tensor.wait_ge(s_dwcv, 11 * (Q + 1))
                        sl = ((gi * 11 + ic // 4) % 2) * 4 + (ic % 4)
                        for dc in range(4):
                            if ic == 0 and gi >= 1:
                                tensor.wait_ge(s_ev, 4 * (gi - 1) + dc + 1)
                            lhs = dwbf[:, par * 22528 + ic * 512 + dc * 128: par * 22528 + ic * 512 + dc * 128 + 128]
                            mm = tensor.matmul(pD[:, dc * 512:dc * 512 + L],
                                               lhs, hch[:, sl * 512:sl * 512 + L],
                                               start=(ic == 0), stop=(ic == ICN - 1))
                        mm.then_inc(s_pic, 1)

        @block.gpsimd
        def _(g_eng):
            def conv(Qt, j):
                jj = Qt * 11 + j
                g_eng.wait_ge(s_dwld[jj % 2], 16 * (jj // 2 + 1))
                if Qt >= 2 and j == 0:
                    g_eng.wait_ge(s_pic, ICN * ng * (Qt - 1))
                g_eng.tensor_copy(
                    dwbf.ap().rearrange("p (b c d) -> p b c d", b=2, c=44, d=512)[:, Qt % 2, j * 4:(j + 1) * 4, :],
                    dwst.ap().rearrange("p (b c d) -> p b c d", b=2, c=4)[:, jj % 2]).then_inc(s_dwcv, 1)

            for j in range(11):
                conv(0, j)
            for j in range(11):
                conv(1, j)
            for Q in range(NQ):
                for grp in range(ng):
                    gi = Q * ng + grp
                    t0, L = groups[grp]
                    for dc in range(4):
                        idx = 4 * gi + dc
                        dcg = Q * 4 + dc
                        g_eng.wait_ge(s_ev, idx + 1)
                        g_eng.dma_start(
                            xres_out.ap()[dcg * 128:(dcg + 1) * 128,
                                          out_col_off + t0:out_col_off + t0 + L],
                            x1o[:, (idx % 2) * 512:(idx % 2) * 512 + L]).then_inc(s_x1st[idx % 2], 16)
                if Q + 2 < NQ:
                    for j in range(11):
                        conv(Q + 2, j)
            g_eng.wait_ge(s_x1st[0], 16 * 2 * NQ * ng)
            g_eng.wait_ge(s_x1st[1], 16 * 2 * NQ * ng)
        block_cm.__exit__(None, None, None)


def _kv_phase(nc, wkT, wvT, xn2_dram, k_dram, ve_dram, vo_dram, consts, tag):
    wk3 = wkT.ap().rearrange("(c p) f -> p c f", p=128)
    wv3 = wvT.ap().rearrange("(c p) f -> p c f", p=128)
    xn3v = xn2_dram.ap().rearrange("(c p) t -> p c t", p=128)
    ng = len(GROUPS)
    vtiles = []
    for g, (t0, L) in enumerate(GROUPS):
        for i in range(L // 128):
            vtiles.append((g, t0, i))
    NVT = len(vtiles)
    with (
        nc.sbuf_tensor(f"{tag}_wk", [128, 16 * 2048], BF16) as wk_sb,
        nc.sbuf_tensor(f"{tag}_wv", [128, 16 * 2048], BF16) as wv_sb,
        nc.sbuf_tensor(f"{tag}_wwst", [128, 2 * 2048], F32) as wwst,
        nc.sbuf_tensor(f"{tag}_xng", [128, 2 * 16 * 512], BF16) as xng,
        nc.sbuf_tensor(f"{tag}_ko", [128, 2 * 512], BF16) as k_out,
        nc.sbuf_tensor(f"{tag}_ve", [128, 2 * 2048], BF16) as ve_out,
        nc.sbuf_tensor(f"{tag}_vo", [128, 2 * 2048], BF16) as vo_out,
        nc.psum_tensor(f"{tag}_pKV", [128, 4096], F32) as pKV,
        nc.semaphore(f"{tag}_xnld0") as s_xnld0,
        nc.semaphore(f"{tag}_xnld1") as s_xnld1,
        nc.semaphore(f"{tag}_kmm") as s_kmm,
        nc.semaphore(f"{tag}_kev") as s_kev,
        nc.semaphore(f"{tag}_kst0") as s_kst0,
        nc.semaphore(f"{tag}_kst1") as s_kst1,
        nc.semaphore(f"{tag}_vmm") as s_vmm,
        nc.semaphore(f"{tag}_vev") as s_vev,
        nc.semaphore(f"{tag}_vst0") as s_vst0,
        nc.semaphore(f"{tag}_vst1") as s_vst1,
        nc.semaphore(f"{tag}_wwld0") as s_wwld0,
        nc.semaphore(f"{tag}_wwld1") as s_wwld1,
        nc.semaphore(f"{tag}_wwcv") as s_wwcv,
    ):
        _clear_sems(nc, [s_xnld0, s_xnld1, s_kmm, s_kev, s_kst0, s_kst1, s_vmm,
                         s_vev, s_vst0, s_vst1, s_wwld0, s_wwld1, s_wwcv])
        block_cm = nc.Block()
        block = block_cm.__enter__()
        s_wwld = [s_wwld0, s_wwld1]
        s_xnld = [s_xnld0, s_xnld1]
        s_kst = [s_kst0, s_kst1]
        s_vst = [s_vst0, s_vst1]

        def xn_load(sync, li, t0, L):
            v = xng.ap().rearrange("p (b c t) -> p b c t", b=2, c=16)[:, li % 2]
            src = xn3v[:, :, t0:t0 + L]
            sync.dma_start(out=v[:, 0:8, 0:L], in_=src[:, 0:8, :]).then_inc(s_xnld[li % 2], 16)
            sync.dma_start(out=v[:, 8:16, 0:L], in_=src[:, 8:16, :]).then_inc(s_xnld[li % 2], 16)

        @block.sync
        def _(sync):
            for jj in range(16):
                if jj >= 2:
                    sync.wait_ge(s_wwcv, jj - 1)
                sync.dma_start(out=wwst.ap().rearrange("p (b f) -> p b f", b=2)[:, jj % 2],
                               in_=wk3[:, jj, :]).then_inc(s_wwld[jj % 2], 16)
            wvj = list(range(16, 32))
            for g, (t0, L) in enumerate(GROUPS):
                li = g
                if li >= 2:
                    sync.wait_ge(s_kmm, 16 * (li - 1))
                xn_load(sync, li, t0, L)
                nw = [4, 3, 3, 3, 3][g]
                for _ in range(nw):
                    jj = wvj.pop(0)
                    if jj >= 2:
                        sync.wait_ge(s_wwcv, jj - 1)
                    sync.dma_start(out=wwst.ap().rearrange("p (b f) -> p b f", b=2)[:, jj % 2],
                                   in_=wv3[:, jj - 16, :]).then_inc(s_wwld[jj % 2], 16)
            tiles_cum = [4, 8, 12, 16, 17]
            for g2, (t0, L) in enumerate(GROUPS):
                li = ng + g2
                if g2 < 2:
                    sync.wait_ge(s_kmm, 16 * (li - 1))
                else:
                    sync.wait_ge(s_vmm, tiles_cum[g2 - 2])
                xn_load(sync, li, t0, L)

        @block.tensor
        def _(tensor):
            for g, (t0, L) in enumerate(GROUPS):
                tensor.wait_ge(s_xnld[g % 2], 32 * (g // 2 + 1))
                if g == 0:
                    tensor.wait_ge(s_wwcv, 16)
                for fc in range(16):
                    kidx = g * 16 + fc
                    if kidx >= 2:
                        tensor.wait_ge(s_kev, kidx - 1)
                    for dc in range(NCH):
                        mm = tensor.matmul(
                            pKV[:, (kidx % 2) * 512:(kidx % 2) * 512 + L],
                            wk_sb[:, dc * 2048 + fc * 128: dc * 2048 + fc * 128 + 128],
                            xng.ap().rearrange("p (b c t) -> p b c t", b=2, c=16)[:, g % 2, dc, 0:L],
                            start=(dc == 0), stop=(dc == NCH - 1))
                    mm.then_inc(s_kmm, 1)
            tensor.wait_ge(s_kev, 80)
            tensor.wait_ge(s_wwcv, 32)
            vidx = 0
            for g2, (t0, L) in enumerate(GROUPS):
                li = ng + g2
                tensor.wait_ge(s_xnld[li % 2], 32 * (li // 2 + 1))
                for tloc in range(L // 128):
                    vpar = vidx % 2
                    if vidx >= 2:
                        tensor.wait_ge(s_vev, vidx - 1)
                    for dc in range(NCH):
                        for fg in range(4):
                            mm = tensor.matmul(
                                pKV[:, vpar * 2048 + fg * 512: vpar * 2048 + fg * 512 + 512],
                                xng.ap().rearrange("p (b c t) -> p b c t", b=2, c=16)[:, li % 2, dc, tloc * 128:(tloc + 1) * 128],
                                wv_sb[:, dc * 2048 + fg * 512: dc * 2048 + fg * 512 + 512],
                                start=(dc == 0), stop=(dc == NCH - 1))
                    mm.then_inc(s_vmm, 1)
                    vidx += 1

        @block.vector
        def _(vector):
            for kidx in range(80):
                vector.wait_ge(s_kmm, kidx + 1)
                L = GROUPS[kidx // 16][1]
                if kidx >= 2:
                    vector.wait_ge(s_kst[kidx % 2], 16 * ((kidx - 2) // 2 + 1))
                vector.tensor_copy(k_out[:, (kidx % 2) * 512:(kidx % 2) * 512 + L],
                                   pKV[:, (kidx % 2) * 512:(kidx % 2) * 512 + L]).then_inc(s_kev, 1)
            vector.memset(ve_out[:], 0.0)
            vector.memset(vo_out[:], 0.0)
            for vidx in range(NVT):
                vector.wait_ge(s_vmm, vidx + 1)
                if vidx >= 2:
                    vector.wait_ge(s_vst[vidx % 2], 32 * ((vidx - 2) // 2 + 1))
                pv3 = pKV.ap().rearrange("p (a c f) -> p a c f", a=2, c=16)[:, vidx % 2]
                ve3 = ve_out.ap().rearrange("p (a c f) -> p a c f", a=2, c=16)[:, vidx % 2]
                vo3 = vo_out.ap().rearrange("p (a c f) -> p a c f", a=2, c=16)[:, vidx % 2]
                vector.tensor_copy(ve3[:, :, 0:64], pv3[:, :, 0:64])
                vector.tensor_copy(vo3[:, :, 64:128], pv3[:, :, 64:128]).then_inc(s_vev, 1)

        @block.gpsimd
        def _(g_eng):
            for jj in range(16):
                g_eng.wait_ge(s_wwld[jj % 2], 16 * (jj // 2 + 1))
                g_eng.tensor_copy(wk_sb[:, jj * 2048:(jj + 1) * 2048],
                                  wwst.ap().rearrange("p (b f) -> p b f", b=2)[:, jj % 2]).then_inc(s_wwcv, 1)
            wvp = list(range(16, 32))
            for kidx in range(80):
                g, fc = kidx // 16, kidx % 16
                t0, L = GROUPS[g]
                g_eng.wait_ge(s_kev, kidx + 1)
                g_eng.dma_start(k_dram.ap()[fc * 128:(fc + 1) * 128, t0:t0 + L],
                                k_out[:, (kidx % 2) * 512:(kidx % 2) * 512 + L]).then_inc(s_kst[kidx % 2], 16)
                if wvp and kidx % 4 == 1:
                    jj = wvp.pop(0)
                    g_eng.wait_ge(s_wwld[jj % 2], 16 * (jj // 2 + 1))
                    g_eng.tensor_copy(wv_sb[:, (jj - 16) * 2048:(jj - 15) * 2048],
                                      wwst.ap().rearrange("p (b f) -> p b f", b=2)[:, jj % 2]).then_inc(s_wwcv, 1)
            for jj in wvp:
                g_eng.wait_ge(s_wwld[jj % 2], 16 * (jj // 2 + 1))
                g_eng.tensor_copy(wv_sb[:, (jj - 16) * 2048:(jj - 15) * 2048],
                                  wwst.ap().rearrange("p (b f) -> p b f", b=2)[:, jj % 2]).then_inc(s_wwcv, 1)
            vidx = 0
            for g2, (t0, L) in enumerate(GROUPS):
                for tloc in range(L // 128):
                    g_eng.wait_ge(s_vev, vidx + 1)
                    r0 = t0 + tloc * 128
                    g_eng.dma_start(ve_dram.ap()[r0:r0 + 128, :],
                                    ve_out[:, (vidx % 2) * 2048:(vidx % 2 + 1) * 2048]).then_inc(s_vst[vidx % 2], 16)
                    g_eng.dma_start(vo_dram.ap()[r0:r0 + 128, :],
                                    vo_out[:, (vidx % 2) * 2048:(vidx % 2 + 1) * 2048]).then_inc(s_vst[vidx % 2], 16)
                    vidx += 1
            g_eng.wait_ge(s_kst[0], 16 * 40)
            g_eng.wait_ge(s_kst[1], 16 * 40)
            g_eng.wait_ge(s_vst[0], 32 * ((NVT + 1) // 2))
            g_eng.wait_ge(s_vst[1], 32 * (NVT // 2))
        block_cm.__exit__(None, None, None)


def _attn_phase(nc, wqT, woT, xn2_dram, k_dram, ve_dram, vo_dram, x1_dram,
                x2_dram, consts, tag, dbg=None):
    maskA_sb, maskB_sb, ident_sb = consts["maskA_sb"], consts["maskB_sb"], consts["ident_sb"]
    wq3 = wqT.ap().rearrange("(c p) f -> p c f", p=128)
    wo3 = woT.ap().rearrange("(c p) f -> p c f", p=128)
    xnv = xn2_dram.ap().rearrange("(c p) t -> p c t", p=128)
    x1v = x1_dram.ap().rearrange("(c p) t -> p c t", p=128)
    x2v = x2_dram.ap().rearrange("(c p) t -> p c t", p=128)
    kv3 = k_dram.ap().rearrange("(c p) t -> p c t", p=128)
    with ExitStack() as ctx:
        ec = ctx.enter_context
        wq_sb = ec(nc.sbuf_tensor(f"{tag}_wq", [128, 16 * 2048], BF16))
        wo_sb = ec(nc.sbuf_tensor(f"{tag}_wo", [128, 16 * 2048], BF16))
        wst = ec(nc.sbuf_tensor(f"{tag}_wst", [128, 2 * 512], F32))
        kr = ec(nc.sbuf_tensor(f"{tag}_kr", [128, 2 * 2048], BF16))
        vre = ec(nc.sbuf_tensor(f"{tag}_vre", [128, 2 * 2048], BF16))
        vro = ec(nc.sbuf_tensor(f"{tag}_vro", [128, 2 * 2048], BF16))
        xn_sb = ec(nc.sbuf_tensor(f"{tag}_xn", [128, 2048], BF16))
        x1_sb = ec(nc.sbuf_tensor(f"{tag}_x1", [128, 2048], F32))
        qe_sb = ec(nc.sbuf_tensor(f"{tag}_qe", [128, 2048], BF16))
        qo_sb = ec(nc.sbuf_tensor(f"{tag}_qo", [128, 2048], BF16))
        s_sb = ec(nc.sbuf_tensor(f"{tag}_ssb", [128, 1024], F32))
        probs_sb = ec(nc.sbuf_tensor(f"{tag}_prb", [128, 2 * 1024], BF16))
        pt_sb = ec(nc.sbuf_tensor(f"{tag}_pt", [128, 4 * 128], BF16))
        ao_sb = ec(nc.sbuf_tensor(f"{tag}_ao", [128, 2048], BF16))
        x2_sb = ec(nc.sbuf_tensor(f"{tag}_x2", [128, 2048], F32))
        nm_sb = ec(nc.sbuf_tensor(f"{tag}_nm", [128, 8], F32))
        sum_sb = ec(nc.sbuf_tensor(f"{tag}_sum", [128, 8], F32))
        rec_sb = ec(nc.sbuf_tensor(f"{tag}_rec", [128, 8], F32))
        pQ = ec(nc.psum_tensor(f"{tag}_pQ", [128, 2048], F32))
        pS = ec(nc.psum_tensor(f"{tag}_pS", [128, 1024], F32))
        pT = ec(nc.psum_tensor(f"{tag}_pT", [128, 512], BF16))
        pV = ec(nc.psum_tensor(f"{tag}_pV", [128, 512], F32))
        names = ["wld0", "wld1", "wqcv", "kvl0", "kvl1", "xnl0", "xnl1",
                 "x1l0", "x1l1", "qmm", "qev", "smm", "sev", "max", "exp",
                 "nrm4", "tmm", "ptev", "pvp", "aoev", "omm", "x2ev", "x2st0",
                 "x2st1", "dbgst", "rcp"]
        sems = {n: ec(nc.semaphore(f"{tag}_{n}")) for n in names}
        _clear_sems(nc, list(sems.values()))
        (s_wld0, s_wld1, s_wqcv, s_kvl0, s_kvl1, s_xnl0, s_xnl1,
         s_x1l0, s_x1l1, s_qmm, s_qev, s_smm, s_sev, s_max, s_exp,
         s_nrm4, s_tmm, s_ptev, s_pvp, s_aoev, s_omm, s_x2ev, s_x2st0,
         s_x2st1, s_dbgst, s_rcp) = (sems[n] for n in names)
        block = ec(nc.Block())
        s_wld = [s_wld0, s_wld1]
        s_kvl = [s_kvl0, s_kvl1]
        s_xnl = [s_xnl0, s_xnl1]
        s_x1l = [s_x1l0, s_x1l1]
        s_x2st = [s_x2st0, s_x2st1]


        def kv_load(sync, b):
            sl = b % 2
            col = b * 128
            dst = kr.ap().rearrange("p (s c t) -> p s c t", s=2, c=16)[:, sl]
            src = kv3[:, :, col:col + 128]
            for q in range(4):
                sync.dma_start(dst[:, q * 4:(q + 1) * 4, :], src[:, q * 4:(q + 1) * 4, :]).then_inc(s_kvl[sl], 16)
            sync.dma_start(vre[:, sl * 2048:(sl + 1) * 2048], ve_dram.ap()[col:col + 128, :]).then_inc(s_kvl[sl], 16)
            sync.dma_start(vro[:, sl * 2048:(sl + 1) * 2048], vo_dram.ap()[col:col + 128, :]).then_inc(s_kvl[sl], 16)

        def xn_load(sync, b):
            col = b * 128
            dst = xn_sb.ap().rearrange("p (c t) -> p c t", c=16)
            for q in range(4):
                sync.dma_start(dst[:, q * 4:(q + 1) * 4, :], xnv[:, q * 4:(q + 1) * 4, col:col + 128]).then_inc(s_xnl0, 16)

        def x1_load(sync, b):
            col = b * 128
            dst1 = x1_sb.ap().rearrange("p (c t) -> p c t", c=16)
            for q in range(4):
                sync.dma_start(dst1[:, q * 4:(q + 1) * 4, :], x1v[:, q * 4:(q + 1) * 4, col:col + 128]).then_inc(s_x1l0, 16)

        @block.sync
        def _(sync):
            for w, (w3,) in enumerate(((wq3,), (wo3,))):
                for c in range(16):
                    for hf in range(4):
                        jj = w * 64 + c * 4 + hf
                        if jj >= 2:
                            sync.wait_ge(s_wqcv, jj - 1)
                        sync.dma_start(out=wst.ap().rearrange("p (b f) -> p b f", b=2)[:, jj % 2],
                                       in_=w3[:, c, hf * 512:(hf + 1) * 512]).then_inc(s_wld[jj % 2], 16)
            kv_load(sync, 0)
            kv_load(sync, 1)
            xn_load(sync, 1)
            x1_load(sync, 1)
            for b in range(1, NBLK + 1):
                bi = b - 1
                if b + 1 <= NBLK:
                    sync.wait_ge(s_pvp, 16 * b)
                    kv_load(sync, b + 1)
                    sync.wait_ge(s_qmm, b)
                    xn_load(sync, b + 1)
                    sync.wait_ge(s_x2ev, b)
                    x1_load(sync, b + 1)

        @block.tensor
        def _(tensor):
            for b in range(1, NBLK + 1):
                bi = b - 1
                p = b % 2
                tensor.wait_ge(s_xnl0, 64 * b)
                if b == 1:
                    tensor.wait_ge(s_wqcv, 64)
                if bi >= 1:
                    tensor.wait_ge(s_x2ev, bi)
                for fc in range(16):
                    for dc in range(16):
                        mm = tensor.matmul(
                            pQ[:, fc * 128:(fc + 1) * 128],
                            wq_sb[:, dc * 2048 + fc * 128: dc * 2048 + fc * 128 + 128],
                            xn_sb[:, dc * 128:(dc + 1) * 128],
                            start=(dc == 0), stop=(dc == 15))
                mm.then_inc(s_qmm, 1)
                slp, slc = (b - 1) % 2, b % 2
                tensor.wait_ge(s_kvl[slp], 96 * ((b - 1) // 2 + 1))
                tensor.wait_ge(s_kvl[slc], 96 * (b // 2 + 1))
                if dbg is not None and bi >= 1:
                    tensor.wait_ge(s_dbgst, 144 * bi)
                tensor.wait_ge(s_qev, bi + 1)
                for hb in range(8):
                    B = 8 * bi + hb
                    tensor.wait_ge(s_sev, B)
                    for j in range(4):
                        h = hb * 4 + j
                        qz = qe_sb if h % 2 == 0 else qo_sb
                        qs = qz[:, (h // 2) * 128:(h // 2) * 128 + 128]
                        for half, sl in ((0, slp), (1, slc)):
                            ks = kr[:, sl * 2048 + (h // 2) * 128: sl * 2048 + (h // 2) * 128 + 128]
                            mm = tensor.matmul(pS[:, j * 256 + half * 128: j * 256 + half * 128 + 128],
                                               qs, ks, start=True, stop=True)
                    mm.then_inc(s_smm, 1)
                    tensor.wait_ge(s_nrm4, B + 1)
                    for j in range(4):
                        h = hb * 4 + j
                        hh = 32 * bi + h
                        if hh >= 2:
                            tensor.wait_ge(s_ptev, 2 * (hh - 2) + 2)
                        for half in (0, 1):
                            sc = ((h % 2) * 2 + half) * 128
                            mm = tensor.transpose(
                                pT[:, sc:sc + 128],
                                probs_sb[:, (hb % 2) * 1024 + j * 256 + half * 128: (hb % 2) * 1024 + j * 256 + half * 128 + 128],
                                ident_sb[:, :])
                        mm.then_inc(s_tmm, 1)
                        hp = hh // 2
                        tensor.wait_ge(s_ptev, 2 * hh + 2)
                        if h % 2 == 0 and hp >= 4:
                            tensor.wait_ge(s_aoev, hp - 3)
                        vz = vre if h % 2 == 0 else vro
                        for half, sl in ((0, slp), (1, slc)):
                            sc = ((h % 2) * 2 + half) * 128
                            mm = tensor.matmul(
                                pV[:, (hp % 4) * 128:(hp % 4) * 128 + 128],
                                vz[:, sl * 2048 + (h // 2) * 128: sl * 2048 + (h // 2) * 128 + 128],
                                pt_sb[:, sc:sc + 128],
                                start=(h % 2 == 0 and half == 0),
                                stop=(h % 2 == 1 and half == 1))
                        if h % 2 == 1:
                            mm.then_inc(s_pvp, 1)
                tensor.wait_ge(s_aoev, 16 * bi + 16)
                if b == 1:
                    tensor.wait_ge(s_wqcv, 64)
                for dc in range(16):
                    for fc in range(16):
                        mm = tensor.matmul(
                            pQ[:, dc * 128:(dc + 1) * 128],
                            wo_sb[:, fc * 2048 + dc * 128: fc * 2048 + dc * 128 + 128],
                            ao_sb[:, fc * 128:(fc + 1) * 128],
                            start=(fc == 0), stop=(fc == 15))
                mm.then_inc(s_omm, 1)

        @block.vector
        def _(vector):
            for w in range(2):
                for c in range(16):
                    for hf in range(4):
                        jj = w * 64 + c * 4 + hf
                        vector.wait_ge(s_wld[jj % 2], 16 * (jj // 2 + 1))
                        dst = wq_sb if w == 0 else wo_sb
                        vector.tensor_copy(dst[:, c * 2048 + hf * 512: c * 2048 + (hf + 1) * 512],
                                           wst.ap().rearrange("p (b f) -> p b f", b=2)[:, jj % 2]).then_inc(s_wqcv, 1)
            for b in range(1, NBLK + 1):
                bi = b - 1
                p = b % 2
                if b == 1:
                    vector.memset(qe_sb[64:128, :], 0.0)
                    vector.memset(qo_sb[0:64, :], 0.0)
                vector.wait_ge(s_qmm, bi + 1)
                if dbg is not None and bi >= 1:
                    vector.wait_ge(s_dbgst, 144 * (bi - 1) + 64)
                vector.tensor_copy(qe_sb[0:64, :], pQ[0:64, :])
                vector.tensor_copy(qo_sb[64:128, :], pQ[64:128, :]).then_inc(s_qev, 1)
                for hb in range(8):
                    B = 8 * bi + hb
                    vector.wait_ge(s_smm, B + 1)
                    if B >= 1:
                        vector.wait_ge(s_exp, 4 * B)
                    msk = maskA_sb if b == 1 else maskB_sb
                    vector.scalar_tensor_tensor(
                        s_sb.ap().rearrange("p (j c) -> p j c", c=256),
                        pS.ap().rearrange("p (j c) -> p j c", c=256),
                        0.125,
                        msk.ap().unsqueeze(1).broadcast_to([128, 4, 256]),
                        ALU.mult, ALU.add).then_inc(s_sev, 1)
                    vector.tensor_reduce(nm_sb[:, (hb % 2) * 4:(hb % 2) * 4 + 4],
                                         s_sb.ap().rearrange("p (j c) -> p j c", c=256),
                                         AX.X, ALU.max, negate=True).then_inc(s_max, 1)
                    vector.wait_ge(s_exp, 4 * B + 4)
                    vector.tensor_reduce(sum_sb[:, (hb % 2) * 4:(hb % 2) * 4 + 4],
                                         probs_sb.ap().rearrange("p (a j c) -> p a j c", a=2, c=256)[:, hb % 2],
                                         AX.X, ALU.add).then_inc(s_rcp, 1)
                    vector.wait_ge(s_rcp, 2 * B + 1)
                    vector.reciprocal(rec_sb[:, (hb % 2) * 4:(hb % 2) * 4 + 4],
                                      sum_sb[:, (hb % 2) * 4:(hb % 2) * 4 + 4]).then_inc(s_rcp, 1)
                    vector.wait_ge(s_rcp, 2 * B + 2)
                    for j in range(4):
                        op = vector.tensor_scalar_mul(
                            probs_sb[:, (hb % 2) * 1024 + j * 256:(hb % 2) * 1024 + j * 256 + 256],
                            probs_sb[:, (hb % 2) * 1024 + j * 256:(hb % 2) * 1024 + j * 256 + 256],
                            rec_sb[:, (hb % 2) * 4 + j:(hb % 2) * 4 + j + 1])
                    op.then_inc(s_nrm4, 1)
                vector.wait_ge(s_omm, bi + 1)
                vector.wait_ge(s_x1l0, 64 * (bi + 1))
                if bi >= 1:
                    vector.wait_ge(s_x2st0, 64 * bi)
                vector.tensor_add(x2_sb[:], pQ[:], x1_sb[:]).then_inc(s_x2ev, 1)

        @block.scalar
        def _(scalar):
            for b in range(1, NBLK + 1):
                bi = b - 1
                for hb in range(8):
                    B = 8 * bi + hb
                    scalar.wait_ge(s_max, B + 1)
                    if B >= 2:
                        scalar.wait_ge(s_tmm, 4 * (B - 2) + 4)
                    if dbg is not None and hb == 2:
                        scalar.wait_ge(s_dbgst, 144 * bi + 80)
                    for j in range(4):
                        scalar.activation(
                            probs_sb[:, (hb % 2) * 1024 + j * 256:(hb % 2) * 1024 + j * 256 + 256],
                            s_sb[:, j * 256:(j + 1) * 256],
                            AF.Exp, bias=nm_sb[:, (hb % 2) * 4 + j:(hb % 2) * 4 + j + 1],
                            scale=1.0).then_inc(s_exp, 1)
                    for j in range(4):
                        h = hb * 4 + j
                        hh = 32 * bi + h
                        scalar.wait_ge(s_tmm, hh + 1)
                        if hh >= 2:
                            scalar.wait_ge(s_pvp, (hh - 2) // 2 + 1)
                        for half in (0, 1):
                            sc = ((h % 2) * 2 + half) * 128
                            scalar.copy(pt_sb[:, sc:sc + 128], pT[:, sc:sc + 128]).then_inc(s_ptev, 1)
                        if h % 2 == 1:
                            hp = hh // 2
                            scalar.wait_ge(s_pvp, 16 * bi + (h - 1) // 2 + 1)
                            pl = (h - 1) // 2
                            scalar.copy(ao_sb[:, pl * 128:(pl + 1) * 128],
                                        pV[:, (hp % 4) * 128:(hp % 4) * 128 + 128]).then_inc(s_aoev, 1)

        @block.gpsimd
        def _(g_eng):
            for b in range(1, NBLK + 1):
                bi = b - 1
                if dbg is not None:
                    qv = dbg["q_dump"].ap().rearrange("(c p) t -> p c t", p=128)
                    av = dbg["ao_dump"].ap().rearrange("(c p) t -> p c t", p=128)
                    g_eng.wait_ge(s_qev, bi + 1)
                    for qq in range(2):
                        g_eng.dma_start(qv[0:64, qq * 8:(qq + 1) * 8, bi * 128:(bi + 1) * 128],
                                        qe_sb.ap().rearrange("p (c t) -> p c t", c=16)[0:64, qq * 8:(qq + 1) * 8, :]).then_inc(s_dbgst, 16)
                    for qq in range(2):
                        g_eng.dma_start(qv[64:128, qq * 8:(qq + 1) * 8, bi * 128:(bi + 1) * 128],
                                        qo_sb.ap().rearrange("p (c t) -> p c t", c=16)[64:128, qq * 8:(qq + 1) * 8, :]).then_inc(s_dbgst, 16)
                    g_eng.wait_ge(s_nrm4, 8 * bi + 1)
                    g_eng.dma_start(dbg["pr_dump"].ap()[:, bi * 1024:(bi + 1) * 1024],
                                    probs_sb[:, 0:1024]).then_inc(s_dbgst, 16)
                    g_eng.wait_ge(s_aoev, 16 * bi + 16)
                    asv = ao_sb.ap().rearrange("p (c t) -> p c t", c=16)
                    for qq in range(4):
                        g_eng.dma_start(av[:, qq * 4:(qq + 1) * 4, bi * 128:(bi + 1) * 128],
                                        asv[:, qq * 4:(qq + 1) * 4, :]).then_inc(s_dbgst, 16)
                g_eng.wait_ge(s_x2ev, bi + 1)
                dst = x2v[:, :, bi * 128:(bi + 1) * 128]
                src = x2_sb.ap().rearrange("p (c t) -> p c t", c=16)
                for q in range(4):
                    g_eng.dma_start(dst[:, q * 4:(q + 1) * 4, :], src[:, q * 4:(q + 1) * 4, :]).then_inc(s_x2st0, 16)
            g_eng.wait_ge(s_x2st0, 64 * NBLK)
            if dbg is not None:
                g_eng.wait_ge(s_dbgst, 144 * NBLK)


def _build(taps=(), upto=99, race_detect=True):
    nc = bass.Bass("TRN2", target_bir_lowering=False, detect_race_conditions=race_detect)
    xT = nc.dram_tensor("xT", [D, T_ALL], F32, kind="ExternalInput")
    gwT = nc.dram_tensor("gwT", [D, IF], F32, kind="ExternalInput")
    uwT = nc.dram_tensor("uwT", [D, IF], F32, kind="ExternalInput")
    dwT = nc.dram_tensor("dwT", [IF, D], F32, kind="ExternalInput")
    gs = nc.dram_tensor("gs", [128, 44], F32, kind="ExternalInput")
    us = nc.dram_tensor("us", [128, 44], F32, kind="ExternalInput")
    ds = nc.dram_tensor("ds", [128, 16], F32, kind="ExternalInput")
    wqT = nc.dram_tensor("wqT", [D, D], F32, kind="ExternalInput")
    wkT = nc.dram_tensor("wkT", [D, D], F32, kind="ExternalInput")
    wvT = nc.dram_tensor("wvT", [D, D], F32, kind="ExternalInput")
    woT = nc.dram_tensor("woT", [D, D], F32, kind="ExternalInput")
    maskA = nc.dram_tensor("maskA", [128, 256], F32, kind="ExternalInput")
    maskB = nc.dram_tensor("maskB", [128, 256], F32, kind="ExternalInput")
    ident = nc.dram_tensor("ident", [128, 128], BF16, kind="ExternalInput")

    def scratch(name, shape, dt):
        kind = "ExternalOutput" if name in taps else "Internal"
        return nc.dram_tensor(name, shape, dt, kind=kind)

    h_dram = scratch("h_dram", [IF, T_ALL], BF16)
    x1_dram = scratch("x1_dram", [D, T_ALL], F32)
    k_dram = scratch("k_dram", [D, T_ALL], BF16)
    ve_dram = scratch("ve_dram", [T_ALL, D], BF16)
    vo_dram = scratch("vo_dram", [T_ALL, D], BF16)
    xn2_dram = scratch("xn2_dram", [D, T_ALL], BF16)
    x2_dram = scratch("x2_dram", [D, T_OWN], F32)
    outT = nc.dram_tensor("outT", [D, T_OWN], F32, kind="ExternalOutput")

    x_view3 = xT.ap().rearrange("(c p) t -> p c t", p=128)
    x1_view3 = x1_dram.ap().rearrange("(c p) t -> p c t", p=128)
    x2_view3 = x2_dram.ap().rearrange("(c p) t -> p c t", p=128)

    with (
        nc.sbuf_tensor("gs_sb", [128, 44], F32) as gs_sb,
        nc.sbuf_tensor("us_sb", [128, 44], F32) as us_sb,
        nc.sbuf_tensor("ds2_sb", [128, 16], F32) as ds2_sb,
        nc.sbuf_tensor("maskA_sb", [128, 256], F32) as maskA_sb,
        nc.sbuf_tensor("maskB_sb", [128, 256], F32) as maskB_sb,
        nc.sbuf_tensor("ident_sb", [128, 128], BF16) as ident_sb,
        nc.sbuf_tensor("ones128", [128, 128], BF16) as ones128,
        nc.sbuf_tensor("eps_col", [128, 1], F32) as eps_col,
    ):
        consts = dict(gs_sb=gs_sb, us_sb=us_sb, ds2_sb=ds2_sb, maskA_sb=maskA_sb,
                      maskB_sb=maskB_sb, ident_sb=ident_sb, ones128=ones128,
                      eps_col=eps_col)
        with (
            nc.semaphore("p0ld") as p0ld,
            nc.Block() as block,
        ):
            @block.sync
            def _(sync):
                sync.dma_start(out=gs_sb[:], in_=gs.ap()).then_inc(p0ld, 16)
                sync.dma_start(out=us_sb[:], in_=us.ap()).then_inc(p0ld, 16)
                sync.dma_start(out=ds2_sb[:], in_=ds.ap()).then_inc(p0ld, 16)
                sync.dma_start(out=maskA_sb[:], in_=maskA.ap()).then_inc(p0ld, 16)
                sync.dma_start(out=maskB_sb[:], in_=maskB.ap()).then_inc(p0ld, 16)
                sync.dma_start(out=ident_sb[:], in_=ident.ap()).then_inc(p0ld, 16)

            @block.vector
            def _(vector):
                vector.wait_ge(p0ld, 96)
                vector.tensor_scalar_mul(ds2_sb[:], ds2_sb[:], 0.5)
                vector.memset(ones128[:], 1.0)
                vector.memset(eps_col[:], EPS)

        if upto < 1:
            return nc
        with nc.sbuf_tensor("xn1", [128, NCH * T_ALL], BF16) as xn1_sb:
            with (
                nc.sbuf_tensor("p1_xst", [128, 2 * 4096], F32) as p1_xst,
                nc.sbuf_tensor("p1_tmp", [128, 4096], F32) as p1_tmp,
                nc.sbuf_tensor("p1_xc", [128, 4096], BF16) as p1_xc,
                nc.sbuf_tensor("p1_xsq", [128, 4096], BF16) as p1_xsq,
                nc.sbuf_tensor("p1_rows", [128, 1024], F32) as p1_rows,
                nc.psum_tensor("p1_pstat", [128, 512], F32) as p1_pstat,
            ):
                bt = dict(xst=[p1_xst.ap()[:, 0:4096], p1_xst.ap()[:, 4096:8192]],
                          tmp=p1_tmp, xc=p1_xc, xsq=p1_xsq, rows=p1_rows,
                          pstat=p1_pstat)
                _ln_phase(nc, bt, x_view3, xn1_sb, T_ALL, LN_STEPS, consts, "ln1")
            if "xn1_dump" in taps:
                xn1_dump = nc.dram_tensor("xn1_dump", [128, NCH * T_ALL], BF16, kind="ExternalOutput")
                with nc.semaphore("dmp") as sdmp:
                    _clear_sems(nc, [sdmp])
                with nc.semaphore("dmp2") as sdmp, nc.Block() as block:
                    @block.gpsimd
                    def _(g_eng):
                        for c in range(NCH):
                            g_eng.dma_start(xn1_dump.ap()[:, c * T_ALL:(c + 1) * T_ALL],
                                            xn1_sb[:, c * T_ALL:(c + 1) * T_ALL]).then_inc(sdmp, 16)
                        g_eng.wait_ge(sdmp, 16 * NCH)
            if upto < 2:
                return nc
            _gateup_phase(nc, xn1_sb, T_ALL, GROUPS, gwT, uwT, h_dram, consts, "g1")

        if upto < 3:
            return nc
        _down_phase(nc, dwT, h_dram, x_view3, x1_dram, GROUPS, consts, "d1")

        if upto < 4:
            return nc
        with (
            nc.sbuf_tensor("p3_xst", [128, 2 * 4096], F32) as p3_xst,
            nc.sbuf_tensor("p3_tmp", [128, 4096], F32) as p3_tmp,
            nc.sbuf_tensor("p3_xc", [128, 4096], BF16) as p3_xc,
            nc.sbuf_tensor("p3_xsq", [128, 4096], BF16) as p3_xsq,
            nc.sbuf_tensor("p3_rows", [128, 1024], F32) as p3_rows,
            nc.sbuf_tensor("p3_xout", [128, 2 * 4096], BF16) as p3_xout,
            nc.psum_tensor("p3_pstat", [128, 512], F32) as p3_pstat,
        ):
            bt = dict(xst=[p3_xst.ap()[:, 0:4096], p3_xst.ap()[:, 4096:8192]],
                      tmp=p3_tmp, xc=p3_xc, xsq=p3_xsq, rows=p3_rows,
                      pstat=p3_pstat)
            _ln_phase(nc, bt, x1_view3, p3_xout, T_ALL, LN_STEPS, consts, "ln2",
                      dram_out=xn2_dram)
        if upto < 5:
            return nc
        _kv_phase(nc, wkT, wvT, xn2_dram, k_dram, ve_dram, vo_dram, consts, "kv")

        if upto < 6:
            return nc
        dbg = None
        if "q_dump" in taps:
            dbg = {"q_dump": nc.dram_tensor("q_dump", [D, T_OWN], BF16, kind="ExternalOutput"),
                   "ao_dump": nc.dram_tensor("ao_dump", [D, T_OWN], BF16, kind="ExternalOutput"),
                   "pr_dump": nc.dram_tensor("pr_dump", [128, NBLK * 1024], BF16, kind="ExternalOutput")}
        _attn_phase(nc, wqT, woT, xn2_dram, k_dram, ve_dram, vo_dram, x1_dram,
                    x2_dram, consts, "at", dbg=dbg)

        if upto < 7:
            return nc
        with nc.sbuf_tensor("xn3", [128, NCH * T_OWN], BF16) as xn3_sb:
            with (
                nc.sbuf_tensor("p5_xst", [128, 2 * 4096], F32) as p5_xst,
                nc.sbuf_tensor("p5_tmp", [128, 4096], F32) as p5_tmp,
                nc.sbuf_tensor("p5_xc", [128, 4096], BF16) as p5_xc,
                nc.sbuf_tensor("p5_xsq", [128, 4096], BF16) as p5_xsq,
                nc.sbuf_tensor("p5_rows", [128, 1024], F32) as p5_rows,
                nc.psum_tensor("p5_pstat", [128, 512], F32) as p5_pstat,
            ):
                bt = dict(xst=[p5_xst.ap()[:, 0:4096], p5_xst.ap()[:, 4096:8192]],
                          tmp=p5_tmp, xc=p5_xc, xsq=p5_xsq, rows=p5_rows,
                          pstat=p5_pstat)
                _ln_phase(nc, bt, x2_view3, xn3_sb, T_OWN, LN_STEPS_OWN, consts, "ln3")
            _gateup_phase(nc, xn3_sb, T_OWN, GROUPS_OWN, gwT, uwT, h_dram, consts,
                          "g2", col_off=HALO)

        if upto < 8:
            return nc
        _down_phase(nc, dwT, h_dram, x2_view3, outT, GROUPS_OWN, consts, "d2",
                    h_col_off=HALO, out_col_off=0)

    return nc



def _prep_maps(L0_attn_x, L0_attn_Wq, L0_attn_Wk, L0_attn_Wv, L0_attn_Wo,
               L0_ffn_gate_w, L0_ffn_up_w, L0_ffn_down_w,
               L0_ffn_gate_s, L0_ffn_up_s, L0_ffn_down_s):
    x = np.asarray(L0_attn_x, np.float32)
    common = {
        "gwT": np.ascontiguousarray(np.asarray(L0_ffn_gate_w, np.float32).T),
        "uwT": np.ascontiguousarray(np.asarray(L0_ffn_up_w, np.float32).T),
        "dwT": np.ascontiguousarray(np.asarray(L0_ffn_down_w, np.float32).T),
        "gs": np.ascontiguousarray(np.asarray(L0_ffn_gate_s, np.float32).reshape(44, 128).T),
        "us": np.ascontiguousarray(np.asarray(L0_ffn_up_s, np.float32).reshape(44, 128).T),
        "ds": np.ascontiguousarray(np.asarray(L0_ffn_down_s, np.float32).reshape(16, 128).T),
        "wqT": np.ascontiguousarray(np.asarray(L0_attn_Wq, np.float32).T),
        "wkT": np.ascontiguousarray(np.asarray(L0_attn_Wk, np.float32).T),
        "wvT": np.ascontiguousarray(np.asarray(L0_attn_Wv, np.float32).T),
        "woT": np.ascontiguousarray(np.asarray(L0_attn_Wo, np.float32).T),
        "ident": np.eye(128, dtype=ml_dtypes.bfloat16),
    }
    r = np.arange(128)[:, None]
    c = np.arange(256)[None, :]
    validB = (c >= r) & (c <= r + 128)
    maskB = np.where(validB, np.float32(0.0), np.float32(NEG))
    maskA0 = np.where(validB & (c >= 128), np.float32(0.0), np.float32(NEG))
    in_maps = []
    for core in range(8):
        b, hf = core // 2, core % 2
        t0 = hf * T_OWN
        own = x[b, t0:t0 + T_OWN]
        halo = x[b, t0 - HALO:t0] if hf == 1 else np.zeros((HALO, D), np.float32)
        xT_c = np.ascontiguousarray(np.concatenate([halo, own], 0).T)
        m = dict(common)
        m["xT"] = xT_c
        m["maskA"] = maskB if hf == 1 else maskA0
        m["maskB"] = maskB
        in_maps.append(m)
    return in_maps


def kernel(**inputs):
    in_maps = _prep_maps(**inputs)
    nc = _build()
    res = run_bass_kernel_spmd(nc, in_maps, core_ids=list(range(8)))
    out = np.empty((4, 4096, D), np.float32)
    for core in range(8):
        b, hf = core // 2, core % 2
        out[b, hf * T_OWN:(hf + 1) * T_OWN] = res.results[core]["outT"].T
    return out


# version 05
# speedup vs baseline: 1.1350x; 1.0663x over previous
from contextlib import ExitStack
import numpy as np
import ml_dtypes
import concourse.bass as bass
import concourse.mybir as mybir
from concourse.bass_utils import run_bass_kernel_spmd

F32 = mybir.dt.float32
BF16 = mybir.dt.bfloat16
AF = mybir.ActivationFunctionType
ALU = mybir.AluOpType
AX = mybir.AxisListType

D = 2048
NCH = 16
IF = 5632
ICN = 44
T_OWN = 2048
HALO = 128
T_ALL = 2176
NBLK = 16
EPS = 1e-5
NEG = -1.0e30

GROUPS = [(0, 512), (512, 512), (1024, 512), (1536, 512), (2048, 128)]
LN_STEPS = [(i * 256, 256) for i in range(8)] + [(2048, 128)]
GROUPS_OWN = [(0, 512), (512, 512), (1024, 512), (1536, 512)]
LN_STEPS_OWN = [(i * 256, 256) for i in range(8)]



def _clear_sems(nc, sems):
    with nc.Block() as blk:
        @blk.gpsimd
        def _(g):
            for sem in sems:
                g.sem_clear(sem)


def _ln_phase(nc, block_tensors, xin_view3, xn_out_sb, t_total, steps, consts, tag,
              dram_out=None, sync_hook=None, pool_hook=None):
    ones128 = consts["ones128"]
    t = block_tensors
    ns = len(steps)
    with (
        nc.semaphore(f"{tag}_xld0") as s_xld0,
        nc.semaphore(f"{tag}_xld1") as s_xld1,
        nc.semaphore(f"{tag}_v2p") as s_v2p,
        nc.semaphore(f"{tag}_pst") as s_pst,
        nc.semaphore(f"{tag}_var") as s_var,
        nc.semaphore(f"{tag}_sd") as s_sd,
        nc.semaphore(f"{tag}_nrm") as s_nrm,
        nc.semaphore(f"{tag}_ost0") as s_ost0,
        nc.semaphore(f"{tag}_ost1") as s_ost1,
    ):
        _clear_sems(nc, [s_xld0, s_xld1, s_v2p, s_pst, s_var, s_sd, s_nrm, s_ost0, s_ost1])
        block = None
        block_cm = nc.Block()
        block = block_cm.__enter__()
        s_xld = [s_xld0, s_xld1]
        s_ost = [s_ost0, s_ost1]

        @block.sync
        def _(sync):
            for s, (t0, L) in enumerate(steps):
                if s >= 2:
                    sync.wait_ge(s_nrm, s - 1)
                xst = t["xst"][s % 2]
                v = xst.rearrange("p (c t) -> p c t", t=256)
                sync.dma_start(out=v[:, 0:8, 0:L], in_=xin_view3[:, 0:8, t0:t0 + L]).then_inc(s_xld[s % 2], 16)
                sync.dma_start(out=v[:, 8:16, 0:L], in_=xin_view3[:, 8:16, t0:t0 + L]).then_inc(s_xld[s % 2], 16)
                if sync_hook is not None:
                    sync_hook(sync, s)

        @block.vector
        def _(vector):
            for s, (t0, L) in enumerate(steps):
                xst = t["xst"][s % 2]
                xv = xst.rearrange("p (c t) -> p c t", t=256)[:, :, 0:L]
                xcv = t["xc"].ap().rearrange("p (c t) -> p c t", t=256)[:, :, 0:L]
                xqv = t["xsq"].ap().rearrange("p (c t) -> p c t", t=256)[:, :, 0:L]
                vector.wait_ge(s_xld[s % 2], 32 * (s // 2 + 1))
                vector.tensor_copy(xcv, xv)
                vector.tensor_tensor(xqv, xcv, xcv, ALU.mult).then_inc(s_v2p, 1)
                vector.wait_ge(s_pst, s + 1)
                rows = t["rows"]
                m = rows[:, 0:256]
                t1 = rows[:, 256:512]
                sd = rows[:, 512:768]
                r = rows[:, 768:1024]
                vector.tensor_scalar_mul(m[:, 0:L], t["pstat"][:, 0:L], 1.0 / D)
                vector.tensor_scalar_mul(t1[:, 0:L], t["pstat"][:, 256:256 + L], 1.0 / D).then_inc(s_var, 1)
                vector.tensor_tensor(sd[:, 0:L], m[:, 0:L], m[:, 0:L], ALU.mult)
                vector.tensor_sub(t1[:, 0:L], t1[:, 0:L], sd[:, 0:L]).then_inc(s_var, 1)
                vector.wait_ge(s_sd, s + 1)
                vector.reciprocal(r[:, 0:L], sd[:, 0:L])
                tmpv = t["tmp"].ap().rearrange("p (c t) -> p c t", t=256)[:, :, 0:L]
                mb = m[:, 0:L].unsqueeze(1).broadcast_to([128, 16, L])
                rb = r[:, 0:L].unsqueeze(1).broadcast_to([128, 16, L])
                vector.tensor_sub(tmpv, xv, mb)
                if dram_out is None:
                    xout = xn_out_sb.ap().rearrange("p (c t) -> p c t", t=t_total)[:, :, t0:t0 + L]
                else:
                    if s >= 2:
                        vector.wait_ge(s_ost[s % 2], 64 * ((s - 2) // 2 + 1))
                    xout = xn_out_sb.ap().rearrange("p (b c t) -> p b c t", b=2, c=NCH)[:, s % 2, :, 0:L]
                vector.tensor_tensor(xout, tmpv, rb, ALU.mult).then_inc(s_nrm, 1)

        @block.tensor
        def _(tensor):
            for s, (t0, L) in enumerate(steps):
                tensor.wait_ge(s_v2p, s + 1)
                if s >= 1:
                    tensor.wait_ge(s_var, 2 * s - 1)
                xcv = t["xc"].ap().rearrange("p (c t) -> p c t", t=256)
                xqv = t["xsq"].ap().rearrange("p (c t) -> p c t", t=256)
                for c in range(NCH):
                    tensor.matmul(t["pstat"][:, 0:L], ones128[:, :], xcv[:, c, 0:L],
                                  start=(c == 0), stop=(c == NCH - 1))
                for c in range(NCH):
                    mm = tensor.matmul(t["pstat"][:, 256:256 + L], ones128[:, :], xqv[:, c, 0:L],
                                       start=(c == 0), stop=(c == NCH - 1))
                mm.then_inc(s_pst, 1)

        @block.scalar
        def _(scalar):
            for s, (t0, L) in enumerate(steps):
                scalar.wait_ge(s_var, 2 * s + 2)
                rows = t["rows"]
                scalar.activation(rows[:, 512:512 + L], rows[:, 256:256 + L], AF.Sqrt,
                                  bias=consts["eps_col"][:, 0:1], scale=1.0).then_inc(s_sd, 1)

        if dram_out is not None or pool_hook is not None:
            @block.gpsimd
            def _(g_eng):
                o3 = None if dram_out is None else dram_out.ap().rearrange("(c p) t -> p c t", p=128)
                for s, (t0, L) in enumerate(steps):
                    if pool_hook is not None:
                        pool_hook(g_eng, s)
                    if dram_out is not None:
                        g_eng.wait_ge(s_nrm, s + 1)
                        v = xn_out_sb.ap().rearrange("p (b c t) -> p b c t", b=2, c=NCH)[:, s % 2]
                        for q in range(4):
                            g_eng.dma_start(o3[:, q * 4:(q + 1) * 4, t0:t0 + L],
                                            v[:, q * 4:(q + 1) * 4, 0:L]).then_inc(s_ost[s % 2], 16)
                if dram_out is not None:
                    g_eng.wait_ge(s_ost[0], 64 * ((ns + 1) // 2))
                    g_eng.wait_ge(s_ost[1], 64 * (ns // 2))
        block_cm.__exit__(None, None, None)


def _gateup_phase(nc, xn_sb, t_total, groups, gwT, uwT, h_dram, consts, tag,
                  col_off=0):
    gs_sb, us_sb = consts["gs_sb"], consts["us_sb"]
    NSUP = 22
    n_per_sup = 2 * len(groups)
    NIT = NSUP * n_per_sup
    with (
        nc.sbuf_tensor(f"{tag}_gwst", [128, 2 * 4096], F32) as gwst,
        nc.sbuf_tensor(f"{tag}_uwst", [128, 2 * 4096], F32) as uwst,
        nc.sbuf_tensor(f"{tag}_gwbf", [128, 2 * 4096], BF16) as gwbf,
        nc.sbuf_tensor(f"{tag}_uwbf", [128, 2 * 4096], BF16) as uwbf,
        nc.sbuf_tensor(f"{tag}_silu", [128, 3 * 512], F32) as silu_sb,
        nc.sbuf_tensor(f"{tag}_hsb", [128, 3 * 512], BF16) as h_sb,
        nc.psum_tensor(f"{tag}_pG", [128, 3 * 512], F32) as pG,
        nc.psum_tensor(f"{tag}_pU", [128, 3 * 512], F32) as pU,
        nc.semaphore(f"{tag}_wld0") as s_wld0,
        nc.semaphore(f"{tag}_wld1") as s_wld1,
        nc.semaphore(f"{tag}_wcv") as s_wcv,
        nc.semaphore(f"{tag}_pmm") as s_pmm,
        nc.semaphore(f"{tag}_sil") as s_sil,
        nc.semaphore(f"{tag}_hm") as s_hm,
        nc.semaphore(f"{tag}_hst0") as s_hst0,
        nc.semaphore(f"{tag}_hst1") as s_hst1,
        nc.semaphore(f"{tag}_hst2") as s_hst2,
    ):
        _clear_sems(nc, [s_wld0, s_wld1, s_wcv, s_pmm, s_sil, s_hm, s_hst0, s_hst1, s_hst2])
        block_cm = nc.Block()
        block = block_cm.__enter__()
        gw3 = gwT.ap().rearrange("(c p) i -> p c i", p=128)
        uw3 = uwT.ap().rearrange("(c p) i -> p c i", p=128)
        s_wld = [s_wld0, s_wld1]
        s_hst = [s_hst0, s_hst1, s_hst2]

        @block.sync
        def _(sync):
            for s2 in range(NSUP):
                if s2 >= 2:
                    sync.wait_ge(s_pmm, n_per_sup * (s2 - 1))
                    sync.wait_ge(s_wcv, 2 * (s2 - 1))
                i0 = s2 * 256
                par = s2 % 2
                for (st, w3) in ((gwst, gw3), (uwst, uw3)):
                    v = st.ap().rearrange("p (b c i) -> p b c i", b=2, c=16)[:, par]
                    sync.dma_start(out=v[:, 0:8, :], in_=w3[:, 0:8, i0:i0 + 256]).then_inc(s_wld[par], 16)
                    sync.dma_start(out=v[:, 8:16, :], in_=w3[:, 8:16, i0:i0 + 256]).then_inc(s_wld[par], 16)

        @block.vector
        def _(vector):
            n = 0
            for s2 in range(NSUP):
                for m in range(n_per_sup):
                    ic_loc = m // len(groups)
                    g = m % len(groups)
                    L = groups[g][1]
                    ic = s2 * 2 + ic_loc
                    p2 = n % 3
                    vector.wait_ge(s_sil, n + 1)
                    if n >= 3:
                        vector.wait_ge(s_hst[n % 3], 16 * ((n - 3) // 3 + 1))
                    vector.scalar_tensor_tensor(
                        h_sb[:, p2 * 512:p2 * 512 + L], pU[:, p2 * 512:p2 * 512 + L],
                        us_sb[:, ic:ic + 1], silu_sb[:, p2 * 512:p2 * 512 + L],
                        ALU.mult, ALU.mult).then_inc(s_hm, 1)
                    n += 1

        @block.tensor
        def _(tensor):
            n = 0
            for s2 in range(NSUP):
                par = s2 % 2
                tensor.wait_ge(s_wcv, 2 * (s2 + 1))
                for m in range(n_per_sup):
                    ic_loc = m // len(groups)
                    g = m % len(groups)
                    t0, L = groups[g]
                    p2 = n % 3
                    if n >= 3:
                        tensor.wait_ge(s_sil, n - 2)
                        tensor.wait_ge(s_hm, n - 2)
                    for c in range(NCH):
                        lhs = gwbf[:, par * 4096 + c * 256 + ic_loc * 128: par * 4096 + c * 256 + ic_loc * 128 + 128]
                        rhs = xn_sb[:, c * t_total + t0: c * t_total + t0 + L]
                        tensor.matmul(pG[:, p2 * 512:p2 * 512 + L], lhs, rhs,
                                      start=(c == 0), stop=(c == NCH - 1))
                    for c in range(NCH):
                        lhs = uwbf[:, par * 4096 + c * 256 + ic_loc * 128: par * 4096 + c * 256 + ic_loc * 128 + 128]
                        rhs = xn_sb[:, c * t_total + t0: c * t_total + t0 + L]
                        mm = tensor.matmul(pU[:, p2 * 512:p2 * 512 + L], lhs, rhs,
                                           start=(c == 0), stop=(c == NCH - 1))
                    mm.then_inc(s_pmm, 1)
                    n += 1

        @block.scalar
        def _(scalar):
            n = 0
            for s2 in range(NSUP):
                for m in range(n_per_sup):
                    ic_loc = m // len(groups)
                    g = m % len(groups)
                    L = groups[g][1]
                    ic = s2 * 2 + ic_loc
                    p2 = n % 3
                    scalar.wait_ge(s_pmm, n + 1)
                    if n >= 3:
                        scalar.wait_ge(s_hm, n - 2)
                    scalar.activation(silu_sb[:, p2 * 512:p2 * 512 + L], pG[:, p2 * 512:p2 * 512 + L],
                                      AF.Silu, bias=0.0, scale=gs_sb[:, ic:ic + 1]).then_inc(s_sil, 1)
                    n += 1

        @block.gpsimd
        def _(g_eng):
            def conv(s2):
                par = s2 % 2
                g_eng.wait_ge(s_wld[par], 64 * (s2 // 2 + 1))
                if s2 >= 2:
                    g_eng.wait_ge(s_pmm, n_per_sup * (s2 - 1))
                g_eng.tensor_copy(gwbf[:, par * 4096:(par + 1) * 4096], gwst[:, par * 4096:(par + 1) * 4096]).then_inc(s_wcv, 1)
                g_eng.tensor_copy(uwbf[:, par * 4096:(par + 1) * 4096], uwst[:, par * 4096:(par + 1) * 4096]).then_inc(s_wcv, 1)
            conv(0)
            conv(1)
            n = 0
            for s2 in range(NSUP):
                for m in range(n_per_sup):
                    ic_loc = m // len(groups)
                    g = m % len(groups)
                    t0, L = groups[g]
                    ic = s2 * 2 + ic_loc
                    p2 = n % 3
                    g_eng.wait_ge(s_hm, n + 1)
                    g_eng.dma_start(h_dram.ap()[ic * 128:(ic + 1) * 128, col_off + t0: col_off + t0 + L],
                                    h_sb[:, p2 * 512:p2 * 512 + L]).then_inc(s_hst[n % 3], 16)
                    if m == n_per_sup - 1 and s2 + 2 < NSUP:
                        conv(s2 + 2)
                    n += 1
            for sl in range(3):
                g_eng.wait_ge(s_hst[sl], 16 * len([x for x in range(NIT) if x % 3 == sl]))
        block_cm.__exit__(None, None, None)


def _down_phase(nc, dwT, h_dram, x_view3, xres_out, groups, consts, tag,
                h_col_off=0, out_col_off=0):
    ds2_sb = consts["ds2_sb"]
    ng = len(groups)
    NQ = 4
    with (
        nc.sbuf_tensor(f"{tag}_dwst", [128, 2 * 2048], F32) as dwst,
        nc.sbuf_tensor(f"{tag}_dwbf", [128, 2 * 44 * 512], BF16) as dwbf,
        nc.sbuf_tensor(f"{tag}_hch", [128, 8 * 512], BF16) as hch,
        nc.sbuf_tensor(f"{tag}_xst", [128, 2 * 4 * 512], F32) as xst,
        nc.sbuf_tensor(f"{tag}_x1o", [128, 2 * 512], F32) as x1o,
        nc.psum_tensor(f"{tag}_pD", [128, 4 * 512], F32) as pD,
        nc.semaphore(f"{tag}_dwld0") as s_dwld0,
        nc.semaphore(f"{tag}_dwld1") as s_dwld1,
        nc.semaphore(f"{tag}_dwcv") as s_dwcv,
        nc.semaphore(f"{tag}_xlda") as s_xlda,
        nc.semaphore(f"{tag}_xldb") as s_xldb,
        nc.semaphore(f"{tag}_hld0") as s_hld0,
        nc.semaphore(f"{tag}_hld1") as s_hld1,
        nc.semaphore(f"{tag}_pic") as s_pic,
        nc.semaphore(f"{tag}_ev") as s_ev,
        nc.semaphore(f"{tag}_x1st0") as s_x1st0,
        nc.semaphore(f"{tag}_x1st1") as s_x1st1,
    ):
        _clear_sems(nc, [s_dwld0, s_dwld1, s_dwcv, s_xlda, s_xldb, s_hld0, s_hld1,
                         s_pic, s_ev, s_x1st0, s_x1st1])
        block_cm = nc.Block()
        block = block_cm.__enter__()
        dw3 = dwT.ap().rearrange("(c p) d -> p c d", p=128)
        s_hld = [s_hld0, s_hld1]
        s_dwld = [s_dwld0, s_dwld1]
        s_xld = [s_xlda, s_xldb]
        s_x1st = [s_x1st0, s_x1st1]

        def _dwload(sync, Qt, j):
            jj = Qt * 11 + j
            if jj >= 2:
                sync.wait_ge(s_dwcv, jj - 1)
            sync.dma_start(
                out=dwst.ap().rearrange("p (b c d) -> p b c d", b=2, c=4)[:, jj % 2],
                in_=dw3[:, j * 4:(j + 1) * 4, Qt * 512:(Qt + 1) * 512],
            ).then_inc(s_dwld[jj % 2], 16)

        @block.scalar
        def _(scalar):
            for Qt in range(NQ):
                for j in range(11):
                    _dwload(scalar, Qt, j)

        @block.sync
        def _(sync):
            for Q in range(NQ):
                for grp in range(ng):
                    gi = Q * ng + grp
                    t0, L = groups[grp]
                    if gi >= 2:
                        sync.wait_ge(s_ev, 4 * (gi - 1))
                    xv = xst.ap().rearrange("p (b c t) -> p b c t", b=2, c=4)[:, gi % 2]
                    sync.dma_start(out=xv[:, :, 0:L],
                                   in_=x_view3[:, Q * 4:(Q + 1) * 4, t0:t0 + L]).then_inc(s_xld[gi % 2], 16)
                    h3 = h_dram.ap().rearrange("(c p) t -> p c t", p=128)
                    hv = hch.ap().rearrange("p (s c t) -> p s c t", s=2, c=4)
                    for qd in range(11):
                        gq = gi * 11 + qd
                        if gq >= 2:
                            sync.wait_ge(s_pic, 4 * (gq - 2) + 4)
                        sync.dma_start(out=hv[:, gq % 2, :, 0:L],
                                       in_=h3[:, qd * 4:(qd + 1) * 4,
                                              h_col_off + t0:h_col_off + t0 + L]).then_inc(s_hld[gq % 2], 16)

        @block.vector
        def _(vector):
            for Q in range(NQ):
                for grp in range(ng):
                    gi = Q * ng + grp
                    t0, L = groups[grp]
                    vector.wait_ge(s_pic, ICN * (gi + 1))
                    vector.wait_ge(s_xld[gi % 2], 16 * (gi // 2 + 1))
                    for dc in range(4):
                        idx = 4 * gi + dc
                        if idx >= 2:
                            vector.wait_ge(s_x1st[idx % 2], 16 * ((idx - 2) // 2 + 1))
                        dcg = Q * 4 + dc
                        xv = xst.ap().rearrange("p (b c t) -> p b c t", b=2, c=4)[:, gi % 2, dc, 0:L]
                        vector.scalar_tensor_tensor(
                            x1o[:, (idx % 2) * 512:(idx % 2) * 512 + L],
                            pD[:, dc * 512:dc * 512 + L],
                            ds2_sb[:, dcg:dcg + 1], xv, ALU.mult, ALU.add).then_inc(s_ev, 1)

        @block.tensor
        def _(tensor):
            for Q in range(NQ):
                par = Q % 2
                for grp in range(ng):
                    gi = Q * ng + grp
                    t0, L = groups[grp]
                    for ic in range(ICN):
                        n_ic = gi * ICN + ic
                        if ic % 4 == 0:
                            gq = gi * 11 + ic // 4
                            tensor.wait_ge(s_hld[gq % 2], 16 * (gq // 2 + 1))
                        if ic == 0:
                            tensor.wait_ge(s_dwcv, 11 * (Q + 1))
                        sl = ((gi * 11 + ic // 4) % 2) * 4 + (ic % 4)
                        for dc in range(4):
                            if ic == 0 and gi >= 1:
                                tensor.wait_ge(s_ev, 4 * (gi - 1) + dc + 1)
                            lhs = dwbf[:, par * 22528 + ic * 512 + dc * 128: par * 22528 + ic * 512 + dc * 128 + 128]
                            mm = tensor.matmul(pD[:, dc * 512:dc * 512 + L],
                                               lhs, hch[:, sl * 512:sl * 512 + L],
                                               start=(ic == 0), stop=(ic == ICN - 1))
                        mm.then_inc(s_pic, 1)

        @block.gpsimd
        def _(g_eng):
            def conv(Qt, j):
                jj = Qt * 11 + j
                g_eng.wait_ge(s_dwld[jj % 2], 16 * (jj // 2 + 1))
                if Qt >= 2 and j == 0:
                    g_eng.wait_ge(s_pic, ICN * ng * (Qt - 1))
                g_eng.tensor_copy(
                    dwbf.ap().rearrange("p (b c d) -> p b c d", b=2, c=44, d=512)[:, Qt % 2, j * 4:(j + 1) * 4, :],
                    dwst.ap().rearrange("p (b c d) -> p b c d", b=2, c=4)[:, jj % 2]).then_inc(s_dwcv, 1)

            for j in range(11):
                conv(0, j)
            for j in range(11):
                conv(1, j)
            for Q in range(NQ):
                for grp in range(ng):
                    gi = Q * ng + grp
                    t0, L = groups[grp]
                    for dc in range(4):
                        idx = 4 * gi + dc
                        dcg = Q * 4 + dc
                        g_eng.wait_ge(s_ev, idx + 1)
                        g_eng.dma_start(
                            xres_out.ap()[dcg * 128:(dcg + 1) * 128,
                                          out_col_off + t0:out_col_off + t0 + L],
                            x1o[:, (idx % 2) * 512:(idx % 2) * 512 + L]).then_inc(s_x1st[idx % 2], 16)
                if Q + 2 < NQ:
                    for j in range(11):
                        conv(Q + 2, j)
            g_eng.wait_ge(s_x1st[0], 16 * 2 * NQ * ng)
            g_eng.wait_ge(s_x1st[1], 16 * 2 * NQ * ng)
        block_cm.__exit__(None, None, None)


def _kv_phase(nc, wkT, wvT, xn2_dram, k_dram, ve_dram, vo_dram, consts, tag):
    wk3 = wkT.ap().rearrange("(c p) f -> p c f", p=128)
    wv3 = wvT.ap().rearrange("(c p) f -> p c f", p=128)
    xn3v = xn2_dram.ap().rearrange("(c p) t -> p c t", p=128)
    ng = len(GROUPS)
    vtiles = []
    for g, (t0, L) in enumerate(GROUPS):
        for i in range(L // 128):
            vtiles.append((g, t0, i))
    NVT = len(vtiles)
    with (
        nc.sbuf_tensor(f"{tag}_wk", [128, 16 * 2048], BF16) as wk_sb,
        nc.sbuf_tensor(f"{tag}_wv", [128, 16 * 2048], BF16) as wv_sb,
        nc.sbuf_tensor(f"{tag}_wwst", [128, 2 * 2048], F32) as wwst,
        nc.sbuf_tensor(f"{tag}_xng", [128, 2 * 16 * 512], BF16) as xng,
        nc.sbuf_tensor(f"{tag}_ko", [128, 2 * 512], BF16) as k_out,
        nc.sbuf_tensor(f"{tag}_ve", [128, 2 * 2048], BF16) as ve_out,
        nc.sbuf_tensor(f"{tag}_vo", [128, 2 * 2048], BF16) as vo_out,
        nc.psum_tensor(f"{tag}_pKV", [128, 4096], F32) as pKV,
        nc.semaphore(f"{tag}_xnld0") as s_xnld0,
        nc.semaphore(f"{tag}_xnld1") as s_xnld1,
        nc.semaphore(f"{tag}_kmm") as s_kmm,
        nc.semaphore(f"{tag}_kev") as s_kev,
        nc.semaphore(f"{tag}_kst0") as s_kst0,
        nc.semaphore(f"{tag}_kst1") as s_kst1,
        nc.semaphore(f"{tag}_vmm") as s_vmm,
        nc.semaphore(f"{tag}_vev") as s_vev,
        nc.semaphore(f"{tag}_vst0") as s_vst0,
        nc.semaphore(f"{tag}_vst1") as s_vst1,
        nc.semaphore(f"{tag}_wwld0") as s_wwld0,
        nc.semaphore(f"{tag}_wwld1") as s_wwld1,
        nc.semaphore(f"{tag}_wwcv") as s_wwcv,
    ):
        _clear_sems(nc, [s_xnld0, s_xnld1, s_kmm, s_kev, s_kst0, s_kst1, s_vmm,
                         s_vev, s_vst0, s_vst1, s_wwld0, s_wwld1, s_wwcv])
        block_cm = nc.Block()
        block = block_cm.__enter__()
        s_wwld = [s_wwld0, s_wwld1]
        s_xnld = [s_xnld0, s_xnld1]
        s_kst = [s_kst0, s_kst1]
        s_vst = [s_vst0, s_vst1]

        def xn_load(sync, li, t0, L):
            v = xng.ap().rearrange("p (b c t) -> p b c t", b=2, c=16)[:, li % 2]
            src = xn3v[:, :, t0:t0 + L]
            sync.dma_start(out=v[:, 0:8, 0:L], in_=src[:, 0:8, :]).then_inc(s_xnld[li % 2], 16)
            sync.dma_start(out=v[:, 8:16, 0:L], in_=src[:, 8:16, :]).then_inc(s_xnld[li % 2], 16)

        @block.sync
        def _(sync):
            for jj in range(16):
                if jj >= 2:
                    sync.wait_ge(s_wwcv, jj - 1)
                sync.dma_start(out=wwst.ap().rearrange("p (b f) -> p b f", b=2)[:, jj % 2],
                               in_=wk3[:, jj, :]).then_inc(s_wwld[jj % 2], 16)
            wvj = list(range(16, 32))
            for g, (t0, L) in enumerate(GROUPS):
                li = g
                if li >= 2:
                    sync.wait_ge(s_kmm, 16 * (li - 1))
                xn_load(sync, li, t0, L)
                nw = [4, 3, 3, 3, 3][g]
                for _ in range(nw):
                    jj = wvj.pop(0)
                    if jj >= 2:
                        sync.wait_ge(s_wwcv, jj - 1)
                    sync.dma_start(out=wwst.ap().rearrange("p (b f) -> p b f", b=2)[:, jj % 2],
                                   in_=wv3[:, jj - 16, :]).then_inc(s_wwld[jj % 2], 16)
            tiles_cum = [4, 8, 12, 16, 17]
            for g2, (t0, L) in enumerate(GROUPS):
                li = ng + g2
                if g2 < 2:
                    sync.wait_ge(s_kmm, 16 * (li - 1))
                else:
                    sync.wait_ge(s_vmm, tiles_cum[g2 - 2])
                xn_load(sync, li, t0, L)

        @block.tensor
        def _(tensor):
            for g, (t0, L) in enumerate(GROUPS):
                tensor.wait_ge(s_xnld[g % 2], 32 * (g // 2 + 1))
                if g == 0:
                    tensor.wait_ge(s_wwcv, 16)
                for fc in range(16):
                    kidx = g * 16 + fc
                    if kidx >= 2:
                        tensor.wait_ge(s_kev, kidx - 1)
                    for dc in range(NCH):
                        mm = tensor.matmul(
                            pKV[:, (kidx % 2) * 512:(kidx % 2) * 512 + L],
                            wk_sb[:, dc * 2048 + fc * 128: dc * 2048 + fc * 128 + 128],
                            xng.ap().rearrange("p (b c t) -> p b c t", b=2, c=16)[:, g % 2, dc, 0:L],
                            start=(dc == 0), stop=(dc == NCH - 1))
                    mm.then_inc(s_kmm, 1)
            tensor.wait_ge(s_kev, 80)
            tensor.wait_ge(s_wwcv, 32)
            vidx = 0
            for g2, (t0, L) in enumerate(GROUPS):
                li = ng + g2
                tensor.wait_ge(s_xnld[li % 2], 32 * (li // 2 + 1))
                for tloc in range(L // 128):
                    vpar = vidx % 2
                    if vidx >= 2:
                        tensor.wait_ge(s_vev, vidx - 1)
                    for dc in range(NCH):
                        for fg in range(4):
                            mm = tensor.matmul(
                                pKV[:, vpar * 2048 + fg * 512: vpar * 2048 + fg * 512 + 512],
                                xng.ap().rearrange("p (b c t) -> p b c t", b=2, c=16)[:, li % 2, dc, tloc * 128:(tloc + 1) * 128],
                                wv_sb[:, dc * 2048 + fg * 512: dc * 2048 + fg * 512 + 512],
                                start=(dc == 0), stop=(dc == NCH - 1))
                    mm.then_inc(s_vmm, 1)
                    vidx += 1

        @block.vector
        def _(vector):
            for kidx in range(80):
                vector.wait_ge(s_kmm, kidx + 1)
                L = GROUPS[kidx // 16][1]
                if kidx >= 2:
                    vector.wait_ge(s_kst[kidx % 2], 16 * ((kidx - 2) // 2 + 1))
                vector.tensor_copy(k_out[:, (kidx % 2) * 512:(kidx % 2) * 512 + L],
                                   pKV[:, (kidx % 2) * 512:(kidx % 2) * 512 + L]).then_inc(s_kev, 1)
            vector.memset(ve_out[:], 0.0)
            vector.memset(vo_out[:], 0.0)
            for vidx in range(NVT):
                vector.wait_ge(s_vmm, vidx + 1)
                if vidx >= 2:
                    vector.wait_ge(s_vst[vidx % 2], 32 * ((vidx - 2) // 2 + 1))
                pv3 = pKV.ap().rearrange("p (a c f) -> p a c f", a=2, c=16)[:, vidx % 2]
                ve3 = ve_out.ap().rearrange("p (a c f) -> p a c f", a=2, c=16)[:, vidx % 2]
                vo3 = vo_out.ap().rearrange("p (a c f) -> p a c f", a=2, c=16)[:, vidx % 2]
                vector.tensor_copy(ve3[:, :, 0:64], pv3[:, :, 0:64])
                vector.tensor_copy(vo3[:, :, 64:128], pv3[:, :, 64:128]).then_inc(s_vev, 1)

        @block.gpsimd
        def _(g_eng):
            for jj in range(16):
                g_eng.wait_ge(s_wwld[jj % 2], 16 * (jj // 2 + 1))
                g_eng.tensor_copy(wk_sb[:, jj * 2048:(jj + 1) * 2048],
                                  wwst.ap().rearrange("p (b f) -> p b f", b=2)[:, jj % 2]).then_inc(s_wwcv, 1)
            wvp = list(range(16, 32))
            for kidx in range(80):
                g, fc = kidx // 16, kidx % 16
                t0, L = GROUPS[g]
                g_eng.wait_ge(s_kev, kidx + 1)
                g_eng.dma_start(k_dram.ap()[fc * 128:(fc + 1) * 128, t0:t0 + L],
                                k_out[:, (kidx % 2) * 512:(kidx % 2) * 512 + L]).then_inc(s_kst[kidx % 2], 16)
                if wvp and kidx % 4 == 1:
                    jj = wvp.pop(0)
                    g_eng.wait_ge(s_wwld[jj % 2], 16 * (jj // 2 + 1))
                    g_eng.tensor_copy(wv_sb[:, (jj - 16) * 2048:(jj - 15) * 2048],
                                      wwst.ap().rearrange("p (b f) -> p b f", b=2)[:, jj % 2]).then_inc(s_wwcv, 1)
            for jj in wvp:
                g_eng.wait_ge(s_wwld[jj % 2], 16 * (jj // 2 + 1))
                g_eng.tensor_copy(wv_sb[:, (jj - 16) * 2048:(jj - 15) * 2048],
                                  wwst.ap().rearrange("p (b f) -> p b f", b=2)[:, jj % 2]).then_inc(s_wwcv, 1)
            vidx = 0
            for g2, (t0, L) in enumerate(GROUPS):
                for tloc in range(L // 128):
                    g_eng.wait_ge(s_vev, vidx + 1)
                    r0 = t0 + tloc * 128
                    g_eng.dma_start(ve_dram.ap()[r0:r0 + 128, :],
                                    ve_out[:, (vidx % 2) * 2048:(vidx % 2 + 1) * 2048]).then_inc(s_vst[vidx % 2], 16)
                    g_eng.dma_start(vo_dram.ap()[r0:r0 + 128, :],
                                    vo_out[:, (vidx % 2) * 2048:(vidx % 2 + 1) * 2048]).then_inc(s_vst[vidx % 2], 16)
                    vidx += 1
            g_eng.wait_ge(s_kst[0], 16 * 40)
            g_eng.wait_ge(s_kst[1], 16 * 40)
            g_eng.wait_ge(s_vst[0], 32 * ((NVT + 1) // 2))
            g_eng.wait_ge(s_vst[1], 32 * (NVT // 2))
        block_cm.__exit__(None, None, None)


def _attn_phase(nc, wqT, woT, xn2_dram, k_dram, ve_dram, vo_dram, x1_dram,
                x2_dram, consts, tag, dbg=None):
    maskA_sb, maskB_sb, ident_sb = consts["maskA_sb"], consts["maskB_sb"], consts["ident_sb"]
    wq3 = wqT.ap().rearrange("(c p) f -> p c f", p=128)
    wo3 = woT.ap().rearrange("(c p) f -> p c f", p=128)
    xnv = xn2_dram.ap().rearrange("(c p) t -> p c t", p=128)
    x1v = x1_dram.ap().rearrange("(c p) t -> p c t", p=128)
    x2v = x2_dram.ap().rearrange("(c p) t -> p c t", p=128)
    kv3 = k_dram.ap().rearrange("(c p) t -> p c t", p=128)
    with ExitStack() as ctx:
        ec = ctx.enter_context
        wq_sb = ec(nc.sbuf_tensor(f"{tag}_wq", [128, 16 * 2048], BF16))
        wo_sb = ec(nc.sbuf_tensor(f"{tag}_wo", [128, 16 * 2048], BF16))
        wst = ec(nc.sbuf_tensor(f"{tag}_wst", [128, 2 * 512], F32))
        kr = ec(nc.sbuf_tensor(f"{tag}_kr", [128, 2 * 2048], BF16))
        vre = ec(nc.sbuf_tensor(f"{tag}_vre", [128, 2 * 2048], BF16))
        vro = ec(nc.sbuf_tensor(f"{tag}_vro", [128, 2 * 2048], BF16))
        xn_sb = ec(nc.sbuf_tensor(f"{tag}_xn", [128, 2048], BF16))
        x1_sb = ec(nc.sbuf_tensor(f"{tag}_x1", [128, 2048], F32))
        qe_sb = ec(nc.sbuf_tensor(f"{tag}_qe", [128, 2048], BF16))
        qo_sb = ec(nc.sbuf_tensor(f"{tag}_qo", [128, 2048], BF16))
        s_sb = ec(nc.sbuf_tensor(f"{tag}_ssb", [128, 1024], F32))
        probs_sb = ec(nc.sbuf_tensor(f"{tag}_prb", [128, 2 * 1024], BF16))
        pt_sb = ec(nc.sbuf_tensor(f"{tag}_pt", [128, 4 * 128], BF16))
        ao_sb = ec(nc.sbuf_tensor(f"{tag}_ao", [128, 2048], BF16))
        x2_sb = ec(nc.sbuf_tensor(f"{tag}_x2", [128, 2048], F32))
        nm_sb = ec(nc.sbuf_tensor(f"{tag}_nm", [128, 8], F32))
        sum_sb = ec(nc.sbuf_tensor(f"{tag}_sum", [128, 8], F32))
        rec_sb = ec(nc.sbuf_tensor(f"{tag}_rec", [128, 8], F32))
        pQ = ec(nc.psum_tensor(f"{tag}_pQ", [128, 2048], F32))
        pS = ec(nc.psum_tensor(f"{tag}_pS", [128, 1024], F32))
        pT = ec(nc.psum_tensor(f"{tag}_pT", [128, 512], BF16))
        pV = ec(nc.psum_tensor(f"{tag}_pV", [128, 512], F32))
        names = ["wld0", "wld1", "wqcv", "kvl0", "kvl1", "xnl0", "xnl1",
                 "x1l0", "x1l1", "qmm", "qev", "smm", "sev", "max", "exp",
                 "nrm4", "tmm", "ptev", "pvp", "aoev", "omm", "x2ev", "x2st0",
                 "x2st1", "dbgst", "rcp"]
        sems = {n: ec(nc.semaphore(f"{tag}_{n}")) for n in names}
        _clear_sems(nc, list(sems.values()))
        (s_wld0, s_wld1, s_wqcv, s_kvl0, s_kvl1, s_xnl0, s_xnl1,
         s_x1l0, s_x1l1, s_qmm, s_qev, s_smm, s_sev, s_max, s_exp,
         s_nrm4, s_tmm, s_ptev, s_pvp, s_aoev, s_omm, s_x2ev, s_x2st0,
         s_x2st1, s_dbgst, s_rcp) = (sems[n] for n in names)
        block = ec(nc.Block())
        s_wld = [s_wld0, s_wld1]
        s_kvl = [s_kvl0, s_kvl1]
        s_xnl = [s_xnl0, s_xnl1]
        s_x1l = [s_x1l0, s_x1l1]
        s_x2st = [s_x2st0, s_x2st1]


        def kv_load(sync, b):
            sl = b % 2
            col = b * 128
            dst = kr.ap().rearrange("p (s c t) -> p s c t", s=2, c=16)[:, sl]
            src = kv3[:, :, col:col + 128]
            for q in range(4):
                sync.dma_start(dst[:, q * 4:(q + 1) * 4, :], src[:, q * 4:(q + 1) * 4, :]).then_inc(s_kvl[sl], 16)
            sync.dma_start(vre[:, sl * 2048:(sl + 1) * 2048], ve_dram.ap()[col:col + 128, :]).then_inc(s_kvl[sl], 16)
            sync.dma_start(vro[:, sl * 2048:(sl + 1) * 2048], vo_dram.ap()[col:col + 128, :]).then_inc(s_kvl[sl], 16)

        def xn_load(sync, b):
            col = b * 128
            dst = xn_sb.ap().rearrange("p (c t) -> p c t", c=16)
            for q in range(4):
                sync.dma_start(dst[:, q * 4:(q + 1) * 4, :], xnv[:, q * 4:(q + 1) * 4, col:col + 128]).then_inc(s_xnl0, 16)

        def x1_load(sync, b):
            col = b * 128
            dst1 = x1_sb.ap().rearrange("p (c t) -> p c t", c=16)
            for q in range(4):
                sync.dma_start(dst1[:, q * 4:(q + 1) * 4, :], x1v[:, q * 4:(q + 1) * 4, col:col + 128]).then_inc(s_x1l0, 16)

        @block.sync
        def _(sync):
            for w, (w3,) in enumerate(((wq3,), (wo3,))):
                for c in range(16):
                    for hf in range(4):
                        jj = w * 64 + c * 4 + hf
                        if jj >= 2:
                            sync.wait_ge(s_wqcv, jj - 1)
                        sync.dma_start(out=wst.ap().rearrange("p (b f) -> p b f", b=2)[:, jj % 2],
                                       in_=w3[:, c, hf * 512:(hf + 1) * 512]).then_inc(s_wld[jj % 2], 16)
            kv_load(sync, 0)
            kv_load(sync, 1)
            xn_load(sync, 1)
            x1_load(sync, 1)
            for b in range(1, NBLK + 1):
                bi = b - 1
                if b + 1 <= NBLK:
                    sync.wait_ge(s_pvp, 16 * b)
                    kv_load(sync, b + 1)
                    sync.wait_ge(s_qmm, b)
                    xn_load(sync, b + 1)
                    sync.wait_ge(s_x2ev, b)
                    x1_load(sync, b + 1)

        @block.tensor
        def _(tensor):
            for b in range(1, NBLK + 1):
                bi = b - 1
                p = b % 2
                tensor.wait_ge(s_xnl0, 64 * b)
                if b == 1:
                    tensor.wait_ge(s_wqcv, 64)
                if bi >= 1:
                    tensor.wait_ge(s_x2ev, bi)
                for fc in range(16):
                    for dc in range(16):
                        mm = tensor.matmul(
                            pQ[:, fc * 128:(fc + 1) * 128],
                            wq_sb[:, dc * 2048 + fc * 128: dc * 2048 + fc * 128 + 128],
                            xn_sb[:, dc * 128:(dc + 1) * 128],
                            start=(dc == 0), stop=(dc == 15))
                mm.then_inc(s_qmm, 1)
                slp, slc = (b - 1) % 2, b % 2
                tensor.wait_ge(s_kvl[slp], 96 * ((b - 1) // 2 + 1))
                tensor.wait_ge(s_kvl[slc], 96 * (b // 2 + 1))
                if dbg is not None and bi >= 1:
                    tensor.wait_ge(s_dbgst, 144 * bi)
                tensor.wait_ge(s_qev, bi + 1)
                for k in range(9):
                    if k <= 7:
                        hb = k
                        B = 8 * bi + hb
                        tensor.wait_ge(s_sev, max(0, B - 1))
                        scb = pS if B % 2 == 0 else pQ
                        for j in range(4):
                            h = hb * 4 + j
                            qz = qe_sb if h % 2 == 0 else qo_sb
                            qs = qz[:, (h // 2) * 128:(h // 2) * 128 + 128]
                            for half, sl in ((0, slp), (1, slc)):
                                ks = kr[:, sl * 2048 + (h // 2) * 128: sl * 2048 + (h // 2) * 128 + 128]
                                mm = tensor.matmul(scb[:, j * 256 + half * 128: j * 256 + half * 128 + 128],
                                                   qs, ks, start=True, stop=True)
                        mm.then_inc(s_smm, 1)
                    if k >= 1:
                        hb = k - 1
                        B = 8 * bi + hb
                        tensor.wait_ge(s_nrm4, B + 1)
                        for j in range(4):
                            h = hb * 4 + j
                            hh = 32 * bi + h
                            if hh >= 2:
                                tensor.wait_ge(s_ptev, 2 * (hh - 2) + 2)
                            for half in (0, 1):
                                sc = ((h % 2) * 2 + half) * 128
                                mm = tensor.transpose(
                                    pT[:, sc:sc + 128],
                                    probs_sb[:, (hb % 2) * 1024 + j * 256 + half * 128: (hb % 2) * 1024 + j * 256 + half * 128 + 128],
                                    ident_sb[:, :])
                            mm.then_inc(s_tmm, 1)
                            hp = hh // 2
                            tensor.wait_ge(s_ptev, 2 * hh + 2)
                            if h % 2 == 0 and hp >= 4:
                                tensor.wait_ge(s_aoev, hp - 3)
                            vz = vre if h % 2 == 0 else vro
                            for half, sl in ((0, slp), (1, slc)):
                                sc = ((h % 2) * 2 + half) * 128
                                mm = tensor.matmul(
                                    pV[:, (hp % 4) * 128:(hp % 4) * 128 + 128],
                                    vz[:, sl * 2048 + (h // 2) * 128: sl * 2048 + (h // 2) * 128 + 128],
                                    pt_sb[:, sc:sc + 128],
                                    start=(h % 2 == 0 and half == 0),
                                    stop=(h % 2 == 1 and half == 1))
                            if h % 2 == 1:
                                mm.then_inc(s_pvp, 1)
                tensor.wait_ge(s_aoev, 16 * bi + 16)
                if b == 1:
                    tensor.wait_ge(s_wqcv, 64)
                for dc in range(16):
                    for fc in range(16):
                        mm = tensor.matmul(
                            pQ[:, dc * 128:(dc + 1) * 128],
                            wo_sb[:, fc * 2048 + dc * 128: fc * 2048 + dc * 128 + 128],
                            ao_sb[:, fc * 128:(fc + 1) * 128],
                            start=(fc == 0), stop=(fc == 15))
                mm.then_inc(s_omm, 1)

        @block.vector
        def _(vector):
            for w in range(2):
                for c in range(16):
                    for hf in range(4):
                        jj = w * 64 + c * 4 + hf
                        vector.wait_ge(s_wld[jj % 2], 16 * (jj // 2 + 1))
                        dst = wq_sb if w == 0 else wo_sb
                        vector.tensor_copy(dst[:, c * 2048 + hf * 512: c * 2048 + (hf + 1) * 512],
                                           wst.ap().rearrange("p (b f) -> p b f", b=2)[:, jj % 2]).then_inc(s_wqcv, 1)
            for b in range(1, NBLK + 1):
                bi = b - 1
                p = b % 2
                if b == 1:
                    vector.memset(qe_sb[64:128, :], 0.0)
                    vector.memset(qo_sb[0:64, :], 0.0)
                vector.wait_ge(s_qmm, bi + 1)
                if dbg is not None and bi >= 1:
                    vector.wait_ge(s_dbgst, 144 * (bi - 1) + 64)
                vector.tensor_copy(qe_sb[0:64, :], pQ[0:64, :])
                vector.tensor_copy(qo_sb[64:128, :], pQ[64:128, :]).then_inc(s_qev, 1)
                for k in range(9):
                    if k >= 1:
                        hb = k - 1
                        B = 8 * bi + hb
                        vector.wait_ge(s_exp, 4 * B + 4)
                        vector.tensor_reduce(sum_sb[:, (hb % 2) * 4:(hb % 2) * 4 + 4],
                                             probs_sb.ap().rearrange("p (a j c) -> p a j c", a=2, c=256)[:, hb % 2],
                                             AX.X, ALU.add).then_inc(s_rcp, 1)
                        vector.wait_ge(s_rcp, 2 * B + 1)
                        vector.reciprocal(rec_sb[:, (hb % 2) * 4:(hb % 2) * 4 + 4],
                                          sum_sb[:, (hb % 2) * 4:(hb % 2) * 4 + 4]).then_inc(s_rcp, 1)
                        vector.wait_ge(s_rcp, 2 * B + 2)
                        for j in range(4):
                            op = vector.tensor_scalar_mul(
                                probs_sb[:, (hb % 2) * 1024 + j * 256:(hb % 2) * 1024 + j * 256 + 256],
                                probs_sb[:, (hb % 2) * 1024 + j * 256:(hb % 2) * 1024 + j * 256 + 256],
                                rec_sb[:, (hb % 2) * 4 + j:(hb % 2) * 4 + j + 1])
                        op.then_inc(s_nrm4, 1)
                    if k <= 7:
                        hb = k
                        B = 8 * bi + hb
                        vector.wait_ge(s_smm, B + 1)
                        if B >= 1:
                            vector.wait_ge(s_exp, 4 * B)
                        msk = maskA_sb if b == 1 else maskB_sb
                        scb = pS if B % 2 == 0 else pQ
                        vector.scalar_tensor_tensor(
                            s_sb.ap().rearrange("p (j c) -> p j c", c=256),
                            scb.ap()[:, 0:1024].rearrange("p (j c) -> p j c", c=256),
                            0.125,
                            msk.ap().unsqueeze(1).broadcast_to([128, 4, 256]),
                            ALU.mult, ALU.add).then_inc(s_sev, 1)
                        vector.tensor_reduce(nm_sb[:, (hb % 2) * 4:(hb % 2) * 4 + 4],
                                             s_sb.ap().rearrange("p (j c) -> p j c", c=256),
                                             AX.X, ALU.max, negate=True).then_inc(s_max, 1)
                vector.wait_ge(s_omm, bi + 1)
                vector.wait_ge(s_x1l0, 64 * (bi + 1))
                if bi >= 1:
                    vector.wait_ge(s_x2st0, 64 * bi)
                vector.tensor_add(x2_sb[:], pQ[:], x1_sb[:]).then_inc(s_x2ev, 1)

        @block.scalar
        def _(scalar):
            for b in range(1, NBLK + 1):
                bi = b - 1
                for k in range(9):
                    if k <= 7:
                        hb = k
                        B = 8 * bi + hb
                        scalar.wait_ge(s_max, B + 1)
                        if B >= 2:
                            scalar.wait_ge(s_tmm, 4 * (B - 2) + 4)
                        if dbg is not None and hb == 2:
                            scalar.wait_ge(s_dbgst, 144 * bi + 80)
                        for j in range(4):
                            scalar.activation(
                                probs_sb[:, (hb % 2) * 1024 + j * 256:(hb % 2) * 1024 + j * 256 + 256],
                                s_sb[:, j * 256:(j + 1) * 256],
                                AF.Exp, bias=nm_sb[:, (hb % 2) * 4 + j:(hb % 2) * 4 + j + 1],
                                scale=1.0).then_inc(s_exp, 1)
                    if k < 1:
                        continue
                    hb = k - 1
                    for j in range(4):
                        h = hb * 4 + j
                        hh = 32 * bi + h
                        scalar.wait_ge(s_tmm, hh + 1)
                        if hh >= 2:
                            scalar.wait_ge(s_pvp, (hh - 2) // 2 + 1)
                        for half in (0, 1):
                            sc = ((h % 2) * 2 + half) * 128
                            scalar.copy(pt_sb[:, sc:sc + 128], pT[:, sc:sc + 128]).then_inc(s_ptev, 1)
                        if h % 2 == 1:
                            hp = hh // 2
                            scalar.wait_ge(s_pvp, 16 * bi + (h - 1) // 2 + 1)
                            pl = (h - 1) // 2
                            scalar.copy(ao_sb[:, pl * 128:(pl + 1) * 128],
                                        pV[:, (hp % 4) * 128:(hp % 4) * 128 + 128]).then_inc(s_aoev, 1)

        @block.gpsimd
        def _(g_eng):
            for b in range(1, NBLK + 1):
                bi = b - 1
                if dbg is not None:
                    qv = dbg["q_dump"].ap().rearrange("(c p) t -> p c t", p=128)
                    av = dbg["ao_dump"].ap().rearrange("(c p) t -> p c t", p=128)
                    g_eng.wait_ge(s_qev, bi + 1)
                    for qq in range(2):
                        g_eng.dma_start(qv[0:64, qq * 8:(qq + 1) * 8, bi * 128:(bi + 1) * 128],
                                        qe_sb.ap().rearrange("p (c t) -> p c t", c=16)[0:64, qq * 8:(qq + 1) * 8, :]).then_inc(s_dbgst, 16)
                    for qq in range(2):
                        g_eng.dma_start(qv[64:128, qq * 8:(qq + 1) * 8, bi * 128:(bi + 1) * 128],
                                        qo_sb.ap().rearrange("p (c t) -> p c t", c=16)[64:128, qq * 8:(qq + 1) * 8, :]).then_inc(s_dbgst, 16)
                    g_eng.wait_ge(s_nrm4, 8 * bi + 1)
                    g_eng.dma_start(dbg["pr_dump"].ap()[:, bi * 1024:(bi + 1) * 1024],
                                    probs_sb[:, 0:1024]).then_inc(s_dbgst, 16)
                    g_eng.wait_ge(s_aoev, 16 * bi + 16)
                    asv = ao_sb.ap().rearrange("p (c t) -> p c t", c=16)
                    for qq in range(4):
                        g_eng.dma_start(av[:, qq * 4:(qq + 1) * 4, bi * 128:(bi + 1) * 128],
                                        asv[:, qq * 4:(qq + 1) * 4, :]).then_inc(s_dbgst, 16)
                g_eng.wait_ge(s_x2ev, bi + 1)
                dst = x2v[:, :, bi * 128:(bi + 1) * 128]
                src = x2_sb.ap().rearrange("p (c t) -> p c t", c=16)
                for q in range(4):
                    g_eng.dma_start(dst[:, q * 4:(q + 1) * 4, :], src[:, q * 4:(q + 1) * 4, :]).then_inc(s_x2st0, 16)
            g_eng.wait_ge(s_x2st0, 64 * NBLK)
            if dbg is not None:
                g_eng.wait_ge(s_dbgst, 144 * NBLK)


def _build(taps=(), upto=99, race_detect=True):
    nc = bass.Bass("TRN2", target_bir_lowering=False, detect_race_conditions=race_detect)
    xT = nc.dram_tensor("xT", [D, T_ALL], F32, kind="ExternalInput")
    gwT = nc.dram_tensor("gwT", [D, IF], F32, kind="ExternalInput")
    uwT = nc.dram_tensor("uwT", [D, IF], F32, kind="ExternalInput")
    dwT = nc.dram_tensor("dwT", [IF, D], F32, kind="ExternalInput")
    gs = nc.dram_tensor("gs", [128, 44], F32, kind="ExternalInput")
    us = nc.dram_tensor("us", [128, 44], F32, kind="ExternalInput")
    ds = nc.dram_tensor("ds", [128, 16], F32, kind="ExternalInput")
    wqT = nc.dram_tensor("wqT", [D, D], F32, kind="ExternalInput")
    wkT = nc.dram_tensor("wkT", [D, D], F32, kind="ExternalInput")
    wvT = nc.dram_tensor("wvT", [D, D], F32, kind="ExternalInput")
    woT = nc.dram_tensor("woT", [D, D], F32, kind="ExternalInput")
    maskA = nc.dram_tensor("maskA", [128, 256], F32, kind="ExternalInput")
    maskB = nc.dram_tensor("maskB", [128, 256], F32, kind="ExternalInput")
    ident = nc.dram_tensor("ident", [128, 128], BF16, kind="ExternalInput")

    def scratch(name, shape, dt):
        kind = "ExternalOutput" if name in taps else "Internal"
        return nc.dram_tensor(name, shape, dt, kind=kind)

    h_dram = scratch("h_dram", [IF, T_ALL], BF16)
    x1_dram = scratch("x1_dram", [D, T_ALL], F32)
    k_dram = scratch("k_dram", [D, T_ALL], BF16)
    ve_dram = scratch("ve_dram", [T_ALL, D], BF16)
    vo_dram = scratch("vo_dram", [T_ALL, D], BF16)
    xn2_dram = scratch("xn2_dram", [D, T_ALL], BF16)
    x2_dram = scratch("x2_dram", [D, T_OWN], F32)
    outT = nc.dram_tensor("outT", [D, T_OWN], F32, kind="ExternalOutput")

    x_view3 = xT.ap().rearrange("(c p) t -> p c t", p=128)
    x1_view3 = x1_dram.ap().rearrange("(c p) t -> p c t", p=128)
    x2_view3 = x2_dram.ap().rearrange("(c p) t -> p c t", p=128)

    with (
        nc.sbuf_tensor("gs_sb", [128, 44], F32) as gs_sb,
        nc.sbuf_tensor("us_sb", [128, 44], F32) as us_sb,
        nc.sbuf_tensor("ds2_sb", [128, 16], F32) as ds2_sb,
        nc.sbuf_tensor("maskA_sb", [128, 256], F32) as maskA_sb,
        nc.sbuf_tensor("maskB_sb", [128, 256], F32) as maskB_sb,
        nc.sbuf_tensor("ident_sb", [128, 128], BF16) as ident_sb,
        nc.sbuf_tensor("ones128", [128, 128], BF16) as ones128,
        nc.sbuf_tensor("eps_col", [128, 1], F32) as eps_col,
    ):
        consts = dict(gs_sb=gs_sb, us_sb=us_sb, ds2_sb=ds2_sb, maskA_sb=maskA_sb,
                      maskB_sb=maskB_sb, ident_sb=ident_sb, ones128=ones128,
                      eps_col=eps_col)
        with (
            nc.semaphore("p0ld") as p0ld,
            nc.Block() as block,
        ):
            @block.sync
            def _(sync):
                sync.dma_start(out=gs_sb[:], in_=gs.ap()).then_inc(p0ld, 16)
                sync.dma_start(out=us_sb[:], in_=us.ap()).then_inc(p0ld, 16)
                sync.dma_start(out=ds2_sb[:], in_=ds.ap()).then_inc(p0ld, 16)
                sync.dma_start(out=maskA_sb[:], in_=maskA.ap()).then_inc(p0ld, 16)
                sync.dma_start(out=maskB_sb[:], in_=maskB.ap()).then_inc(p0ld, 16)
                sync.dma_start(out=ident_sb[:], in_=ident.ap()).then_inc(p0ld, 16)

            @block.vector
            def _(vector):
                vector.wait_ge(p0ld, 96)
                vector.tensor_scalar_mul(ds2_sb[:], ds2_sb[:], 0.5)
                vector.memset(ones128[:], 1.0)
                vector.memset(eps_col[:], EPS)

        if upto < 1:
            return nc
        with nc.sbuf_tensor("xn1", [128, NCH * T_ALL], BF16) as xn1_sb:
            with (
                nc.sbuf_tensor("p1_xst", [128, 2 * 4096], F32) as p1_xst,
                nc.sbuf_tensor("p1_tmp", [128, 4096], F32) as p1_tmp,
                nc.sbuf_tensor("p1_xc", [128, 4096], BF16) as p1_xc,
                nc.sbuf_tensor("p1_xsq", [128, 4096], BF16) as p1_xsq,
                nc.sbuf_tensor("p1_rows", [128, 1024], F32) as p1_rows,
                nc.psum_tensor("p1_pstat", [128, 512], F32) as p1_pstat,
            ):
                bt = dict(xst=[p1_xst.ap()[:, 0:4096], p1_xst.ap()[:, 4096:8192]],
                          tmp=p1_tmp, xc=p1_xc, xsq=p1_xsq, rows=p1_rows,
                          pstat=p1_pstat)
                _ln_phase(nc, bt, x_view3, xn1_sb, T_ALL, LN_STEPS, consts, "ln1")
            if "xn1_dump" in taps:
                xn1_dump = nc.dram_tensor("xn1_dump", [128, NCH * T_ALL], BF16, kind="ExternalOutput")
                with nc.semaphore("dmp") as sdmp:
                    _clear_sems(nc, [sdmp])
                with nc.semaphore("dmp2") as sdmp, nc.Block() as block:
                    @block.gpsimd
                    def _(g_eng):
                        for c in range(NCH):
                            g_eng.dma_start(xn1_dump.ap()[:, c * T_ALL:(c + 1) * T_ALL],
                                            xn1_sb[:, c * T_ALL:(c + 1) * T_ALL]).then_inc(sdmp, 16)
                        g_eng.wait_ge(sdmp, 16 * NCH)
            if upto < 2:
                return nc
            _gateup_phase(nc, xn1_sb, T_ALL, GROUPS, gwT, uwT, h_dram, consts, "g1")

        if upto < 3:
            return nc
        _down_phase(nc, dwT, h_dram, x_view3, x1_dram, GROUPS, consts, "d1")

        if upto < 4:
            return nc
        with (
            nc.sbuf_tensor("p3_xst", [128, 2 * 4096], F32) as p3_xst,
            nc.sbuf_tensor("p3_tmp", [128, 4096], F32) as p3_tmp,
            nc.sbuf_tensor("p3_xc", [128, 4096], BF16) as p3_xc,
            nc.sbuf_tensor("p3_xsq", [128, 4096], BF16) as p3_xsq,
            nc.sbuf_tensor("p3_rows", [128, 1024], F32) as p3_rows,
            nc.sbuf_tensor("p3_xout", [128, 2 * 4096], BF16) as p3_xout,
            nc.psum_tensor("p3_pstat", [128, 512], F32) as p3_pstat,
        ):
            bt = dict(xst=[p3_xst.ap()[:, 0:4096], p3_xst.ap()[:, 4096:8192]],
                      tmp=p3_tmp, xc=p3_xc, xsq=p3_xsq, rows=p3_rows,
                      pstat=p3_pstat)
            _ln_phase(nc, bt, x1_view3, p3_xout, T_ALL, LN_STEPS, consts, "ln2",
                      dram_out=xn2_dram)
        if upto < 5:
            return nc
        _kv_phase(nc, wkT, wvT, xn2_dram, k_dram, ve_dram, vo_dram, consts, "kv")

        if upto < 6:
            return nc
        dbg = None
        if "q_dump" in taps:
            dbg = {"q_dump": nc.dram_tensor("q_dump", [D, T_OWN], BF16, kind="ExternalOutput"),
                   "ao_dump": nc.dram_tensor("ao_dump", [D, T_OWN], BF16, kind="ExternalOutput"),
                   "pr_dump": nc.dram_tensor("pr_dump", [128, NBLK * 1024], BF16, kind="ExternalOutput")}
        _attn_phase(nc, wqT, woT, xn2_dram, k_dram, ve_dram, vo_dram, x1_dram,
                    x2_dram, consts, "at", dbg=dbg)

        if upto < 7:
            return nc
        with nc.sbuf_tensor("xn3", [128, NCH * T_OWN], BF16) as xn3_sb:
            with (
                nc.sbuf_tensor("p5_xst", [128, 2 * 4096], F32) as p5_xst,
                nc.sbuf_tensor("p5_tmp", [128, 4096], F32) as p5_tmp,
                nc.sbuf_tensor("p5_xc", [128, 4096], BF16) as p5_xc,
                nc.sbuf_tensor("p5_xsq", [128, 4096], BF16) as p5_xsq,
                nc.sbuf_tensor("p5_rows", [128, 1024], F32) as p5_rows,
                nc.psum_tensor("p5_pstat", [128, 512], F32) as p5_pstat,
            ):
                bt = dict(xst=[p5_xst.ap()[:, 0:4096], p5_xst.ap()[:, 4096:8192]],
                          tmp=p5_tmp, xc=p5_xc, xsq=p5_xsq, rows=p5_rows,
                          pstat=p5_pstat)
                _ln_phase(nc, bt, x2_view3, xn3_sb, T_OWN, LN_STEPS_OWN, consts, "ln3")
            _gateup_phase(nc, xn3_sb, T_OWN, GROUPS_OWN, gwT, uwT, h_dram, consts,
                          "g2", col_off=HALO)

        if upto < 8:
            return nc
        _down_phase(nc, dwT, h_dram, x2_view3, outT, GROUPS_OWN, consts, "d2",
                    h_col_off=HALO, out_col_off=0)

    return nc



def _prep_maps(L0_attn_x, L0_attn_Wq, L0_attn_Wk, L0_attn_Wv, L0_attn_Wo,
               L0_ffn_gate_w, L0_ffn_up_w, L0_ffn_down_w,
               L0_ffn_gate_s, L0_ffn_up_s, L0_ffn_down_s):
    x = np.asarray(L0_attn_x, np.float32)
    common = {
        "gwT": np.ascontiguousarray(np.asarray(L0_ffn_gate_w, np.float32).T),
        "uwT": np.ascontiguousarray(np.asarray(L0_ffn_up_w, np.float32).T),
        "dwT": np.ascontiguousarray(np.asarray(L0_ffn_down_w, np.float32).T),
        "gs": np.ascontiguousarray(np.asarray(L0_ffn_gate_s, np.float32).reshape(44, 128).T),
        "us": np.ascontiguousarray(np.asarray(L0_ffn_up_s, np.float32).reshape(44, 128).T),
        "ds": np.ascontiguousarray(np.asarray(L0_ffn_down_s, np.float32).reshape(16, 128).T),
        "wqT": np.ascontiguousarray(np.asarray(L0_attn_Wq, np.float32).T),
        "wkT": np.ascontiguousarray(np.asarray(L0_attn_Wk, np.float32).T),
        "wvT": np.ascontiguousarray(np.asarray(L0_attn_Wv, np.float32).T),
        "woT": np.ascontiguousarray(np.asarray(L0_attn_Wo, np.float32).T),
        "ident": np.eye(128, dtype=ml_dtypes.bfloat16),
    }
    r = np.arange(128)[:, None]
    c = np.arange(256)[None, :]
    validB = (c >= r) & (c <= r + 128)
    maskB = np.where(validB, np.float32(0.0), np.float32(NEG))
    maskA0 = np.where(validB & (c >= 128), np.float32(0.0), np.float32(NEG))
    in_maps = []
    for core in range(8):
        b, hf = core // 2, core % 2
        t0 = hf * T_OWN
        own = x[b, t0:t0 + T_OWN]
        halo = x[b, t0 - HALO:t0] if hf == 1 else np.zeros((HALO, D), np.float32)
        xT_c = np.ascontiguousarray(np.concatenate([halo, own], 0).T)
        m = dict(common)
        m["xT"] = xT_c
        m["maskA"] = maskB if hf == 1 else maskA0
        m["maskB"] = maskB
        in_maps.append(m)
    return in_maps


def kernel(**inputs):
    in_maps = _prep_maps(**inputs)
    nc = _build()
    res = run_bass_kernel_spmd(nc, in_maps, core_ids=list(range(8)))
    out = np.empty((4, 4096, D), np.float32)
    for core in range(8):
        b, hf = core // 2, core % 2
        out[b, hf * T_OWN:(hf + 1) * T_OWN] = res.results[core]["outT"].T
    return out


# version 06
# speedup vs baseline: 1.1500x; 1.0132x over previous
from contextlib import ExitStack
import numpy as np
import ml_dtypes
import concourse.bass as bass
import concourse.mybir as mybir
from concourse.bass_utils import run_bass_kernel_spmd

F32 = mybir.dt.float32
BF16 = mybir.dt.bfloat16
AF = mybir.ActivationFunctionType
ALU = mybir.AluOpType
AX = mybir.AxisListType

D = 2048
NCH = 16
IF = 5632
ICN = 44
T_OWN = 2048
HALO = 128
T_ALL = 2176
NBLK = 16
EPS = 1e-5
NEG = -1.0e30

GROUPS = [(0, 512), (512, 512), (1024, 512), (1536, 512), (2048, 128)]
LN_STEPS = [(i * 256, 256) for i in range(8)] + [(2048, 128)]
GROUPS_OWN = [(0, 512), (512, 512), (1024, 512), (1536, 512)]
LN_STEPS_OWN = [(i * 256, 256) for i in range(8)]



def _clear_sems(nc, sems):
    with nc.Block() as blk:
        @blk.gpsimd
        def _(g):
            for sem in sems:
                g.sem_clear(sem)


def _ln_phase(nc, block_tensors, xin_view3, xn_out_sb, t_total, steps, consts, tag,
              dram_out=None, sync_hook=None, pool_hook=None):
    ones128 = consts["ones128"]
    t = block_tensors
    ns = len(steps)
    with (
        nc.semaphore(f"{tag}_xld0") as s_xld0,
        nc.semaphore(f"{tag}_xld1") as s_xld1,
        nc.semaphore(f"{tag}_v2p") as s_v2p,
        nc.semaphore(f"{tag}_pst") as s_pst,
        nc.semaphore(f"{tag}_var") as s_var,
        nc.semaphore(f"{tag}_sd") as s_sd,
        nc.semaphore(f"{tag}_nrm") as s_nrm,
        nc.semaphore(f"{tag}_ost0") as s_ost0,
        nc.semaphore(f"{tag}_ost1") as s_ost1,
    ):
        _clear_sems(nc, [s_xld0, s_xld1, s_v2p, s_pst, s_var, s_sd, s_nrm, s_ost0, s_ost1])
        block = None
        block_cm = nc.Block()
        block = block_cm.__enter__()
        s_xld = [s_xld0, s_xld1]
        s_ost = [s_ost0, s_ost1]

        @block.sync
        def _(sync):
            for s, (t0, L) in enumerate(steps):
                if s >= 2:
                    sync.wait_ge(s_nrm, s - 1)
                xst = t["xst"][s % 2]
                v = xst.rearrange("p (c t) -> p c t", t=256)
                sync.dma_start(out=v[:, 0:8, 0:L], in_=xin_view3[:, 0:8, t0:t0 + L]).then_inc(s_xld[s % 2], 16)
                sync.dma_start(out=v[:, 8:16, 0:L], in_=xin_view3[:, 8:16, t0:t0 + L]).then_inc(s_xld[s % 2], 16)
                if sync_hook is not None:
                    sync_hook(sync, s)

        @block.vector
        def _(vector):
            for s, (t0, L) in enumerate(steps):
                xst = t["xst"][s % 2]
                xv = xst.rearrange("p (c t) -> p c t", t=256)[:, :, 0:L]
                vector.wait_ge(s_pst, s + 1)
                rows = t["rows"]
                m = rows[:, 0:256]
                t1 = rows[:, 256:512]
                sd = rows[:, 512:768]
                r = rows[:, 768:1024]
                vector.tensor_scalar_mul(m[:, 0:L], t["pstat"][:, 0:L], 1.0 / D)
                vector.tensor_scalar_mul(t1[:, 0:L], t["pstat"][:, 256:256 + L], 1.0 / D).then_inc(s_var, 1)
                vector.tensor_tensor(sd[:, 0:L], m[:, 0:L], m[:, 0:L], ALU.mult)
                vector.tensor_sub(t1[:, 0:L], t1[:, 0:L], sd[:, 0:L]).then_inc(s_var, 1)
                vector.wait_ge(s_sd, s + 1)
                vector.reciprocal(r[:, 0:L], sd[:, 0:L])
                tmpv = t["tmp"].ap().rearrange("p (c t) -> p c t", t=256)[:, :, 0:L]
                mb = m[:, 0:L].unsqueeze(1).broadcast_to([128, 16, L])
                rb = r[:, 0:L].unsqueeze(1).broadcast_to([128, 16, L])
                vector.tensor_sub(tmpv, xv, mb)
                if dram_out is None:
                    xout = xn_out_sb.ap().rearrange("p (c t) -> p c t", t=t_total)[:, :, t0:t0 + L]
                else:
                    if s >= 2:
                        vector.wait_ge(s_ost[s % 2], 64 * ((s - 2) // 2 + 1))
                    xout = xn_out_sb.ap().rearrange("p (b c t) -> p b c t", b=2, c=NCH)[:, s % 2, :, 0:L]
                vector.tensor_tensor(xout, tmpv, rb, ALU.mult).then_inc(s_nrm, 1)

        @block.tensor
        def _(tensor):
            for s, (t0, L) in enumerate(steps):
                tensor.wait_ge(s_v2p, s + 1)
                if s >= 1:
                    tensor.wait_ge(s_var, 2 * s - 1)
                xcv = t["xc"].ap().rearrange("p (c t) -> p c t", t=256)
                xqv = t["xsq"].ap().rearrange("p (c t) -> p c t", t=256)
                for c in range(NCH):
                    tensor.matmul(t["pstat"][:, 0:L], ones128[:, :], xcv[:, c, 0:L],
                                  start=(c == 0), stop=(c == NCH - 1))
                for c in range(NCH):
                    mm = tensor.matmul(t["pstat"][:, 256:256 + L], ones128[:, :], xqv[:, c, 0:L],
                                       start=(c == 0), stop=(c == NCH - 1))
                mm.then_inc(s_pst, 1)

        @block.scalar
        def _(scalar):
            for s, (t0, L) in enumerate(steps):
                xst = t["xst"][s % 2]
                xv = xst.rearrange("p (c t) -> p c t", t=256)[:, :, 0:L]
                xcv = t["xc"].ap().rearrange("p (c t) -> p c t", t=256)[:, :, 0:L]
                xqv = t["xsq"].ap().rearrange("p (c t) -> p c t", t=256)[:, :, 0:L]
                scalar.wait_ge(s_xld[s % 2], 32 * (s // 2 + 1))
                scalar.copy(xcv, xv)
                scalar.square(xqv, xv).then_inc(s_v2p, 1)
                scalar.wait_ge(s_var, 2 * s + 2)
                rows = t["rows"]
                scalar.activation(rows[:, 512:512 + L], rows[:, 256:256 + L], AF.Sqrt,
                                  bias=consts["eps_col"][:, 0:1], scale=1.0).then_inc(s_sd, 1)

        if dram_out is not None or pool_hook is not None:
            @block.gpsimd
            def _(g_eng):
                o3 = None if dram_out is None else dram_out.ap().rearrange("(c p) t -> p c t", p=128)
                for s, (t0, L) in enumerate(steps):
                    if pool_hook is not None:
                        pool_hook(g_eng, s)
                    if dram_out is not None:
                        g_eng.wait_ge(s_nrm, s + 1)
                        v = xn_out_sb.ap().rearrange("p (b c t) -> p b c t", b=2, c=NCH)[:, s % 2]
                        for q in range(4):
                            g_eng.dma_start(o3[:, q * 4:(q + 1) * 4, t0:t0 + L],
                                            v[:, q * 4:(q + 1) * 4, 0:L]).then_inc(s_ost[s % 2], 16)
                if dram_out is not None:
                    g_eng.wait_ge(s_ost[0], 64 * ((ns + 1) // 2))
                    g_eng.wait_ge(s_ost[1], 64 * (ns // 2))
        block_cm.__exit__(None, None, None)


def _gateup_phase(nc, xn_sb, t_total, groups, gwT, uwT, h_dram, consts, tag,
                  col_off=0):
    gs_sb, us_sb = consts["gs_sb"], consts["us_sb"]
    NSUP = 22
    n_per_sup = 2 * len(groups)
    NIT = NSUP * n_per_sup
    with (
        nc.sbuf_tensor(f"{tag}_gwst", [128, 2 * 4096], F32) as gwst,
        nc.sbuf_tensor(f"{tag}_uwst", [128, 2 * 4096], F32) as uwst,
        nc.sbuf_tensor(f"{tag}_gwbf", [128, 2 * 4096], BF16) as gwbf,
        nc.sbuf_tensor(f"{tag}_uwbf", [128, 2 * 4096], BF16) as uwbf,
        nc.sbuf_tensor(f"{tag}_silu", [128, 3 * 512], F32) as silu_sb,
        nc.sbuf_tensor(f"{tag}_hsb", [128, 3 * 512], BF16) as h_sb,
        nc.psum_tensor(f"{tag}_pG", [128, 3 * 512], F32) as pG,
        nc.psum_tensor(f"{tag}_pU", [128, 3 * 512], F32) as pU,
        nc.semaphore(f"{tag}_wld0") as s_wld0,
        nc.semaphore(f"{tag}_wld1") as s_wld1,
        nc.semaphore(f"{tag}_wcv") as s_wcv,
        nc.semaphore(f"{tag}_pmm") as s_pmm,
        nc.semaphore(f"{tag}_sil") as s_sil,
        nc.semaphore(f"{tag}_hm") as s_hm,
        nc.semaphore(f"{tag}_hst0") as s_hst0,
        nc.semaphore(f"{tag}_hst1") as s_hst1,
        nc.semaphore(f"{tag}_hst2") as s_hst2,
    ):
        _clear_sems(nc, [s_wld0, s_wld1, s_wcv, s_pmm, s_sil, s_hm, s_hst0, s_hst1, s_hst2])
        block_cm = nc.Block()
        block = block_cm.__enter__()
        gw3 = gwT.ap().rearrange("(c p) i -> p c i", p=128)
        uw3 = uwT.ap().rearrange("(c p) i -> p c i", p=128)
        s_wld = [s_wld0, s_wld1]
        s_hst = [s_hst0, s_hst1, s_hst2]

        @block.sync
        def _(sync):
            for s2 in range(NSUP):
                if s2 >= 2:
                    sync.wait_ge(s_pmm, n_per_sup * (s2 - 1))
                    sync.wait_ge(s_wcv, 2 * (s2 - 1))
                i0 = s2 * 256
                par = s2 % 2
                for (st, w3) in ((gwst, gw3), (uwst, uw3)):
                    v = st.ap().rearrange("p (b c i) -> p b c i", b=2, c=16)[:, par]
                    sync.dma_start(out=v[:, 0:8, :], in_=w3[:, 0:8, i0:i0 + 256]).then_inc(s_wld[par], 16)
                    sync.dma_start(out=v[:, 8:16, :], in_=w3[:, 8:16, i0:i0 + 256]).then_inc(s_wld[par], 16)

        @block.vector
        def _(vector):
            n = 0
            for s2 in range(NSUP):
                for m in range(n_per_sup):
                    ic_loc = m // len(groups)
                    g = m % len(groups)
                    L = groups[g][1]
                    ic = s2 * 2 + ic_loc
                    p2 = n % 3
                    vector.wait_ge(s_sil, n + 1)
                    if n >= 3:
                        vector.wait_ge(s_hst[n % 3], 16 * ((n - 3) // 3 + 1))
                    vector.scalar_tensor_tensor(
                        h_sb[:, p2 * 512:p2 * 512 + L], pU[:, p2 * 512:p2 * 512 + L],
                        us_sb[:, ic:ic + 1], silu_sb[:, p2 * 512:p2 * 512 + L],
                        ALU.mult, ALU.mult).then_inc(s_hm, 1)
                    n += 1

        @block.tensor
        def _(tensor):
            n = 0
            for s2 in range(NSUP):
                par = s2 % 2
                tensor.wait_ge(s_wcv, 2 * (s2 + 1))
                for m in range(n_per_sup):
                    ic_loc = m // len(groups)
                    g = m % len(groups)
                    t0, L = groups[g]
                    p2 = n % 3
                    if n >= 3:
                        tensor.wait_ge(s_sil, n - 2)
                        tensor.wait_ge(s_hm, n - 2)
                    for c in range(NCH):
                        lhs = gwbf[:, par * 4096 + c * 256 + ic_loc * 128: par * 4096 + c * 256 + ic_loc * 128 + 128]
                        rhs = xn_sb[:, c * t_total + t0: c * t_total + t0 + L]
                        tensor.matmul(pG[:, p2 * 512:p2 * 512 + L], lhs, rhs,
                                      start=(c == 0), stop=(c == NCH - 1))
                    for c in range(NCH):
                        lhs = uwbf[:, par * 4096 + c * 256 + ic_loc * 128: par * 4096 + c * 256 + ic_loc * 128 + 128]
                        rhs = xn_sb[:, c * t_total + t0: c * t_total + t0 + L]
                        mm = tensor.matmul(pU[:, p2 * 512:p2 * 512 + L], lhs, rhs,
                                           start=(c == 0), stop=(c == NCH - 1))
                    mm.then_inc(s_pmm, 1)
                    n += 1

        @block.scalar
        def _(scalar):
            n = 0
            for s2 in range(NSUP):
                for m in range(n_per_sup):
                    ic_loc = m // len(groups)
                    g = m % len(groups)
                    L = groups[g][1]
                    ic = s2 * 2 + ic_loc
                    p2 = n % 3
                    scalar.wait_ge(s_pmm, n + 1)
                    if n >= 3:
                        scalar.wait_ge(s_hm, n - 2)
                    scalar.activation(silu_sb[:, p2 * 512:p2 * 512 + L], pG[:, p2 * 512:p2 * 512 + L],
                                      AF.Silu, bias=0.0, scale=gs_sb[:, ic:ic + 1]).then_inc(s_sil, 1)
                    n += 1

        @block.gpsimd
        def _(g_eng):
            def conv(s2):
                par = s2 % 2
                g_eng.wait_ge(s_wld[par], 64 * (s2 // 2 + 1))
                if s2 >= 2:
                    g_eng.wait_ge(s_pmm, n_per_sup * (s2 - 1))
                g_eng.tensor_copy(gwbf[:, par * 4096:(par + 1) * 4096], gwst[:, par * 4096:(par + 1) * 4096]).then_inc(s_wcv, 1)
                g_eng.tensor_copy(uwbf[:, par * 4096:(par + 1) * 4096], uwst[:, par * 4096:(par + 1) * 4096]).then_inc(s_wcv, 1)
            conv(0)
            conv(1)
            n = 0
            for s2 in range(NSUP):
                for m in range(n_per_sup):
                    ic_loc = m // len(groups)
                    g = m % len(groups)
                    t0, L = groups[g]
                    ic = s2 * 2 + ic_loc
                    p2 = n % 3
                    g_eng.wait_ge(s_hm, n + 1)
                    g_eng.dma_start(h_dram.ap()[ic * 128:(ic + 1) * 128, col_off + t0: col_off + t0 + L],
                                    h_sb[:, p2 * 512:p2 * 512 + L]).then_inc(s_hst[n % 3], 16)
                    if m == n_per_sup - 1 and s2 + 2 < NSUP:
                        conv(s2 + 2)
                    n += 1
            for sl in range(3):
                g_eng.wait_ge(s_hst[sl], 16 * len([x for x in range(NIT) if x % 3 == sl]))
        block_cm.__exit__(None, None, None)


def _down_phase(nc, dwT, h_dram, x_view3, xres_out, groups, consts, tag,
                h_col_off=0, out_col_off=0):
    ds2_sb = consts["ds2_sb"]
    ng = len(groups)
    NQ = 4
    with (
        nc.sbuf_tensor(f"{tag}_dwst", [128, 2 * 2048], F32) as dwst,
        nc.sbuf_tensor(f"{tag}_dwbf", [128, 2 * 44 * 512], BF16) as dwbf,
        nc.sbuf_tensor(f"{tag}_hch", [128, 8 * 512], BF16) as hch,
        nc.sbuf_tensor(f"{tag}_xst", [128, 2 * 4 * 512], F32) as xst,
        nc.sbuf_tensor(f"{tag}_x1o", [128, 2 * 512], F32) as x1o,
        nc.psum_tensor(f"{tag}_pD", [128, 4 * 512], F32) as pD,
        nc.semaphore(f"{tag}_dwld0") as s_dwld0,
        nc.semaphore(f"{tag}_dwld1") as s_dwld1,
        nc.semaphore(f"{tag}_dwcv") as s_dwcv,
        nc.semaphore(f"{tag}_xlda") as s_xlda,
        nc.semaphore(f"{tag}_xldb") as s_xldb,
        nc.semaphore(f"{tag}_hld0") as s_hld0,
        nc.semaphore(f"{tag}_hld1") as s_hld1,
        nc.semaphore(f"{tag}_pic") as s_pic,
        nc.semaphore(f"{tag}_ev") as s_ev,
        nc.semaphore(f"{tag}_x1st0") as s_x1st0,
        nc.semaphore(f"{tag}_x1st1") as s_x1st1,
    ):
        _clear_sems(nc, [s_dwld0, s_dwld1, s_dwcv, s_xlda, s_xldb, s_hld0, s_hld1,
                         s_pic, s_ev, s_x1st0, s_x1st1])
        block_cm = nc.Block()
        block = block_cm.__enter__()
        dw3 = dwT.ap().rearrange("(c p) d -> p c d", p=128)
        s_hld = [s_hld0, s_hld1]
        s_dwld = [s_dwld0, s_dwld1]
        s_xld = [s_xlda, s_xldb]
        s_x1st = [s_x1st0, s_x1st1]

        def _dwload(sync, Qt, j):
            jj = Qt * 11 + j
            if jj >= 2:
                sync.wait_ge(s_dwcv, jj - 1)
            sync.dma_start(
                out=dwst.ap().rearrange("p (b c d) -> p b c d", b=2, c=4)[:, jj % 2],
                in_=dw3[:, j * 4:(j + 1) * 4, Qt * 512:(Qt + 1) * 512],
            ).then_inc(s_dwld[jj % 2], 16)

        @block.scalar
        def _(scalar):
            for Qt in range(NQ):
                for j in range(11):
                    _dwload(scalar, Qt, j)

        @block.sync
        def _(sync):
            for Q in range(NQ):
                for grp in range(ng):
                    gi = Q * ng + grp
                    t0, L = groups[grp]
                    if gi >= 2:
                        sync.wait_ge(s_ev, 4 * (gi - 1))
                    xv = xst.ap().rearrange("p (b c t) -> p b c t", b=2, c=4)[:, gi % 2]
                    sync.dma_start(out=xv[:, :, 0:L],
                                   in_=x_view3[:, Q * 4:(Q + 1) * 4, t0:t0 + L]).then_inc(s_xld[gi % 2], 16)
                    h3 = h_dram.ap().rearrange("(c p) t -> p c t", p=128)
                    hv = hch.ap().rearrange("p (s c t) -> p s c t", s=2, c=4)
                    for qd in range(11):
                        gq = gi * 11 + qd
                        if gq >= 2:
                            sync.wait_ge(s_pic, 4 * (gq - 2) + 4)
                        sync.dma_start(out=hv[:, gq % 2, :, 0:L],
                                       in_=h3[:, qd * 4:(qd + 1) * 4,
                                              h_col_off + t0:h_col_off + t0 + L]).then_inc(s_hld[gq % 2], 16)

        @block.vector
        def _(vector):
            for Q in range(NQ):
                for grp in range(ng):
                    gi = Q * ng + grp
                    t0, L = groups[grp]
                    vector.wait_ge(s_pic, ICN * (gi + 1))
                    vector.wait_ge(s_xld[gi % 2], 16 * (gi // 2 + 1))
                    for dc in range(4):
                        idx = 4 * gi + dc
                        if idx >= 2:
                            vector.wait_ge(s_x1st[idx % 2], 16 * ((idx - 2) // 2 + 1))
                        dcg = Q * 4 + dc
                        xv = xst.ap().rearrange("p (b c t) -> p b c t", b=2, c=4)[:, gi % 2, dc, 0:L]
                        vector.scalar_tensor_tensor(
                            x1o[:, (idx % 2) * 512:(idx % 2) * 512 + L],
                            pD[:, dc * 512:dc * 512 + L],
                            ds2_sb[:, dcg:dcg + 1], xv, ALU.mult, ALU.add).then_inc(s_ev, 1)

        @block.tensor
        def _(tensor):
            for Q in range(NQ):
                par = Q % 2
                for grp in range(ng):
                    gi = Q * ng + grp
                    t0, L = groups[grp]
                    for ic in range(ICN):
                        n_ic = gi * ICN + ic
                        if ic % 4 == 0:
                            gq = gi * 11 + ic // 4
                            tensor.wait_ge(s_hld[gq % 2], 16 * (gq // 2 + 1))
                        if ic == 0:
                            tensor.wait_ge(s_dwcv, 11 * (Q + 1))
                        sl = ((gi * 11 + ic // 4) % 2) * 4 + (ic % 4)
                        for dc in range(4):
                            if ic == 0 and gi >= 1:
                                tensor.wait_ge(s_ev, 4 * (gi - 1) + dc + 1)
                            lhs = dwbf[:, par * 22528 + ic * 512 + dc * 128: par * 22528 + ic * 512 + dc * 128 + 128]
                            mm = tensor.matmul(pD[:, dc * 512:dc * 512 + L],
                                               lhs, hch[:, sl * 512:sl * 512 + L],
                                               start=(ic == 0), stop=(ic == ICN - 1))
                        mm.then_inc(s_pic, 1)

        @block.gpsimd
        def _(g_eng):
            def conv(Qt, j):
                jj = Qt * 11 + j
                g_eng.wait_ge(s_dwld[jj % 2], 16 * (jj // 2 + 1))
                if Qt >= 2 and j == 0:
                    g_eng.wait_ge(s_pic, ICN * ng * (Qt - 1))
                g_eng.tensor_copy(
                    dwbf.ap().rearrange("p (b c d) -> p b c d", b=2, c=44, d=512)[:, Qt % 2, j * 4:(j + 1) * 4, :],
                    dwst.ap().rearrange("p (b c d) -> p b c d", b=2, c=4)[:, jj % 2]).then_inc(s_dwcv, 1)

            for j in range(11):
                conv(0, j)
            for j in range(11):
                conv(1, j)
            for Q in range(NQ):
                for grp in range(ng):
                    gi = Q * ng + grp
                    t0, L = groups[grp]
                    for dc in range(4):
                        idx = 4 * gi + dc
                        dcg = Q * 4 + dc
                        g_eng.wait_ge(s_ev, idx + 1)
                        g_eng.dma_start(
                            xres_out.ap()[dcg * 128:(dcg + 1) * 128,
                                          out_col_off + t0:out_col_off + t0 + L],
                            x1o[:, (idx % 2) * 512:(idx % 2) * 512 + L]).then_inc(s_x1st[idx % 2], 16)
                if Q + 2 < NQ:
                    for j in range(11):
                        conv(Q + 2, j)
            g_eng.wait_ge(s_x1st[0], 16 * 2 * NQ * ng)
            g_eng.wait_ge(s_x1st[1], 16 * 2 * NQ * ng)
        block_cm.__exit__(None, None, None)


def _kv_phase(nc, wkT, wvT, xn2_dram, k_dram, ve_dram, vo_dram, consts, tag):
    wk3 = wkT.ap().rearrange("(c p) f -> p c f", p=128)
    wv3 = wvT.ap().rearrange("(c p) f -> p c f", p=128)
    xn3v = xn2_dram.ap().rearrange("(c p) t -> p c t", p=128)
    ng = len(GROUPS)
    vtiles = []
    for g, (t0, L) in enumerate(GROUPS):
        for i in range(L // 128):
            vtiles.append((g, t0, i))
    NVT = len(vtiles)
    with (
        nc.sbuf_tensor(f"{tag}_wk", [128, 16 * 2048], BF16) as wk_sb,
        nc.sbuf_tensor(f"{tag}_wv", [128, 16 * 2048], BF16) as wv_sb,
        nc.sbuf_tensor(f"{tag}_wwst", [128, 2 * 2048], F32) as wwst,
        nc.sbuf_tensor(f"{tag}_xng", [128, 2 * 16 * 512], BF16) as xng,
        nc.sbuf_tensor(f"{tag}_ko", [128, 2 * 512], BF16) as k_out,
        nc.sbuf_tensor(f"{tag}_ve", [128, 2 * 2048], BF16) as ve_out,
        nc.sbuf_tensor(f"{tag}_vo", [128, 2 * 2048], BF16) as vo_out,
        nc.psum_tensor(f"{tag}_pKV", [128, 4096], F32) as pKV,
        nc.semaphore(f"{tag}_xnld0") as s_xnld0,
        nc.semaphore(f"{tag}_xnld1") as s_xnld1,
        nc.semaphore(f"{tag}_kmm") as s_kmm,
        nc.semaphore(f"{tag}_kev") as s_kev,
        nc.semaphore(f"{tag}_kst0") as s_kst0,
        nc.semaphore(f"{tag}_kst1") as s_kst1,
        nc.semaphore(f"{tag}_vmm") as s_vmm,
        nc.semaphore(f"{tag}_vev") as s_vev,
        nc.semaphore(f"{tag}_vst0") as s_vst0,
        nc.semaphore(f"{tag}_vst1") as s_vst1,
        nc.semaphore(f"{tag}_wwld0") as s_wwld0,
        nc.semaphore(f"{tag}_wwld1") as s_wwld1,
        nc.semaphore(f"{tag}_wwcv") as s_wwcv,
    ):
        _clear_sems(nc, [s_xnld0, s_xnld1, s_kmm, s_kev, s_kst0, s_kst1, s_vmm,
                         s_vev, s_vst0, s_vst1, s_wwld0, s_wwld1, s_wwcv])
        block_cm = nc.Block()
        block = block_cm.__enter__()
        s_wwld = [s_wwld0, s_wwld1]
        s_xnld = [s_xnld0, s_xnld1]
        s_kst = [s_kst0, s_kst1]
        s_vst = [s_vst0, s_vst1]

        def xn_load(sync, li, t0, L):
            v = xng.ap().rearrange("p (b c t) -> p b c t", b=2, c=16)[:, li % 2]
            src = xn3v[:, :, t0:t0 + L]
            sync.dma_start(out=v[:, 0:8, 0:L], in_=src[:, 0:8, :]).then_inc(s_xnld[li % 2], 16)
            sync.dma_start(out=v[:, 8:16, 0:L], in_=src[:, 8:16, :]).then_inc(s_xnld[li % 2], 16)

        @block.sync
        def _(sync):
            for jj in range(16):
                if jj >= 2:
                    sync.wait_ge(s_wwcv, jj - 1)
                sync.dma_start(out=wwst.ap().rearrange("p (b f) -> p b f", b=2)[:, jj % 2],
                               in_=wk3[:, jj, :]).then_inc(s_wwld[jj % 2], 16)
            wvj = list(range(16, 32))
            for g, (t0, L) in enumerate(GROUPS):
                li = g
                if li >= 2:
                    sync.wait_ge(s_kmm, 16 * (li - 1))
                xn_load(sync, li, t0, L)
                nw = [4, 3, 3, 3, 3][g]
                for _ in range(nw):
                    jj = wvj.pop(0)
                    if jj >= 2:
                        sync.wait_ge(s_wwcv, jj - 1)
                    sync.dma_start(out=wwst.ap().rearrange("p (b f) -> p b f", b=2)[:, jj % 2],
                                   in_=wv3[:, jj - 16, :]).then_inc(s_wwld[jj % 2], 16)
            tiles_cum = [4, 8, 12, 16, 17]
            for g2, (t0, L) in enumerate(GROUPS):
                li = ng + g2
                if g2 < 2:
                    sync.wait_ge(s_kmm, 16 * (li - 1))
                else:
                    sync.wait_ge(s_vmm, tiles_cum[g2 - 2])
                xn_load(sync, li, t0, L)

        @block.tensor
        def _(tensor):
            for g, (t0, L) in enumerate(GROUPS):
                tensor.wait_ge(s_xnld[g % 2], 32 * (g // 2 + 1))
                if g == 0:
                    tensor.wait_ge(s_wwcv, 16)
                for fc in range(16):
                    kidx = g * 16 + fc
                    if kidx >= 2:
                        tensor.wait_ge(s_kev, kidx - 1)
                    for dc in range(NCH):
                        mm = tensor.matmul(
                            pKV[:, (kidx % 2) * 512:(kidx % 2) * 512 + L],
                            wk_sb[:, dc * 2048 + fc * 128: dc * 2048 + fc * 128 + 128],
                            xng.ap().rearrange("p (b c t) -> p b c t", b=2, c=16)[:, g % 2, dc, 0:L],
                            start=(dc == 0), stop=(dc == NCH - 1))
                    mm.then_inc(s_kmm, 1)
            tensor.wait_ge(s_kev, 80)
            tensor.wait_ge(s_wwcv, 32)
            vidx = 0
            for g2, (t0, L) in enumerate(GROUPS):
                li = ng + g2
                tensor.wait_ge(s_xnld[li % 2], 32 * (li // 2 + 1))
                for tloc in range(L // 128):
                    vpar = vidx % 2
                    if vidx >= 2:
                        tensor.wait_ge(s_vev, vidx - 1)
                    for dc in range(NCH):
                        for fg in range(4):
                            mm = tensor.matmul(
                                pKV[:, vpar * 2048 + fg * 512: vpar * 2048 + fg * 512 + 512],
                                xng.ap().rearrange("p (b c t) -> p b c t", b=2, c=16)[:, li % 2, dc, tloc * 128:(tloc + 1) * 128],
                                wv_sb[:, dc * 2048 + fg * 512: dc * 2048 + fg * 512 + 512],
                                start=(dc == 0), stop=(dc == NCH - 1))
                    mm.then_inc(s_vmm, 1)
                    vidx += 1

        @block.vector
        def _(vector):
            for kidx in range(80):
                vector.wait_ge(s_kmm, kidx + 1)
                L = GROUPS[kidx // 16][1]
                if kidx >= 2:
                    vector.wait_ge(s_kst[kidx % 2], 16 * ((kidx - 2) // 2 + 1))
                vector.tensor_copy(k_out[:, (kidx % 2) * 512:(kidx % 2) * 512 + L],
                                   pKV[:, (kidx % 2) * 512:(kidx % 2) * 512 + L]).then_inc(s_kev, 1)
            vector.memset(ve_out[:], 0.0)
            vector.memset(vo_out[:], 0.0)
            for vidx in range(NVT):
                vector.wait_ge(s_vmm, vidx + 1)
                if vidx >= 2:
                    vector.wait_ge(s_vst[vidx % 2], 32 * ((vidx - 2) // 2 + 1))
                pv3 = pKV.ap().rearrange("p (a c f) -> p a c f", a=2, c=16)[:, vidx % 2]
                ve3 = ve_out.ap().rearrange("p (a c f) -> p a c f", a=2, c=16)[:, vidx % 2]
                vo3 = vo_out.ap().rearrange("p (a c f) -> p a c f", a=2, c=16)[:, vidx % 2]
                vector.tensor_copy(ve3[:, :, 0:64], pv3[:, :, 0:64])
                vector.tensor_copy(vo3[:, :, 64:128], pv3[:, :, 64:128]).then_inc(s_vev, 1)

        @block.gpsimd
        def _(g_eng):
            for jj in range(16):
                g_eng.wait_ge(s_wwld[jj % 2], 16 * (jj // 2 + 1))
                g_eng.tensor_copy(wk_sb[:, jj * 2048:(jj + 1) * 2048],
                                  wwst.ap().rearrange("p (b f) -> p b f", b=2)[:, jj % 2]).then_inc(s_wwcv, 1)
            wvp = list(range(16, 32))
            for kidx in range(80):
                g, fc = kidx // 16, kidx % 16
                t0, L = GROUPS[g]
                g_eng.wait_ge(s_kev, kidx + 1)
                g_eng.dma_start(k_dram.ap()[fc * 128:(fc + 1) * 128, t0:t0 + L],
                                k_out[:, (kidx % 2) * 512:(kidx % 2) * 512 + L]).then_inc(s_kst[kidx % 2], 16)
                if wvp and kidx % 4 == 1:
                    jj = wvp.pop(0)
                    g_eng.wait_ge(s_wwld[jj % 2], 16 * (jj // 2 + 1))
                    g_eng.tensor_copy(wv_sb[:, (jj - 16) * 2048:(jj - 15) * 2048],
                                      wwst.ap().rearrange("p (b f) -> p b f", b=2)[:, jj % 2]).then_inc(s_wwcv, 1)
            for jj in wvp:
                g_eng.wait_ge(s_wwld[jj % 2], 16 * (jj // 2 + 1))
                g_eng.tensor_copy(wv_sb[:, (jj - 16) * 2048:(jj - 15) * 2048],
                                  wwst.ap().rearrange("p (b f) -> p b f", b=2)[:, jj % 2]).then_inc(s_wwcv, 1)
            vidx = 0
            for g2, (t0, L) in enumerate(GROUPS):
                for tloc in range(L // 128):
                    g_eng.wait_ge(s_vev, vidx + 1)
                    r0 = t0 + tloc * 128
                    g_eng.dma_start(ve_dram.ap()[r0:r0 + 128, :],
                                    ve_out[:, (vidx % 2) * 2048:(vidx % 2 + 1) * 2048]).then_inc(s_vst[vidx % 2], 16)
                    g_eng.dma_start(vo_dram.ap()[r0:r0 + 128, :],
                                    vo_out[:, (vidx % 2) * 2048:(vidx % 2 + 1) * 2048]).then_inc(s_vst[vidx % 2], 16)
                    vidx += 1
            g_eng.wait_ge(s_kst[0], 16 * 40)
            g_eng.wait_ge(s_kst[1], 16 * 40)
            g_eng.wait_ge(s_vst[0], 32 * ((NVT + 1) // 2))
            g_eng.wait_ge(s_vst[1], 32 * (NVT // 2))
        block_cm.__exit__(None, None, None)


def _attn_phase(nc, wqT, woT, xn2_dram, k_dram, ve_dram, vo_dram, x1_dram,
                x2_dram, consts, tag, dbg=None):
    maskA_sb, maskB_sb, ident_sb = consts["maskA_sb"], consts["maskB_sb"], consts["ident_sb"]
    wq3 = wqT.ap().rearrange("(c p) f -> p c f", p=128)
    wo3 = woT.ap().rearrange("(c p) f -> p c f", p=128)
    xnv = xn2_dram.ap().rearrange("(c p) t -> p c t", p=128)
    x1v = x1_dram.ap().rearrange("(c p) t -> p c t", p=128)
    x2v = x2_dram.ap().rearrange("(c p) t -> p c t", p=128)
    kv3 = k_dram.ap().rearrange("(c p) t -> p c t", p=128)
    with ExitStack() as ctx:
        ec = ctx.enter_context
        wq_sb = ec(nc.sbuf_tensor(f"{tag}_wq", [128, 16 * 2048], BF16))
        wo_sb = ec(nc.sbuf_tensor(f"{tag}_wo", [128, 16 * 2048], BF16))
        wst = ec(nc.sbuf_tensor(f"{tag}_wst", [128, 2 * 512], F32))
        kr = ec(nc.sbuf_tensor(f"{tag}_kr", [128, 2 * 2048], BF16))
        vre = ec(nc.sbuf_tensor(f"{tag}_vre", [128, 2 * 2048], BF16))
        vro = ec(nc.sbuf_tensor(f"{tag}_vro", [128, 2 * 2048], BF16))
        xn_sb = ec(nc.sbuf_tensor(f"{tag}_xn", [128, 2048], BF16))
        x1_sb = ec(nc.sbuf_tensor(f"{tag}_x1", [128, 2048], F32))
        qe_sb = ec(nc.sbuf_tensor(f"{tag}_qe", [128, 2048], BF16))
        qo_sb = ec(nc.sbuf_tensor(f"{tag}_qo", [128, 2048], BF16))
        s_sb = ec(nc.sbuf_tensor(f"{tag}_ssb", [128, 1024], F32))
        probs_sb = ec(nc.sbuf_tensor(f"{tag}_prb", [128, 2 * 1024], BF16))
        pt_sb = ec(nc.sbuf_tensor(f"{tag}_pt", [128, 4 * 128], BF16))
        ao_sb = ec(nc.sbuf_tensor(f"{tag}_ao", [128, 2048], BF16))
        x2_sb = ec(nc.sbuf_tensor(f"{tag}_x2", [128, 2048], F32))
        nm_sb = ec(nc.sbuf_tensor(f"{tag}_nm", [128, 8], F32))
        sum_sb = ec(nc.sbuf_tensor(f"{tag}_sum", [128, 8], F32))
        rec_sb = ec(nc.sbuf_tensor(f"{tag}_rec", [128, 8], F32))
        pQ = ec(nc.psum_tensor(f"{tag}_pQ", [128, 2048], F32))
        pS = ec(nc.psum_tensor(f"{tag}_pS", [128, 1024], F32))
        pT = ec(nc.psum_tensor(f"{tag}_pT", [128, 512], BF16))
        pV = ec(nc.psum_tensor(f"{tag}_pV", [128, 512], F32))
        names = ["wld0", "wld1", "wqcv", "kvl0", "kvl1", "xnl0", "xnl1",
                 "x1l0", "x1l1", "qmm", "qev", "smm", "sev", "max", "exp",
                 "nrm4", "tmm", "ptev", "pvp", "aoev", "omm", "x2ev", "x2st0",
                 "x2st1", "dbgst", "rcp"]
        sems = {n: ec(nc.semaphore(f"{tag}_{n}")) for n in names}
        _clear_sems(nc, list(sems.values()))
        (s_wld0, s_wld1, s_wqcv, s_kvl0, s_kvl1, s_xnl0, s_xnl1,
         s_x1l0, s_x1l1, s_qmm, s_qev, s_smm, s_sev, s_max, s_exp,
         s_nrm4, s_tmm, s_ptev, s_pvp, s_aoev, s_omm, s_x2ev, s_x2st0,
         s_x2st1, s_dbgst, s_rcp) = (sems[n] for n in names)
        block = ec(nc.Block())
        s_wld = [s_wld0, s_wld1]
        s_kvl = [s_kvl0, s_kvl1]
        s_xnl = [s_xnl0, s_xnl1]
        s_x1l = [s_x1l0, s_x1l1]
        s_x2st = [s_x2st0, s_x2st1]


        def kv_load(sync, b):
            sl = b % 2
            col = b * 128
            dst = kr.ap().rearrange("p (s c t) -> p s c t", s=2, c=16)[:, sl]
            src = kv3[:, :, col:col + 128]
            for q in range(4):
                sync.dma_start(dst[:, q * 4:(q + 1) * 4, :], src[:, q * 4:(q + 1) * 4, :]).then_inc(s_kvl[sl], 16)
            sync.dma_start(vre[:, sl * 2048:(sl + 1) * 2048], ve_dram.ap()[col:col + 128, :]).then_inc(s_kvl[sl], 16)
            sync.dma_start(vro[:, sl * 2048:(sl + 1) * 2048], vo_dram.ap()[col:col + 128, :]).then_inc(s_kvl[sl], 16)

        def xn_load(sync, b):
            col = b * 128
            dst = xn_sb.ap().rearrange("p (c t) -> p c t", c=16)
            for q in range(4):
                sync.dma_start(dst[:, q * 4:(q + 1) * 4, :], xnv[:, q * 4:(q + 1) * 4, col:col + 128]).then_inc(s_xnl0, 16)

        def x1_load(sync, b):
            col = b * 128
            dst1 = x1_sb.ap().rearrange("p (c t) -> p c t", c=16)
            for q in range(4):
                sync.dma_start(dst1[:, q * 4:(q + 1) * 4, :], x1v[:, q * 4:(q + 1) * 4, col:col + 128]).then_inc(s_x1l0, 16)

        @block.sync
        def _(sync):
            for w, (w3,) in enumerate(((wq3,), (wo3,))):
                for c in range(16):
                    for hf in range(4):
                        jj = w * 64 + c * 4 + hf
                        if jj >= 2:
                            sync.wait_ge(s_wqcv, jj - 1)
                        sync.dma_start(out=wst.ap().rearrange("p (b f) -> p b f", b=2)[:, jj % 2],
                                       in_=w3[:, c, hf * 512:(hf + 1) * 512]).then_inc(s_wld[jj % 2], 16)
            kv_load(sync, 0)
            kv_load(sync, 1)
            xn_load(sync, 1)
            x1_load(sync, 1)
            for b in range(1, NBLK + 1):
                bi = b - 1
                if b + 1 <= NBLK:
                    sync.wait_ge(s_pvp, 16 * b)
                    kv_load(sync, b + 1)
                    sync.wait_ge(s_qmm, b)
                    xn_load(sync, b + 1)
                    sync.wait_ge(s_x2ev, b)
                    x1_load(sync, b + 1)

        @block.tensor
        def _(tensor):
            for b in range(1, NBLK + 1):
                bi = b - 1
                p = b % 2
                tensor.wait_ge(s_xnl0, 64 * b)
                if b == 1:
                    tensor.wait_ge(s_wqcv, 64)
                if bi >= 1:
                    tensor.wait_ge(s_x2ev, bi)
                for fc in range(16):
                    for dc in range(16):
                        mm = tensor.matmul(
                            pQ[:, fc * 128:(fc + 1) * 128],
                            wq_sb[:, dc * 2048 + fc * 128: dc * 2048 + fc * 128 + 128],
                            xn_sb[:, dc * 128:(dc + 1) * 128],
                            start=(dc == 0), stop=(dc == 15))
                mm.then_inc(s_qmm, 1)
                slp, slc = (b - 1) % 2, b % 2
                tensor.wait_ge(s_kvl[slp], 96 * ((b - 1) // 2 + 1))
                tensor.wait_ge(s_kvl[slc], 96 * (b // 2 + 1))
                if dbg is not None and bi >= 1:
                    tensor.wait_ge(s_dbgst, 144 * bi)
                tensor.wait_ge(s_qev, bi + 1)
                for k in range(9):
                    if k <= 7:
                        hb = k
                        B = 8 * bi + hb
                        tensor.wait_ge(s_sev, max(0, B - 1))
                        scb = pS if B % 2 == 0 else pQ
                        for j in range(4):
                            h = hb * 4 + j
                            qz = qe_sb if h % 2 == 0 else qo_sb
                            qs = qz[:, (h // 2) * 128:(h // 2) * 128 + 128]
                            for half, sl in ((0, slp), (1, slc)):
                                ks = kr[:, sl * 2048 + (h // 2) * 128: sl * 2048 + (h // 2) * 128 + 128]
                                mm = tensor.matmul(scb[:, j * 256 + half * 128: j * 256 + half * 128 + 128],
                                                   qs, ks, start=True, stop=True)
                        mm.then_inc(s_smm, 1)
                    if k >= 1:
                        hb = k - 1
                        B = 8 * bi + hb
                        tensor.wait_ge(s_nrm4, B + 1)
                        for j in range(4):
                            h = hb * 4 + j
                            hh = 32 * bi + h
                            if hh >= 2:
                                tensor.wait_ge(s_ptev, 2 * (hh - 2) + 2)
                            for half in (0, 1):
                                sc = ((h % 2) * 2 + half) * 128
                                mm = tensor.transpose(
                                    pT[:, sc:sc + 128],
                                    probs_sb[:, (hb % 2) * 1024 + j * 256 + half * 128: (hb % 2) * 1024 + j * 256 + half * 128 + 128],
                                    ident_sb[:, :])
                            mm.then_inc(s_tmm, 1)
                            hp = hh // 2
                            tensor.wait_ge(s_ptev, 2 * hh + 2)
                            if h % 2 == 0 and hp >= 4:
                                tensor.wait_ge(s_aoev, hp - 3)
                            vz = vre if h % 2 == 0 else vro
                            for half, sl in ((0, slp), (1, slc)):
                                sc = ((h % 2) * 2 + half) * 128
                                mm = tensor.matmul(
                                    pV[:, (hp % 4) * 128:(hp % 4) * 128 + 128],
                                    vz[:, sl * 2048 + (h // 2) * 128: sl * 2048 + (h // 2) * 128 + 128],
                                    pt_sb[:, sc:sc + 128],
                                    start=(h % 2 == 0 and half == 0),
                                    stop=(h % 2 == 1 and half == 1))
                            if h % 2 == 1:
                                mm.then_inc(s_pvp, 1)
                tensor.wait_ge(s_aoev, 16 * bi + 16)
                if b == 1:
                    tensor.wait_ge(s_wqcv, 64)
                for dc in range(16):
                    for fc in range(16):
                        mm = tensor.matmul(
                            pQ[:, dc * 128:(dc + 1) * 128],
                            wo_sb[:, fc * 2048 + dc * 128: fc * 2048 + dc * 128 + 128],
                            ao_sb[:, fc * 128:(fc + 1) * 128],
                            start=(fc == 0), stop=(fc == 15))
                mm.then_inc(s_omm, 1)

        @block.vector
        def _(vector):
            for w in range(2):
                for c in range(16):
                    for hf in range(4):
                        jj = w * 64 + c * 4 + hf
                        vector.wait_ge(s_wld[jj % 2], 16 * (jj // 2 + 1))
                        dst = wq_sb if w == 0 else wo_sb
                        vector.tensor_copy(dst[:, c * 2048 + hf * 512: c * 2048 + (hf + 1) * 512],
                                           wst.ap().rearrange("p (b f) -> p b f", b=2)[:, jj % 2]).then_inc(s_wqcv, 1)
            for b in range(1, NBLK + 1):
                bi = b - 1
                p = b % 2
                if b == 1:
                    vector.memset(qe_sb[64:128, :], 0.0)
                    vector.memset(qo_sb[0:64, :], 0.0)
                vector.wait_ge(s_qmm, bi + 1)
                if dbg is not None and bi >= 1:
                    vector.wait_ge(s_dbgst, 144 * (bi - 1) + 64)
                vector.tensor_copy(qe_sb[0:64, :], pQ[0:64, :])
                vector.tensor_copy(qo_sb[64:128, :], pQ[64:128, :]).then_inc(s_qev, 1)
                for k in range(9):
                    if k >= 1:
                        hb = k - 1
                        B = 8 * bi + hb
                        vector.wait_ge(s_exp, 4 * B + 4)
                        vector.tensor_reduce(sum_sb[:, (hb % 2) * 4:(hb % 2) * 4 + 4],
                                             probs_sb.ap().rearrange("p (a j c) -> p a j c", a=2, c=256)[:, hb % 2],
                                             AX.X, ALU.add).then_inc(s_rcp, 1)
                        vector.wait_ge(s_rcp, 2 * B + 1)
                        vector.reciprocal(rec_sb[:, (hb % 2) * 4:(hb % 2) * 4 + 4],
                                          sum_sb[:, (hb % 2) * 4:(hb % 2) * 4 + 4]).then_inc(s_rcp, 1)
                        vector.wait_ge(s_rcp, 2 * B + 2)
                        for j in range(4):
                            op = vector.tensor_scalar_mul(
                                probs_sb[:, (hb % 2) * 1024 + j * 256:(hb % 2) * 1024 + j * 256 + 256],
                                probs_sb[:, (hb % 2) * 1024 + j * 256:(hb % 2) * 1024 + j * 256 + 256],
                                rec_sb[:, (hb % 2) * 4 + j:(hb % 2) * 4 + j + 1])
                        op.then_inc(s_nrm4, 1)
                    if k <= 7:
                        hb = k
                        B = 8 * bi + hb
                        vector.wait_ge(s_smm, B + 1)
                        if B >= 1:
                            vector.wait_ge(s_exp, 4 * B)
                        msk = maskA_sb if b == 1 else maskB_sb
                        scb = pS if B % 2 == 0 else pQ
                        vector.scalar_tensor_tensor(
                            s_sb.ap().rearrange("p (j c) -> p j c", c=256),
                            scb.ap()[:, 0:1024].rearrange("p (j c) -> p j c", c=256),
                            0.125,
                            msk.ap().unsqueeze(1).broadcast_to([128, 4, 256]),
                            ALU.mult, ALU.add).then_inc(s_sev, 1)
                        vector.tensor_reduce(nm_sb[:, (hb % 2) * 4:(hb % 2) * 4 + 4],
                                             s_sb.ap().rearrange("p (j c) -> p j c", c=256),
                                             AX.X, ALU.max, negate=True).then_inc(s_max, 1)
                vector.wait_ge(s_omm, bi + 1)
                vector.wait_ge(s_x1l0, 64 * (bi + 1))
                if bi >= 1:
                    vector.wait_ge(s_x2st0, 64 * bi)
                vector.tensor_add(x2_sb[:], pQ[:], x1_sb[:]).then_inc(s_x2ev, 1)

        @block.scalar
        def _(scalar):
            for b in range(1, NBLK + 1):
                bi = b - 1
                for k in range(9):
                    if k <= 7:
                        hb = k
                        B = 8 * bi + hb
                        scalar.wait_ge(s_max, B + 1)
                        if B >= 2:
                            scalar.wait_ge(s_tmm, 4 * (B - 2) + 4)
                        if dbg is not None and hb == 2:
                            scalar.wait_ge(s_dbgst, 144 * bi + 80)
                        for j in range(4):
                            scalar.activation(
                                probs_sb[:, (hb % 2) * 1024 + j * 256:(hb % 2) * 1024 + j * 256 + 256],
                                s_sb[:, j * 256:(j + 1) * 256],
                                AF.Exp, bias=nm_sb[:, (hb % 2) * 4 + j:(hb % 2) * 4 + j + 1],
                                scale=1.0).then_inc(s_exp, 1)
                    if k < 1:
                        continue
                    hb = k - 1
                    for j in range(4):
                        h = hb * 4 + j
                        hh = 32 * bi + h
                        scalar.wait_ge(s_tmm, hh + 1)
                        if hh >= 2:
                            scalar.wait_ge(s_pvp, (hh - 2) // 2 + 1)
                        for half in (0, 1):
                            sc = ((h % 2) * 2 + half) * 128
                            scalar.copy(pt_sb[:, sc:sc + 128], pT[:, sc:sc + 128]).then_inc(s_ptev, 1)
                        if h % 2 == 1:
                            hp = hh // 2
                            scalar.wait_ge(s_pvp, 16 * bi + (h - 1) // 2 + 1)
                            pl = (h - 1) // 2
                            scalar.copy(ao_sb[:, pl * 128:(pl + 1) * 128],
                                        pV[:, (hp % 4) * 128:(hp % 4) * 128 + 128]).then_inc(s_aoev, 1)

        @block.gpsimd
        def _(g_eng):
            for b in range(1, NBLK + 1):
                bi = b - 1
                if dbg is not None:
                    qv = dbg["q_dump"].ap().rearrange("(c p) t -> p c t", p=128)
                    av = dbg["ao_dump"].ap().rearrange("(c p) t -> p c t", p=128)
                    g_eng.wait_ge(s_qev, bi + 1)
                    for qq in range(2):
                        g_eng.dma_start(qv[0:64, qq * 8:(qq + 1) * 8, bi * 128:(bi + 1) * 128],
                                        qe_sb.ap().rearrange("p (c t) -> p c t", c=16)[0:64, qq * 8:(qq + 1) * 8, :]).then_inc(s_dbgst, 16)
                    for qq in range(2):
                        g_eng.dma_start(qv[64:128, qq * 8:(qq + 1) * 8, bi * 128:(bi + 1) * 128],
                                        qo_sb.ap().rearrange("p (c t) -> p c t", c=16)[64:128, qq * 8:(qq + 1) * 8, :]).then_inc(s_dbgst, 16)
                    g_eng.wait_ge(s_nrm4, 8 * bi + 1)
                    g_eng.dma_start(dbg["pr_dump"].ap()[:, bi * 1024:(bi + 1) * 1024],
                                    probs_sb[:, 0:1024]).then_inc(s_dbgst, 16)
                    g_eng.wait_ge(s_aoev, 16 * bi + 16)
                    asv = ao_sb.ap().rearrange("p (c t) -> p c t", c=16)
                    for qq in range(4):
                        g_eng.dma_start(av[:, qq * 4:(qq + 1) * 4, bi * 128:(bi + 1) * 128],
                                        asv[:, qq * 4:(qq + 1) * 4, :]).then_inc(s_dbgst, 16)
                g_eng.wait_ge(s_x2ev, bi + 1)
                dst = x2v[:, :, bi * 128:(bi + 1) * 128]
                src = x2_sb.ap().rearrange("p (c t) -> p c t", c=16)
                for q in range(4):
                    g_eng.dma_start(dst[:, q * 4:(q + 1) * 4, :], src[:, q * 4:(q + 1) * 4, :]).then_inc(s_x2st0, 16)
            g_eng.wait_ge(s_x2st0, 64 * NBLK)
            if dbg is not None:
                g_eng.wait_ge(s_dbgst, 144 * NBLK)


def _build(taps=(), upto=99, race_detect=True):
    nc = bass.Bass("TRN2", target_bir_lowering=False, detect_race_conditions=race_detect)
    xT = nc.dram_tensor("xT", [D, T_ALL], F32, kind="ExternalInput")
    gwT = nc.dram_tensor("gwT", [D, IF], F32, kind="ExternalInput")
    uwT = nc.dram_tensor("uwT", [D, IF], F32, kind="ExternalInput")
    dwT = nc.dram_tensor("dwT", [IF, D], F32, kind="ExternalInput")
    gs = nc.dram_tensor("gs", [128, 44], F32, kind="ExternalInput")
    us = nc.dram_tensor("us", [128, 44], F32, kind="ExternalInput")
    ds = nc.dram_tensor("ds", [128, 16], F32, kind="ExternalInput")
    wqT = nc.dram_tensor("wqT", [D, D], F32, kind="ExternalInput")
    wkT = nc.dram_tensor("wkT", [D, D], F32, kind="ExternalInput")
    wvT = nc.dram_tensor("wvT", [D, D], F32, kind="ExternalInput")
    woT = nc.dram_tensor("woT", [D, D], F32, kind="ExternalInput")
    maskA = nc.dram_tensor("maskA", [128, 256], F32, kind="ExternalInput")
    maskB = nc.dram_tensor("maskB", [128, 256], F32, kind="ExternalInput")
    ident = nc.dram_tensor("ident", [128, 128], BF16, kind="ExternalInput")

    def scratch(name, shape, dt):
        kind = "ExternalOutput" if name in taps else "Internal"
        return nc.dram_tensor(name, shape, dt, kind=kind)

    h_dram = scratch("h_dram", [IF, T_ALL], BF16)
    x1_dram = scratch("x1_dram", [D, T_ALL], F32)
    k_dram = scratch("k_dram", [D, T_ALL], BF16)
    ve_dram = scratch("ve_dram", [T_ALL, D], BF16)
    vo_dram = scratch("vo_dram", [T_ALL, D], BF16)
    xn2_dram = scratch("xn2_dram", [D, T_ALL], BF16)
    x2_dram = scratch("x2_dram", [D, T_OWN], F32)
    outT = nc.dram_tensor("outT", [D, T_OWN], F32, kind="ExternalOutput")

    x_view3 = xT.ap().rearrange("(c p) t -> p c t", p=128)
    x1_view3 = x1_dram.ap().rearrange("(c p) t -> p c t", p=128)
    x2_view3 = x2_dram.ap().rearrange("(c p) t -> p c t", p=128)

    with (
        nc.sbuf_tensor("gs_sb", [128, 44], F32) as gs_sb,
        nc.sbuf_tensor("us_sb", [128, 44], F32) as us_sb,
        nc.sbuf_tensor("ds2_sb", [128, 16], F32) as ds2_sb,
        nc.sbuf_tensor("maskA_sb", [128, 256], F32) as maskA_sb,
        nc.sbuf_tensor("maskB_sb", [128, 256], F32) as maskB_sb,
        nc.sbuf_tensor("ident_sb", [128, 128], BF16) as ident_sb,
        nc.sbuf_tensor("ones128", [128, 128], BF16) as ones128,
        nc.sbuf_tensor("eps_col", [128, 1], F32) as eps_col,
    ):
        consts = dict(gs_sb=gs_sb, us_sb=us_sb, ds2_sb=ds2_sb, maskA_sb=maskA_sb,
                      maskB_sb=maskB_sb, ident_sb=ident_sb, ones128=ones128,
                      eps_col=eps_col)
        with (
            nc.semaphore("p0ld") as p0ld,
            nc.Block() as block,
        ):
            @block.sync
            def _(sync):
                sync.dma_start(out=gs_sb[:], in_=gs.ap()).then_inc(p0ld, 16)
                sync.dma_start(out=us_sb[:], in_=us.ap()).then_inc(p0ld, 16)
                sync.dma_start(out=ds2_sb[:], in_=ds.ap()).then_inc(p0ld, 16)
                sync.dma_start(out=maskA_sb[:], in_=maskA.ap()).then_inc(p0ld, 16)
                sync.dma_start(out=maskB_sb[:], in_=maskB.ap()).then_inc(p0ld, 16)
                sync.dma_start(out=ident_sb[:], in_=ident.ap()).then_inc(p0ld, 16)

            @block.vector
            def _(vector):
                vector.wait_ge(p0ld, 96)
                vector.tensor_scalar_mul(ds2_sb[:], ds2_sb[:], 0.5)
                vector.memset(ones128[:], 1.0)
                vector.memset(eps_col[:], EPS)

        if upto < 1:
            return nc
        with nc.sbuf_tensor("xn1", [128, NCH * T_ALL], BF16) as xn1_sb:
            with (
                nc.sbuf_tensor("p1_xst", [128, 2 * 4096], F32) as p1_xst,
                nc.sbuf_tensor("p1_tmp", [128, 4096], F32) as p1_tmp,
                nc.sbuf_tensor("p1_xc", [128, 4096], BF16) as p1_xc,
                nc.sbuf_tensor("p1_xsq", [128, 4096], BF16) as p1_xsq,
                nc.sbuf_tensor("p1_rows", [128, 1024], F32) as p1_rows,
                nc.psum_tensor("p1_pstat", [128, 512], F32) as p1_pstat,
            ):
                bt = dict(xst=[p1_xst.ap()[:, 0:4096], p1_xst.ap()[:, 4096:8192]],
                          tmp=p1_tmp, xc=p1_xc, xsq=p1_xsq, rows=p1_rows,
                          pstat=p1_pstat)
                _ln_phase(nc, bt, x_view3, xn1_sb, T_ALL, LN_STEPS, consts, "ln1")
            if "xn1_dump" in taps:
                xn1_dump = nc.dram_tensor("xn1_dump", [128, NCH * T_ALL], BF16, kind="ExternalOutput")
                with nc.semaphore("dmp") as sdmp:
                    _clear_sems(nc, [sdmp])
                with nc.semaphore("dmp2") as sdmp, nc.Block() as block:
                    @block.gpsimd
                    def _(g_eng):
                        for c in range(NCH):
                            g_eng.dma_start(xn1_dump.ap()[:, c * T_ALL:(c + 1) * T_ALL],
                                            xn1_sb[:, c * T_ALL:(c + 1) * T_ALL]).then_inc(sdmp, 16)
                        g_eng.wait_ge(sdmp, 16 * NCH)
            if upto < 2:
                return nc
            _gateup_phase(nc, xn1_sb, T_ALL, GROUPS, gwT, uwT, h_dram, consts, "g1")

        if upto < 3:
            return nc
        _down_phase(nc, dwT, h_dram, x_view3, x1_dram, GROUPS, consts, "d1")

        if upto < 4:
            return nc
        with (
            nc.sbuf_tensor("p3_xst", [128, 2 * 4096], F32) as p3_xst,
            nc.sbuf_tensor("p3_tmp", [128, 4096], F32) as p3_tmp,
            nc.sbuf_tensor("p3_xc", [128, 4096], BF16) as p3_xc,
            nc.sbuf_tensor("p3_xsq", [128, 4096], BF16) as p3_xsq,
            nc.sbuf_tensor("p3_rows", [128, 1024], F32) as p3_rows,
            nc.sbuf_tensor("p3_xout", [128, 2 * 4096], BF16) as p3_xout,
            nc.psum_tensor("p3_pstat", [128, 512], F32) as p3_pstat,
        ):
            bt = dict(xst=[p3_xst.ap()[:, 0:4096], p3_xst.ap()[:, 4096:8192]],
                      tmp=p3_tmp, xc=p3_xc, xsq=p3_xsq, rows=p3_rows,
                      pstat=p3_pstat)
            _ln_phase(nc, bt, x1_view3, p3_xout, T_ALL, LN_STEPS, consts, "ln2",
                      dram_out=xn2_dram)
        if upto < 5:
            return nc
        _kv_phase(nc, wkT, wvT, xn2_dram, k_dram, ve_dram, vo_dram, consts, "kv")

        if upto < 6:
            return nc
        dbg = None
        if "q_dump" in taps:
            dbg = {"q_dump": nc.dram_tensor("q_dump", [D, T_OWN], BF16, kind="ExternalOutput"),
                   "ao_dump": nc.dram_tensor("ao_dump", [D, T_OWN], BF16, kind="ExternalOutput"),
                   "pr_dump": nc.dram_tensor("pr_dump", [128, NBLK * 1024], BF16, kind="ExternalOutput")}
        _attn_phase(nc, wqT, woT, xn2_dram, k_dram, ve_dram, vo_dram, x1_dram,
                    x2_dram, consts, "at", dbg=dbg)

        if upto < 7:
            return nc
        with nc.sbuf_tensor("xn3", [128, NCH * T_OWN], BF16) as xn3_sb:
            with (
                nc.sbuf_tensor("p5_xst", [128, 2 * 4096], F32) as p5_xst,
                nc.sbuf_tensor("p5_tmp", [128, 4096], F32) as p5_tmp,
                nc.sbuf_tensor("p5_xc", [128, 4096], BF16) as p5_xc,
                nc.sbuf_tensor("p5_xsq", [128, 4096], BF16) as p5_xsq,
                nc.sbuf_tensor("p5_rows", [128, 1024], F32) as p5_rows,
                nc.psum_tensor("p5_pstat", [128, 512], F32) as p5_pstat,
            ):
                bt = dict(xst=[p5_xst.ap()[:, 0:4096], p5_xst.ap()[:, 4096:8192]],
                          tmp=p5_tmp, xc=p5_xc, xsq=p5_xsq, rows=p5_rows,
                          pstat=p5_pstat)
                _ln_phase(nc, bt, x2_view3, xn3_sb, T_OWN, LN_STEPS_OWN, consts, "ln3")
            _gateup_phase(nc, xn3_sb, T_OWN, GROUPS_OWN, gwT, uwT, h_dram, consts,
                          "g2", col_off=HALO)

        if upto < 8:
            return nc
        _down_phase(nc, dwT, h_dram, x2_view3, outT, GROUPS_OWN, consts, "d2",
                    h_col_off=HALO, out_col_off=0)

    return nc



def _prep_maps(L0_attn_x, L0_attn_Wq, L0_attn_Wk, L0_attn_Wv, L0_attn_Wo,
               L0_ffn_gate_w, L0_ffn_up_w, L0_ffn_down_w,
               L0_ffn_gate_s, L0_ffn_up_s, L0_ffn_down_s):
    x = np.asarray(L0_attn_x, np.float32)
    common = {
        "gwT": np.ascontiguousarray(np.asarray(L0_ffn_gate_w, np.float32).T),
        "uwT": np.ascontiguousarray(np.asarray(L0_ffn_up_w, np.float32).T),
        "dwT": np.ascontiguousarray(np.asarray(L0_ffn_down_w, np.float32).T),
        "gs": np.ascontiguousarray(np.asarray(L0_ffn_gate_s, np.float32).reshape(44, 128).T),
        "us": np.ascontiguousarray(np.asarray(L0_ffn_up_s, np.float32).reshape(44, 128).T),
        "ds": np.ascontiguousarray(np.asarray(L0_ffn_down_s, np.float32).reshape(16, 128).T),
        "wqT": np.ascontiguousarray(np.asarray(L0_attn_Wq, np.float32).T),
        "wkT": np.ascontiguousarray(np.asarray(L0_attn_Wk, np.float32).T),
        "wvT": np.ascontiguousarray(np.asarray(L0_attn_Wv, np.float32).T),
        "woT": np.ascontiguousarray(np.asarray(L0_attn_Wo, np.float32).T),
        "ident": np.eye(128, dtype=ml_dtypes.bfloat16),
    }
    r = np.arange(128)[:, None]
    c = np.arange(256)[None, :]
    validB = (c >= r) & (c <= r + 128)
    maskB = np.where(validB, np.float32(0.0), np.float32(NEG))
    maskA0 = np.where(validB & (c >= 128), np.float32(0.0), np.float32(NEG))
    in_maps = []
    for core in range(8):
        b, hf = core // 2, core % 2
        t0 = hf * T_OWN
        own = x[b, t0:t0 + T_OWN]
        halo = x[b, t0 - HALO:t0] if hf == 1 else np.zeros((HALO, D), np.float32)
        xT_c = np.ascontiguousarray(np.concatenate([halo, own], 0).T)
        m = dict(common)
        m["xT"] = xT_c
        m["maskA"] = maskB if hf == 1 else maskA0
        m["maskB"] = maskB
        in_maps.append(m)
    return in_maps


def kernel(**inputs):
    in_maps = _prep_maps(**inputs)
    nc = _build()
    res = run_bass_kernel_spmd(nc, in_maps, core_ids=list(range(8)))
    out = np.empty((4, 4096, D), np.float32)
    for core in range(8):
        b, hf = core // 2, core % 2
        out[b, hf * T_OWN:(hf + 1) * T_OWN] = res.results[core]["outT"].T
    return out


# version 07 (final)
# speedup vs baseline: 1.1594x; 1.0082x over previous
from contextlib import ExitStack
import numpy as np
import ml_dtypes
import concourse.bass as bass
import concourse.mybir as mybir
from concourse.bass_utils import run_bass_kernel_spmd

F32 = mybir.dt.float32
BF16 = mybir.dt.bfloat16
AF = mybir.ActivationFunctionType
ALU = mybir.AluOpType
AX = mybir.AxisListType

D = 2048
NCH = 16
IF = 5632
ICN = 44
T_OWN = 2048
HALO = 128
T_ALL = 2176
NBLK = 16
EPS = 1e-5
NEG = -1.0e30

GROUPS = [(0, 512), (512, 512), (1024, 512), (1536, 512), (2048, 128)]
LN_STEPS = [(i * 256, 256) for i in range(8)] + [(2048, 128)]
GROUPS_OWN = [(0, 512), (512, 512), (1024, 512), (1536, 512)]
LN_STEPS_OWN = [(i * 256, 256) for i in range(8)]



def _clear_sems(nc, sems):
    with nc.Block() as blk:
        @blk.gpsimd
        def _(g):
            for sem in sems:
                g.sem_clear(sem)


def _ln_phase(nc, block_tensors, xin_view3, xn_out_sb, t_total, steps, consts, tag,
              dram_out=None, sync_hook=None, pool_hook=None):
    ones128 = consts["ones128"]
    t = block_tensors
    ns = len(steps)
    with (
        nc.semaphore(f"{tag}_xld0") as s_xld0,
        nc.semaphore(f"{tag}_xld1") as s_xld1,
        nc.semaphore(f"{tag}_v2p") as s_v2p,
        nc.semaphore(f"{tag}_pst") as s_pst,
        nc.semaphore(f"{tag}_var") as s_var,
        nc.semaphore(f"{tag}_sd") as s_sd,
        nc.semaphore(f"{tag}_nrm") as s_nrm,
        nc.semaphore(f"{tag}_ost0") as s_ost0,
        nc.semaphore(f"{tag}_ost1") as s_ost1,
    ):
        _clear_sems(nc, [s_xld0, s_xld1, s_v2p, s_pst, s_var, s_sd, s_nrm, s_ost0, s_ost1])
        block = None
        block_cm = nc.Block()
        block = block_cm.__enter__()
        s_xld = [s_xld0, s_xld1]
        s_ost = [s_ost0, s_ost1]

        @block.sync
        def _(sync):
            for s, (t0, L) in enumerate(steps):
                if s >= 2:
                    sync.wait_ge(s_nrm, s - 1)
                xst = t["xst"][s % 2]
                v = xst.rearrange("p (c t) -> p c t", t=256)
                sync.dma_start(out=v[:, 0:8, 0:L], in_=xin_view3[:, 0:8, t0:t0 + L]).then_inc(s_xld[s % 2], 16)
                sync.dma_start(out=v[:, 8:16, 0:L], in_=xin_view3[:, 8:16, t0:t0 + L]).then_inc(s_xld[s % 2], 16)
                if sync_hook is not None:
                    sync_hook(sync, s)

        @block.vector
        def _(vector):
            for s, (t0, L) in enumerate(steps):
                xst = t["xst"][s % 2]
                xv = xst.rearrange("p (c t) -> p c t", t=256)[:, :, 0:L]
                vector.wait_ge(s_pst, s + 1)
                rows = t["rows"]
                m = rows[:, 0:256]
                t1 = rows[:, 256:512]
                sd = rows[:, 512:768]
                r = rows[:, 768:1024]
                vector.tensor_scalar_mul(m[:, 0:L], t["pstat"][:, 0:L], 1.0 / D)
                vector.tensor_scalar_mul(t1[:, 0:L], t["pstat"][:, 256:256 + L], 1.0 / D).then_inc(s_var, 1)
                vector.tensor_tensor(sd[:, 0:L], m[:, 0:L], m[:, 0:L], ALU.mult)
                vector.tensor_sub(t1[:, 0:L], t1[:, 0:L], sd[:, 0:L]).then_inc(s_var, 1)
                vector.wait_ge(s_sd, s + 1)
                vector.reciprocal(r[:, 0:L], sd[:, 0:L])
                tmpv = t["tmp"].ap().rearrange("p (c t) -> p c t", t=256)[:, :, 0:L]
                mb = m[:, 0:L].unsqueeze(1).broadcast_to([128, 16, L])
                rb = r[:, 0:L].unsqueeze(1).broadcast_to([128, 16, L])
                vector.tensor_sub(tmpv, xv, mb)
                if dram_out is None:
                    xout = xn_out_sb.ap().rearrange("p (c t) -> p c t", t=t_total)[:, :, t0:t0 + L]
                else:
                    if s >= 2:
                        vector.wait_ge(s_ost[s % 2], 64 * ((s - 2) // 2 + 1))
                    xout = xn_out_sb.ap().rearrange("p (b c t) -> p b c t", b=2, c=NCH)[:, s % 2, :, 0:L]
                vector.tensor_tensor(xout, tmpv, rb, ALU.mult).then_inc(s_nrm, 1)

        @block.tensor
        def _(tensor):
            for s, (t0, L) in enumerate(steps):
                tensor.wait_ge(s_v2p, s + 1)
                if s >= 1:
                    tensor.wait_ge(s_var, 2 * s - 1)
                xcv = t["xc"].ap().rearrange("p (c t) -> p c t", t=256)
                xqv = t["xsq"].ap().rearrange("p (c t) -> p c t", t=256)
                for c in range(NCH):
                    tensor.matmul(t["pstat"][:, 0:L], ones128[:, :], xcv[:, c, 0:L],
                                  start=(c == 0), stop=(c == NCH - 1))
                for c in range(NCH):
                    mm = tensor.matmul(t["pstat"][:, 256:256 + L], ones128[:, :], xqv[:, c, 0:L],
                                       start=(c == 0), stop=(c == NCH - 1))
                mm.then_inc(s_pst, 1)

        @block.scalar
        def _(scalar):
            for s, (t0, L) in enumerate(steps):
                xst = t["xst"][s % 2]
                xv = xst.rearrange("p (c t) -> p c t", t=256)[:, :, 0:L]
                xcv = t["xc"].ap().rearrange("p (c t) -> p c t", t=256)[:, :, 0:L]
                xqv = t["xsq"].ap().rearrange("p (c t) -> p c t", t=256)[:, :, 0:L]
                scalar.wait_ge(s_xld[s % 2], 32 * (s // 2 + 1))
                scalar.copy(xcv, xv)
                scalar.square(xqv, xv).then_inc(s_v2p, 1)
                scalar.wait_ge(s_var, 2 * s + 2)
                rows = t["rows"]
                scalar.activation(rows[:, 512:512 + L], rows[:, 256:256 + L], AF.Sqrt,
                                  bias=consts["eps_col"][:, 0:1], scale=1.0).then_inc(s_sd, 1)

        if dram_out is not None or pool_hook is not None:
            @block.gpsimd
            def _(g_eng):
                o3 = None if dram_out is None else dram_out.ap().rearrange("(c p) t -> p c t", p=128)
                for s, (t0, L) in enumerate(steps):
                    if pool_hook is not None:
                        pool_hook(g_eng, s)
                    if dram_out is not None:
                        g_eng.wait_ge(s_nrm, s + 1)
                        v = xn_out_sb.ap().rearrange("p (b c t) -> p b c t", b=2, c=NCH)[:, s % 2]
                        for q in range(4):
                            g_eng.dma_start(o3[:, q * 4:(q + 1) * 4, t0:t0 + L],
                                            v[:, q * 4:(q + 1) * 4, 0:L]).then_inc(s_ost[s % 2], 16)
                if dram_out is not None:
                    g_eng.wait_ge(s_ost[0], 64 * ((ns + 1) // 2))
                    g_eng.wait_ge(s_ost[1], 64 * (ns // 2))
        block_cm.__exit__(None, None, None)


def _gateup_phase(nc, xn_sb, t_total, groups, gwT, uwT, h_dram, consts, tag,
                  col_off=0):
    gs_sb, us_sb = consts["gs_sb"], consts["us_sb"]
    NSUP = 22
    n_per_sup = 2 * len(groups)
    NIT = NSUP * n_per_sup
    with (
        nc.sbuf_tensor(f"{tag}_gwst", [128, 2 * 4096], F32) as gwst,
        nc.sbuf_tensor(f"{tag}_uwst", [128, 2 * 4096], F32) as uwst,
        nc.sbuf_tensor(f"{tag}_gwbf", [128, 2 * 4096], BF16) as gwbf,
        nc.sbuf_tensor(f"{tag}_uwbf", [128, 2 * 4096], BF16) as uwbf,
        nc.sbuf_tensor(f"{tag}_silu", [128, 3 * 512], F32) as silu_sb,
        nc.sbuf_tensor(f"{tag}_hsb", [128, 3 * 512], BF16) as h_sb,
        nc.psum_tensor(f"{tag}_pG", [128, 3 * 512], F32) as pG,
        nc.psum_tensor(f"{tag}_pU", [128, 3 * 512], F32) as pU,
        nc.semaphore(f"{tag}_wld0") as s_wld0,
        nc.semaphore(f"{tag}_wld1") as s_wld1,
        nc.semaphore(f"{tag}_wcv") as s_wcv,
        nc.semaphore(f"{tag}_pmm") as s_pmm,
        nc.semaphore(f"{tag}_sil") as s_sil,
        nc.semaphore(f"{tag}_hm") as s_hm,
        nc.semaphore(f"{tag}_hst0") as s_hst0,
        nc.semaphore(f"{tag}_hst1") as s_hst1,
        nc.semaphore(f"{tag}_hst2") as s_hst2,
    ):
        _clear_sems(nc, [s_wld0, s_wld1, s_wcv, s_pmm, s_sil, s_hm, s_hst0, s_hst1, s_hst2])
        block_cm = nc.Block()
        block = block_cm.__enter__()
        gw3 = gwT.ap().rearrange("(c p) i -> p c i", p=128)
        uw3 = uwT.ap().rearrange("(c p) i -> p c i", p=128)
        s_wld = [s_wld0, s_wld1]
        s_hst = [s_hst0, s_hst1, s_hst2]

        @block.sync
        def _(sync):
            for s2 in range(NSUP):
                if s2 >= 2:
                    sync.wait_ge(s_pmm, n_per_sup * (s2 - 1))
                    sync.wait_ge(s_wcv, 2 * (s2 - 1))
                i0 = s2 * 256
                par = s2 % 2
                for (st, w3) in ((gwst, gw3), (uwst, uw3)):
                    v = st.ap().rearrange("p (b c i) -> p b c i", b=2, c=16)[:, par]
                    sync.dma_start(out=v[:, 0:8, :], in_=w3[:, 0:8, i0:i0 + 256]).then_inc(s_wld[par], 16)
                    sync.dma_start(out=v[:, 8:16, :], in_=w3[:, 8:16, i0:i0 + 256]).then_inc(s_wld[par], 16)

        @block.vector
        def _(vector):
            n = 0
            for s2 in range(NSUP):
                for m in range(n_per_sup):
                    ic_loc = m // len(groups)
                    g = m % len(groups)
                    L = groups[g][1]
                    ic = s2 * 2 + ic_loc
                    p2 = n % 3
                    vector.wait_ge(s_sil, n + 1)
                    if n >= 3:
                        vector.wait_ge(s_hst[n % 3], 16 * ((n - 3) // 3 + 1))
                    vector.scalar_tensor_tensor(
                        h_sb[:, p2 * 512:p2 * 512 + L], pU[:, p2 * 512:p2 * 512 + L],
                        us_sb[:, ic:ic + 1], silu_sb[:, p2 * 512:p2 * 512 + L],
                        ALU.mult, ALU.mult).then_inc(s_hm, 1)
                    n += 1

        @block.tensor
        def _(tensor):
            n = 0
            for s2 in range(NSUP):
                par = s2 % 2
                tensor.wait_ge(s_wcv, 2 * (s2 + 1))
                for m in range(n_per_sup):
                    ic_loc = m // len(groups)
                    g = m % len(groups)
                    t0, L = groups[g]
                    p2 = n % 3
                    if n >= 3:
                        tensor.wait_ge(s_sil, n - 2)
                        tensor.wait_ge(s_hm, n - 2)
                    for c in range(NCH):
                        lhs = gwbf[:, par * 4096 + c * 256 + ic_loc * 128: par * 4096 + c * 256 + ic_loc * 128 + 128]
                        rhs = xn_sb[:, c * t_total + t0: c * t_total + t0 + L]
                        tensor.matmul(pG[:, p2 * 512:p2 * 512 + L], lhs, rhs,
                                      start=(c == 0), stop=(c == NCH - 1))
                    for c in range(NCH):
                        lhs = uwbf[:, par * 4096 + c * 256 + ic_loc * 128: par * 4096 + c * 256 + ic_loc * 128 + 128]
                        rhs = xn_sb[:, c * t_total + t0: c * t_total + t0 + L]
                        mm = tensor.matmul(pU[:, p2 * 512:p2 * 512 + L], lhs, rhs,
                                           start=(c == 0), stop=(c == NCH - 1))
                    mm.then_inc(s_pmm, 1)
                    n += 1

        @block.scalar
        def _(scalar):
            n = 0
            for s2 in range(NSUP):
                for m in range(n_per_sup):
                    ic_loc = m // len(groups)
                    g = m % len(groups)
                    L = groups[g][1]
                    ic = s2 * 2 + ic_loc
                    p2 = n % 3
                    scalar.wait_ge(s_pmm, n + 1)
                    if n >= 3:
                        scalar.wait_ge(s_hm, n - 2)
                    scalar.activation(silu_sb[:, p2 * 512:p2 * 512 + L], pG[:, p2 * 512:p2 * 512 + L],
                                      AF.Silu, bias=0.0, scale=gs_sb[:, ic:ic + 1]).then_inc(s_sil, 1)
                    n += 1

        @block.gpsimd
        def _(g_eng):
            def conv(s2):
                par = s2 % 2
                g_eng.wait_ge(s_wld[par], 64 * (s2 // 2 + 1))
                if s2 >= 2:
                    g_eng.wait_ge(s_pmm, n_per_sup * (s2 - 1))
                g_eng.tensor_copy(gwbf[:, par * 4096:(par + 1) * 4096], gwst[:, par * 4096:(par + 1) * 4096]).then_inc(s_wcv, 1)
                g_eng.tensor_copy(uwbf[:, par * 4096:(par + 1) * 4096], uwst[:, par * 4096:(par + 1) * 4096]).then_inc(s_wcv, 1)
            conv(0)
            conv(1)
            n = 0
            for s2 in range(NSUP):
                for m in range(n_per_sup):
                    ic_loc = m // len(groups)
                    g = m % len(groups)
                    t0, L = groups[g]
                    ic = s2 * 2 + ic_loc
                    p2 = n % 3
                    g_eng.wait_ge(s_hm, n + 1)
                    g_eng.dma_start(h_dram.ap()[ic * 128:(ic + 1) * 128, col_off + t0: col_off + t0 + L],
                                    h_sb[:, p2 * 512:p2 * 512 + L]).then_inc(s_hst[n % 3], 16)
                    if m == n_per_sup - 1 and s2 + 2 < NSUP:
                        conv(s2 + 2)
                    n += 1
            for sl in range(3):
                g_eng.wait_ge(s_hst[sl], 16 * len([x for x in range(NIT) if x % 3 == sl]))
        block_cm.__exit__(None, None, None)


def _down_phase(nc, dwT, h_dram, x_view3, xres_out, groups, consts, tag,
                h_col_off=0, out_col_off=0):
    ds2_sb = consts["ds2_sb"]
    ng = len(groups)
    NQ = 4
    with (
        nc.sbuf_tensor(f"{tag}_dwst", [128, 2 * 2048], F32) as dwst,
        nc.sbuf_tensor(f"{tag}_dwbf", [128, 2 * 44 * 512], BF16) as dwbf,
        nc.sbuf_tensor(f"{tag}_hch", [128, 16 * 512], BF16) as hch,
        nc.sbuf_tensor(f"{tag}_xst", [128, 2 * 4 * 512], F32) as xst,
        nc.sbuf_tensor(f"{tag}_x1o", [128, 2 * 512], F32) as x1o,
        nc.psum_tensor(f"{tag}_pD", [128, 4 * 512], F32) as pD,
        nc.semaphore(f"{tag}_dwld0") as s_dwld0,
        nc.semaphore(f"{tag}_dwld1") as s_dwld1,
        nc.semaphore(f"{tag}_dwcv") as s_dwcv,
        nc.semaphore(f"{tag}_xlda") as s_xlda,
        nc.semaphore(f"{tag}_xldb") as s_xldb,
        nc.semaphore(f"{tag}_hld0") as s_hld0,
        nc.semaphore(f"{tag}_hld1") as s_hld1,
        nc.semaphore(f"{tag}_hld2") as s_hld2,
        nc.semaphore(f"{tag}_hld3") as s_hld3,
        nc.semaphore(f"{tag}_pic") as s_pic,
        nc.semaphore(f"{tag}_ev") as s_ev,
        nc.semaphore(f"{tag}_x1st0") as s_x1st0,
        nc.semaphore(f"{tag}_x1st1") as s_x1st1,
    ):
        _clear_sems(nc, [s_dwld0, s_dwld1, s_dwcv, s_xlda, s_xldb, s_hld0, s_hld1,
                         s_hld2, s_hld3, s_pic, s_ev, s_x1st0, s_x1st1])
        block_cm = nc.Block()
        block = block_cm.__enter__()
        dw3 = dwT.ap().rearrange("(c p) d -> p c d", p=128)
        s_hld = [s_hld0, s_hld1, s_hld2, s_hld3]
        s_dwld = [s_dwld0, s_dwld1]
        s_xld = [s_xlda, s_xldb]
        s_x1st = [s_x1st0, s_x1st1]

        def _dwload(sync, Qt, j):
            jj = Qt * 11 + j
            if jj >= 2:
                sync.wait_ge(s_dwcv, jj - 1)
            sync.dma_start(
                out=dwst.ap().rearrange("p (b c d) -> p b c d", b=2, c=4)[:, jj % 2],
                in_=dw3[:, j * 4:(j + 1) * 4, Qt * 512:(Qt + 1) * 512],
            ).then_inc(s_dwld[jj % 2], 16)

        @block.scalar
        def _(scalar):
            for Qt in range(NQ):
                for j in range(11):
                    _dwload(scalar, Qt, j)

        @block.sync
        def _(sync):
            for Q in range(NQ):
                for grp in range(ng):
                    gi = Q * ng + grp
                    t0, L = groups[grp]
                    if gi >= 2:
                        sync.wait_ge(s_ev, 4 * (gi - 1))
                    xv = xst.ap().rearrange("p (b c t) -> p b c t", b=2, c=4)[:, gi % 2]
                    sync.dma_start(out=xv[:, :, 0:L],
                                   in_=x_view3[:, Q * 4:(Q + 1) * 4, t0:t0 + L]).then_inc(s_xld[gi % 2], 16)
                    h3 = h_dram.ap().rearrange("(c p) t -> p c t", p=128)
                    hv = hch.ap().rearrange("p (s c t) -> p s c t", s=4, c=4)
                    for qd in range(11):
                        gq = gi * 11 + qd
                        if gq >= 4:
                            sync.wait_ge(s_pic, 4 * (gq - 4) + 4)
                        sync.dma_start(out=hv[:, gq % 4, :, 0:L],
                                       in_=h3[:, qd * 4:(qd + 1) * 4,
                                              h_col_off + t0:h_col_off + t0 + L]).then_inc(s_hld[gq % 4], 16)

        @block.vector
        def _(vector):
            for Q in range(NQ):
                for grp in range(ng):
                    gi = Q * ng + grp
                    t0, L = groups[grp]
                    vector.wait_ge(s_pic, ICN * (gi + 1))
                    vector.wait_ge(s_xld[gi % 2], 16 * (gi // 2 + 1))
                    for dc in range(4):
                        idx = 4 * gi + dc
                        if idx >= 2:
                            vector.wait_ge(s_x1st[idx % 2], 16 * ((idx - 2) // 2 + 1))
                        dcg = Q * 4 + dc
                        xv = xst.ap().rearrange("p (b c t) -> p b c t", b=2, c=4)[:, gi % 2, dc, 0:L]
                        vector.scalar_tensor_tensor(
                            x1o[:, (idx % 2) * 512:(idx % 2) * 512 + L],
                            pD[:, dc * 512:dc * 512 + L],
                            ds2_sb[:, dcg:dcg + 1], xv, ALU.mult, ALU.add).then_inc(s_ev, 1)

        @block.tensor
        def _(tensor):
            for Q in range(NQ):
                par = Q % 2
                for grp in range(ng):
                    gi = Q * ng + grp
                    t0, L = groups[grp]
                    for ic in range(ICN):
                        n_ic = gi * ICN + ic
                        if ic % 4 == 0:
                            gq = gi * 11 + ic // 4
                            tensor.wait_ge(s_hld[gq % 4], 16 * (gq // 4 + 1))
                        if ic == 0:
                            tensor.wait_ge(s_dwcv, 11 * (Q + 1))
                        sl = ((gi * 11 + ic // 4) % 4) * 4 + (ic % 4)
                        for dc in range(4):
                            if ic == 0 and gi >= 1:
                                tensor.wait_ge(s_ev, 4 * (gi - 1) + dc + 1)
                            lhs = dwbf[:, par * 22528 + ic * 512 + dc * 128: par * 22528 + ic * 512 + dc * 128 + 128]
                            mm = tensor.matmul(pD[:, dc * 512:dc * 512 + L],
                                               lhs, hch[:, sl * 512:sl * 512 + L],
                                               start=(ic == 0), stop=(ic == ICN - 1))
                        mm.then_inc(s_pic, 1)

        @block.gpsimd
        def _(g_eng):
            def conv(Qt, j):
                jj = Qt * 11 + j
                g_eng.wait_ge(s_dwld[jj % 2], 16 * (jj // 2 + 1))
                if Qt >= 2 and j == 0:
                    g_eng.wait_ge(s_pic, ICN * ng * (Qt - 1))
                g_eng.tensor_copy(
                    dwbf.ap().rearrange("p (b c d) -> p b c d", b=2, c=44, d=512)[:, Qt % 2, j * 4:(j + 1) * 4, :],
                    dwst.ap().rearrange("p (b c d) -> p b c d", b=2, c=4)[:, jj % 2]).then_inc(s_dwcv, 1)

            for j in range(11):
                conv(0, j)
            for j in range(11):
                conv(1, j)
            for Q in range(NQ):
                for grp in range(ng):
                    gi = Q * ng + grp
                    t0, L = groups[grp]
                    for dc in range(4):
                        idx = 4 * gi + dc
                        dcg = Q * 4 + dc
                        g_eng.wait_ge(s_ev, idx + 1)
                        g_eng.dma_start(
                            xres_out.ap()[dcg * 128:(dcg + 1) * 128,
                                          out_col_off + t0:out_col_off + t0 + L],
                            x1o[:, (idx % 2) * 512:(idx % 2) * 512 + L]).then_inc(s_x1st[idx % 2], 16)
                if Q + 2 < NQ:
                    for j in range(11):
                        conv(Q + 2, j)
            g_eng.wait_ge(s_x1st[0], 16 * 2 * NQ * ng)
            g_eng.wait_ge(s_x1st[1], 16 * 2 * NQ * ng)
        block_cm.__exit__(None, None, None)


def _kv_phase(nc, wkT, wvT, xn2_dram, k_dram, ve_dram, vo_dram, consts, tag):
    wk3 = wkT.ap().rearrange("(c p) f -> p c f", p=128)
    wv3 = wvT.ap().rearrange("(c p) f -> p c f", p=128)
    xn3v = xn2_dram.ap().rearrange("(c p) t -> p c t", p=128)
    ng = len(GROUPS)
    vtiles = []
    for g, (t0, L) in enumerate(GROUPS):
        for i in range(L // 128):
            vtiles.append((g, t0, i))
    NVT = len(vtiles)
    with (
        nc.sbuf_tensor(f"{tag}_wk", [128, 16 * 2048], BF16) as wk_sb,
        nc.sbuf_tensor(f"{tag}_wv", [128, 16 * 2048], BF16) as wv_sb,
        nc.sbuf_tensor(f"{tag}_wwst", [128, 2 * 2048], F32) as wwst,
        nc.sbuf_tensor(f"{tag}_xng", [128, 2 * 16 * 512], BF16) as xng,
        nc.sbuf_tensor(f"{tag}_ko", [128, 2 * 512], BF16) as k_out,
        nc.sbuf_tensor(f"{tag}_ve", [128, 2 * 2048], BF16) as ve_out,
        nc.sbuf_tensor(f"{tag}_vo", [128, 2 * 2048], BF16) as vo_out,
        nc.psum_tensor(f"{tag}_pKV", [128, 4096], F32) as pKV,
        nc.semaphore(f"{tag}_xnld0") as s_xnld0,
        nc.semaphore(f"{tag}_xnld1") as s_xnld1,
        nc.semaphore(f"{tag}_kmm") as s_kmm,
        nc.semaphore(f"{tag}_kev") as s_kev,
        nc.semaphore(f"{tag}_kst0") as s_kst0,
        nc.semaphore(f"{tag}_kst1") as s_kst1,
        nc.semaphore(f"{tag}_vmm") as s_vmm,
        nc.semaphore(f"{tag}_vev") as s_vev,
        nc.semaphore(f"{tag}_vst0") as s_vst0,
        nc.semaphore(f"{tag}_vst1") as s_vst1,
        nc.semaphore(f"{tag}_wwld0") as s_wwld0,
        nc.semaphore(f"{tag}_wwld1") as s_wwld1,
        nc.semaphore(f"{tag}_wwcv") as s_wwcv,
    ):
        _clear_sems(nc, [s_xnld0, s_xnld1, s_kmm, s_kev, s_kst0, s_kst1, s_vmm,
                         s_vev, s_vst0, s_vst1, s_wwld0, s_wwld1, s_wwcv])
        block_cm = nc.Block()
        block = block_cm.__enter__()
        s_wwld = [s_wwld0, s_wwld1]
        s_xnld = [s_xnld0, s_xnld1]
        s_kst = [s_kst0, s_kst1]
        s_vst = [s_vst0, s_vst1]

        def xn_load(sync, li, t0, L):
            v = xng.ap().rearrange("p (b c t) -> p b c t", b=2, c=16)[:, li % 2]
            src = xn3v[:, :, t0:t0 + L]
            sync.dma_start(out=v[:, 0:8, 0:L], in_=src[:, 0:8, :]).then_inc(s_xnld[li % 2], 16)
            sync.dma_start(out=v[:, 8:16, 0:L], in_=src[:, 8:16, :]).then_inc(s_xnld[li % 2], 16)

        @block.sync
        def _(sync):
            for jj in range(16):
                if jj >= 2:
                    sync.wait_ge(s_wwcv, jj - 1)
                sync.dma_start(out=wwst.ap().rearrange("p (b f) -> p b f", b=2)[:, jj % 2],
                               in_=wk3[:, jj, :]).then_inc(s_wwld[jj % 2], 16)
            wvj = list(range(16, 32))
            for g, (t0, L) in enumerate(GROUPS):
                li = g
                if li >= 2:
                    sync.wait_ge(s_kmm, 16 * (li - 1))
                xn_load(sync, li, t0, L)
                nw = [4, 3, 3, 3, 3][g]
                for _ in range(nw):
                    jj = wvj.pop(0)
                    if jj >= 2:
                        sync.wait_ge(s_wwcv, jj - 1)
                    sync.dma_start(out=wwst.ap().rearrange("p (b f) -> p b f", b=2)[:, jj % 2],
                                   in_=wv3[:, jj - 16, :]).then_inc(s_wwld[jj % 2], 16)
            tiles_cum = [4, 8, 12, 16, 17]
            for g2, (t0, L) in enumerate(GROUPS):
                li = ng + g2
                if g2 < 2:
                    sync.wait_ge(s_kmm, 16 * (li - 1))
                else:
                    sync.wait_ge(s_vmm, tiles_cum[g2 - 2])
                xn_load(sync, li, t0, L)

        @block.tensor
        def _(tensor):
            for g, (t0, L) in enumerate(GROUPS):
                tensor.wait_ge(s_xnld[g % 2], 32 * (g // 2 + 1))
                if g == 0:
                    tensor.wait_ge(s_wwcv, 16)
                for fc in range(16):
                    kidx = g * 16 + fc
                    if kidx >= 2:
                        tensor.wait_ge(s_kev, kidx - 1)
                    for dc in range(NCH):
                        mm = tensor.matmul(
                            pKV[:, (kidx % 2) * 512:(kidx % 2) * 512 + L],
                            wk_sb[:, dc * 2048 + fc * 128: dc * 2048 + fc * 128 + 128],
                            xng.ap().rearrange("p (b c t) -> p b c t", b=2, c=16)[:, g % 2, dc, 0:L],
                            start=(dc == 0), stop=(dc == NCH - 1))
                    mm.then_inc(s_kmm, 1)
            tensor.wait_ge(s_kev, 80)
            tensor.wait_ge(s_wwcv, 32)
            vidx = 0
            for g2, (t0, L) in enumerate(GROUPS):
                li = ng + g2
                tensor.wait_ge(s_xnld[li % 2], 32 * (li // 2 + 1))
                for tloc in range(L // 128):
                    vpar = vidx % 2
                    if vidx >= 2:
                        tensor.wait_ge(s_vev, vidx - 1)
                    for dc in range(NCH):
                        for fg in range(4):
                            mm = tensor.matmul(
                                pKV[:, vpar * 2048 + fg * 512: vpar * 2048 + fg * 512 + 512],
                                xng.ap().rearrange("p (b c t) -> p b c t", b=2, c=16)[:, li % 2, dc, tloc * 128:(tloc + 1) * 128],
                                wv_sb[:, dc * 2048 + fg * 512: dc * 2048 + fg * 512 + 512],
                                start=(dc == 0), stop=(dc == NCH - 1))
                    mm.then_inc(s_vmm, 1)
                    vidx += 1

        @block.vector
        def _(vector):
            for kidx in range(80):
                vector.wait_ge(s_kmm, kidx + 1)
                L = GROUPS[kidx // 16][1]
                if kidx >= 2:
                    vector.wait_ge(s_kst[kidx % 2], 16 * ((kidx - 2) // 2 + 1))
                vector.tensor_copy(k_out[:, (kidx % 2) * 512:(kidx % 2) * 512 + L],
                                   pKV[:, (kidx % 2) * 512:(kidx % 2) * 512 + L]).then_inc(s_kev, 1)
            vector.memset(ve_out[:], 0.0)
            vector.memset(vo_out[:], 0.0)
            for vidx in range(NVT):
                vector.wait_ge(s_vmm, vidx + 1)
                if vidx >= 2:
                    vector.wait_ge(s_vst[vidx % 2], 32 * ((vidx - 2) // 2 + 1))
                pv3 = pKV.ap().rearrange("p (a c f) -> p a c f", a=2, c=16)[:, vidx % 2]
                ve3 = ve_out.ap().rearrange("p (a c f) -> p a c f", a=2, c=16)[:, vidx % 2]
                vo3 = vo_out.ap().rearrange("p (a c f) -> p a c f", a=2, c=16)[:, vidx % 2]
                vector.tensor_copy(ve3[:, :, 0:64], pv3[:, :, 0:64])
                vector.tensor_copy(vo3[:, :, 64:128], pv3[:, :, 64:128]).then_inc(s_vev, 1)

        @block.gpsimd
        def _(g_eng):
            for jj in range(16):
                g_eng.wait_ge(s_wwld[jj % 2], 16 * (jj // 2 + 1))
                g_eng.tensor_copy(wk_sb[:, jj * 2048:(jj + 1) * 2048],
                                  wwst.ap().rearrange("p (b f) -> p b f", b=2)[:, jj % 2]).then_inc(s_wwcv, 1)
            wvp = list(range(16, 32))
            for kidx in range(80):
                g, fc = kidx // 16, kidx % 16
                t0, L = GROUPS[g]
                g_eng.wait_ge(s_kev, kidx + 1)
                g_eng.dma_start(k_dram.ap()[fc * 128:(fc + 1) * 128, t0:t0 + L],
                                k_out[:, (kidx % 2) * 512:(kidx % 2) * 512 + L]).then_inc(s_kst[kidx % 2], 16)
                if wvp and kidx % 4 == 1:
                    jj = wvp.pop(0)
                    g_eng.wait_ge(s_wwld[jj % 2], 16 * (jj // 2 + 1))
                    g_eng.tensor_copy(wv_sb[:, (jj - 16) * 2048:(jj - 15) * 2048],
                                      wwst.ap().rearrange("p (b f) -> p b f", b=2)[:, jj % 2]).then_inc(s_wwcv, 1)
            for jj in wvp:
                g_eng.wait_ge(s_wwld[jj % 2], 16 * (jj // 2 + 1))
                g_eng.tensor_copy(wv_sb[:, (jj - 16) * 2048:(jj - 15) * 2048],
                                  wwst.ap().rearrange("p (b f) -> p b f", b=2)[:, jj % 2]).then_inc(s_wwcv, 1)
            vidx = 0
            for g2, (t0, L) in enumerate(GROUPS):
                for tloc in range(L // 128):
                    g_eng.wait_ge(s_vev, vidx + 1)
                    r0 = t0 + tloc * 128
                    g_eng.dma_start(ve_dram.ap()[r0:r0 + 128, :],
                                    ve_out[:, (vidx % 2) * 2048:(vidx % 2 + 1) * 2048]).then_inc(s_vst[vidx % 2], 16)
                    g_eng.dma_start(vo_dram.ap()[r0:r0 + 128, :],
                                    vo_out[:, (vidx % 2) * 2048:(vidx % 2 + 1) * 2048]).then_inc(s_vst[vidx % 2], 16)
                    vidx += 1
            g_eng.wait_ge(s_kst[0], 16 * 40)
            g_eng.wait_ge(s_kst[1], 16 * 40)
            g_eng.wait_ge(s_vst[0], 32 * ((NVT + 1) // 2))
            g_eng.wait_ge(s_vst[1], 32 * (NVT // 2))
        block_cm.__exit__(None, None, None)


def _attn_phase(nc, wqT, woT, xn2_dram, k_dram, ve_dram, vo_dram, x1_dram,
                x2_dram, consts, tag, dbg=None):
    maskA_sb, maskB_sb, ident_sb = consts["maskA_sb"], consts["maskB_sb"], consts["ident_sb"]
    wq3 = wqT.ap().rearrange("(c p) f -> p c f", p=128)
    wo3 = woT.ap().rearrange("(c p) f -> p c f", p=128)
    xnv = xn2_dram.ap().rearrange("(c p) t -> p c t", p=128)
    x1v = x1_dram.ap().rearrange("(c p) t -> p c t", p=128)
    x2v = x2_dram.ap().rearrange("(c p) t -> p c t", p=128)
    kv3 = k_dram.ap().rearrange("(c p) t -> p c t", p=128)
    with ExitStack() as ctx:
        ec = ctx.enter_context
        wq_sb = ec(nc.sbuf_tensor(f"{tag}_wq", [128, 16 * 2048], BF16))
        wo_sb = ec(nc.sbuf_tensor(f"{tag}_wo", [128, 16 * 2048], BF16))
        wst = ec(nc.sbuf_tensor(f"{tag}_wst", [128, 2 * 512], F32))
        kr = ec(nc.sbuf_tensor(f"{tag}_kr", [128, 2 * 2048], BF16))
        vre = ec(nc.sbuf_tensor(f"{tag}_vre", [128, 2 * 2048], BF16))
        vro = ec(nc.sbuf_tensor(f"{tag}_vro", [128, 2 * 2048], BF16))
        xn_sb = ec(nc.sbuf_tensor(f"{tag}_xn", [128, 2048], BF16))
        x1_sb = ec(nc.sbuf_tensor(f"{tag}_x1", [128, 2048], F32))
        qe_sb = ec(nc.sbuf_tensor(f"{tag}_qe", [128, 2048], BF16))
        qo_sb = ec(nc.sbuf_tensor(f"{tag}_qo", [128, 2048], BF16))
        s_sb = ec(nc.sbuf_tensor(f"{tag}_ssb", [128, 1024], F32))
        probs_sb = ec(nc.sbuf_tensor(f"{tag}_prb", [128, 2 * 1024], BF16))
        pt_sb = ec(nc.sbuf_tensor(f"{tag}_pt", [128, 4 * 128], BF16))
        ao_sb = ec(nc.sbuf_tensor(f"{tag}_ao", [128, 2048], BF16))
        x2_sb = ec(nc.sbuf_tensor(f"{tag}_x2", [128, 2048], F32))
        nm_sb = ec(nc.sbuf_tensor(f"{tag}_nm", [128, 8], F32))
        sum_sb = ec(nc.sbuf_tensor(f"{tag}_sum", [128, 8], F32))
        rec_sb = ec(nc.sbuf_tensor(f"{tag}_rec", [128, 8], F32))
        pQ = ec(nc.psum_tensor(f"{tag}_pQ", [128, 2048], F32))
        pS = ec(nc.psum_tensor(f"{tag}_pS", [128, 1024], F32))
        pT = ec(nc.psum_tensor(f"{tag}_pT", [128, 512], BF16))
        pV = ec(nc.psum_tensor(f"{tag}_pV", [128, 512], F32))
        names = ["wld0", "wld1", "wqcv", "kvl0", "kvl1", "xnl0", "xnl1",
                 "x1l0", "x1l1", "qmm", "qev", "smm", "sev", "max", "exp",
                 "nrm4", "tmm", "ptev", "pvp", "aoev", "omm", "x2ev", "x2st0",
                 "x2st1", "dbgst", "rcp"]
        sems = {n: ec(nc.semaphore(f"{tag}_{n}")) for n in names}
        _clear_sems(nc, list(sems.values()))
        (s_wld0, s_wld1, s_wqcv, s_kvl0, s_kvl1, s_xnl0, s_xnl1,
         s_x1l0, s_x1l1, s_qmm, s_qev, s_smm, s_sev, s_max, s_exp,
         s_nrm4, s_tmm, s_ptev, s_pvp, s_aoev, s_omm, s_x2ev, s_x2st0,
         s_x2st1, s_dbgst, s_rcp) = (sems[n] for n in names)
        block = ec(nc.Block())
        s_wld = [s_wld0, s_wld1]
        s_kvl = [s_kvl0, s_kvl1]
        s_xnl = [s_xnl0, s_xnl1]
        s_x1l = [s_x1l0, s_x1l1]
        s_x2st = [s_x2st0, s_x2st1]


        def kv_load(sync, b):
            sl = b % 2
            col = b * 128
            dst = kr.ap().rearrange("p (s c t) -> p s c t", s=2, c=16)[:, sl]
            src = kv3[:, :, col:col + 128]
            for q in range(4):
                sync.dma_start(dst[:, q * 4:(q + 1) * 4, :], src[:, q * 4:(q + 1) * 4, :]).then_inc(s_kvl[sl], 16)
            sync.dma_start(vre[:, sl * 2048:(sl + 1) * 2048], ve_dram.ap()[col:col + 128, :]).then_inc(s_kvl[sl], 16)
            sync.dma_start(vro[:, sl * 2048:(sl + 1) * 2048], vo_dram.ap()[col:col + 128, :]).then_inc(s_kvl[sl], 16)

        def xn_load(sync, b):
            col = b * 128
            dst = xn_sb.ap().rearrange("p (c t) -> p c t", c=16)
            for q in range(4):
                sync.dma_start(dst[:, q * 4:(q + 1) * 4, :], xnv[:, q * 4:(q + 1) * 4, col:col + 128]).then_inc(s_xnl0, 16)

        def x1_load(sync, b):
            col = b * 128
            dst1 = x1_sb.ap().rearrange("p (c t) -> p c t", c=16)
            for q in range(4):
                sync.dma_start(dst1[:, q * 4:(q + 1) * 4, :], x1v[:, q * 4:(q + 1) * 4, col:col + 128]).then_inc(s_x1l0, 16)

        @block.sync
        def _(sync):
            for w, (w3,) in enumerate(((wq3,), (wo3,))):
                for c in range(16):
                    for hf in range(4):
                        jj = w * 64 + c * 4 + hf
                        if jj >= 2:
                            sync.wait_ge(s_wqcv, jj - 1)
                        sync.dma_start(out=wst.ap().rearrange("p (b f) -> p b f", b=2)[:, jj % 2],
                                       in_=w3[:, c, hf * 512:(hf + 1) * 512]).then_inc(s_wld[jj % 2], 16)
            kv_load(sync, 0)
            kv_load(sync, 1)
            xn_load(sync, 1)
            x1_load(sync, 1)
            for b in range(1, NBLK + 1):
                bi = b - 1
                if b + 1 <= NBLK:
                    sync.wait_ge(s_pvp, 16 * b)
                    kv_load(sync, b + 1)
                    sync.wait_ge(s_qmm, b)
                    xn_load(sync, b + 1)
                    sync.wait_ge(s_x2ev, b)
                    x1_load(sync, b + 1)

        @block.tensor
        def _(tensor):
            for b in range(1, NBLK + 1):
                bi = b - 1
                p = b % 2
                tensor.wait_ge(s_xnl0, 64 * b)
                if b == 1:
                    tensor.wait_ge(s_wqcv, 64)
                if bi >= 1:
                    tensor.wait_ge(s_x2ev, bi)
                for fc in range(16):
                    for dc in range(16):
                        mm = tensor.matmul(
                            pQ[:, fc * 128:(fc + 1) * 128],
                            wq_sb[:, dc * 2048 + fc * 128: dc * 2048 + fc * 128 + 128],
                            xn_sb[:, dc * 128:(dc + 1) * 128],
                            start=(dc == 0), stop=(dc == 15))
                mm.then_inc(s_qmm, 1)
                slp, slc = (b - 1) % 2, b % 2
                tensor.wait_ge(s_kvl[slp], 96 * ((b - 1) // 2 + 1))
                tensor.wait_ge(s_kvl[slc], 96 * (b // 2 + 1))
                if dbg is not None and bi >= 1:
                    tensor.wait_ge(s_dbgst, 144 * bi)
                tensor.wait_ge(s_qev, bi + 1)
                for k in range(9):
                    if k <= 7:
                        hb = k
                        B = 8 * bi + hb
                        tensor.wait_ge(s_sev, max(0, B - 1))
                        scb = pS if B % 2 == 0 else pQ
                        for j in range(4):
                            h = hb * 4 + j
                            qz = qe_sb if h % 2 == 0 else qo_sb
                            qs = qz[:, (h // 2) * 128:(h // 2) * 128 + 128]
                            for half, sl in ((0, slp), (1, slc)):
                                ks = kr[:, sl * 2048 + (h // 2) * 128: sl * 2048 + (h // 2) * 128 + 128]
                                mm = tensor.matmul(scb[:, j * 256 + half * 128: j * 256 + half * 128 + 128],
                                                   qs, ks, start=True, stop=True)
                        mm.then_inc(s_smm, 1)
                    if k >= 1:
                        hb = k - 1
                        B = 8 * bi + hb
                        tensor.wait_ge(s_nrm4, B + 1)
                        for j in range(4):
                            h = hb * 4 + j
                            hh = 32 * bi + h
                            if hh >= 2:
                                tensor.wait_ge(s_ptev, 2 * (hh - 2) + 2)
                            for half in (0, 1):
                                sc = ((h % 2) * 2 + half) * 128
                                mm = tensor.transpose(
                                    pT[:, sc:sc + 128],
                                    probs_sb[:, (hb % 2) * 1024 + j * 256 + half * 128: (hb % 2) * 1024 + j * 256 + half * 128 + 128],
                                    ident_sb[:, :])
                            mm.then_inc(s_tmm, 1)
                            hp = hh // 2
                            tensor.wait_ge(s_ptev, 2 * hh + 2)
                            if h % 2 == 0 and hp >= 4:
                                tensor.wait_ge(s_aoev, hp - 3)
                            vz = vre if h % 2 == 0 else vro
                            for half, sl in ((0, slp), (1, slc)):
                                sc = ((h % 2) * 2 + half) * 128
                                mm = tensor.matmul(
                                    pV[:, (hp % 4) * 128:(hp % 4) * 128 + 128],
                                    vz[:, sl * 2048 + (h // 2) * 128: sl * 2048 + (h // 2) * 128 + 128],
                                    pt_sb[:, sc:sc + 128],
                                    start=(h % 2 == 0 and half == 0),
                                    stop=(h % 2 == 1 and half == 1))
                            if h % 2 == 1:
                                mm.then_inc(s_pvp, 1)
                tensor.wait_ge(s_aoev, 16 * bi + 16)
                if b == 1:
                    tensor.wait_ge(s_wqcv, 64)
                for dc in range(16):
                    for fc in range(16):
                        mm = tensor.matmul(
                            pQ[:, dc * 128:(dc + 1) * 128],
                            wo_sb[:, fc * 2048 + dc * 128: fc * 2048 + dc * 128 + 128],
                            ao_sb[:, fc * 128:(fc + 1) * 128],
                            start=(fc == 0), stop=(fc == 15))
                mm.then_inc(s_omm, 1)

        @block.vector
        def _(vector):
            for w in range(2):
                for c in range(16):
                    for hf in range(4):
                        jj = w * 64 + c * 4 + hf
                        vector.wait_ge(s_wld[jj % 2], 16 * (jj // 2 + 1))
                        dst = wq_sb if w == 0 else wo_sb
                        vector.tensor_copy(dst[:, c * 2048 + hf * 512: c * 2048 + (hf + 1) * 512],
                                           wst.ap().rearrange("p (b f) -> p b f", b=2)[:, jj % 2]).then_inc(s_wqcv, 1)
            for b in range(1, NBLK + 1):
                bi = b - 1
                p = b % 2
                if b == 1:
                    vector.memset(qe_sb[64:128, :], 0.0)
                    vector.memset(qo_sb[0:64, :], 0.0)
                vector.wait_ge(s_qmm, bi + 1)
                if dbg is not None and bi >= 1:
                    vector.wait_ge(s_dbgst, 144 * (bi - 1) + 64)
                vector.tensor_copy(qe_sb[0:64, :], pQ[0:64, :])
                vector.tensor_copy(qo_sb[64:128, :], pQ[64:128, :]).then_inc(s_qev, 1)
                for k in range(9):
                    if k >= 1:
                        hb = k - 1
                        B = 8 * bi + hb
                        vector.wait_ge(s_exp, 4 * B + 4)
                        vector.tensor_reduce(sum_sb[:, (hb % 2) * 4:(hb % 2) * 4 + 4],
                                             probs_sb.ap().rearrange("p (a j c) -> p a j c", a=2, c=256)[:, hb % 2],
                                             AX.X, ALU.add).then_inc(s_rcp, 1)
                        vector.wait_ge(s_rcp, 2 * B + 1)
                        vector.reciprocal(rec_sb[:, (hb % 2) * 4:(hb % 2) * 4 + 4],
                                          sum_sb[:, (hb % 2) * 4:(hb % 2) * 4 + 4]).then_inc(s_rcp, 1)
                        vector.wait_ge(s_rcp, 2 * B + 2)
                        for j in range(4):
                            op = vector.tensor_scalar_mul(
                                probs_sb[:, (hb % 2) * 1024 + j * 256:(hb % 2) * 1024 + j * 256 + 256],
                                probs_sb[:, (hb % 2) * 1024 + j * 256:(hb % 2) * 1024 + j * 256 + 256],
                                rec_sb[:, (hb % 2) * 4 + j:(hb % 2) * 4 + j + 1])
                        op.then_inc(s_nrm4, 1)
                    if k <= 7:
                        hb = k
                        B = 8 * bi + hb
                        vector.wait_ge(s_smm, B + 1)
                        if B >= 1:
                            vector.wait_ge(s_exp, 4 * B)
                        msk = maskA_sb if b == 1 else maskB_sb
                        scb = pS if B % 2 == 0 else pQ
                        vector.scalar_tensor_tensor(
                            s_sb.ap().rearrange("p (j c) -> p j c", c=256),
                            scb.ap()[:, 0:1024].rearrange("p (j c) -> p j c", c=256),
                            0.125,
                            msk.ap().unsqueeze(1).broadcast_to([128, 4, 256]),
                            ALU.mult, ALU.add).then_inc(s_sev, 1)
                        vector.tensor_reduce(nm_sb[:, (hb % 2) * 4:(hb % 2) * 4 + 4],
                                             s_sb.ap().rearrange("p (j c) -> p j c", c=256),
                                             AX.X, ALU.max, negate=True).then_inc(s_max, 1)
                vector.wait_ge(s_omm, bi + 1)
                vector.wait_ge(s_x1l0, 64 * (bi + 1))
                if bi >= 1:
                    vector.wait_ge(s_x2st0, 64 * bi)
                vector.tensor_add(x2_sb[:], pQ[:], x1_sb[:]).then_inc(s_x2ev, 1)

        @block.scalar
        def _(scalar):
            for b in range(1, NBLK + 1):
                bi = b - 1
                for k in range(9):
                    if k <= 7:
                        hb = k
                        B = 8 * bi + hb
                        scalar.wait_ge(s_max, B + 1)
                        if B >= 2:
                            scalar.wait_ge(s_tmm, 4 * (B - 2) + 4)
                        if dbg is not None and hb == 2:
                            scalar.wait_ge(s_dbgst, 144 * bi + 80)
                        for j in range(4):
                            scalar.activation(
                                probs_sb[:, (hb % 2) * 1024 + j * 256:(hb % 2) * 1024 + j * 256 + 256],
                                s_sb[:, j * 256:(j + 1) * 256],
                                AF.Exp, bias=nm_sb[:, (hb % 2) * 4 + j:(hb % 2) * 4 + j + 1],
                                scale=1.0).then_inc(s_exp, 1)
                    if k < 1:
                        continue
                    hb = k - 1
                    for j in range(4):
                        h = hb * 4 + j
                        hh = 32 * bi + h
                        scalar.wait_ge(s_tmm, hh + 1)
                        if hh >= 2:
                            scalar.wait_ge(s_pvp, (hh - 2) // 2 + 1)
                        for half in (0, 1):
                            sc = ((h % 2) * 2 + half) * 128
                            scalar.copy(pt_sb[:, sc:sc + 128], pT[:, sc:sc + 128]).then_inc(s_ptev, 1)
                        if h % 2 == 1:
                            hp = hh // 2
                            scalar.wait_ge(s_pvp, 16 * bi + (h - 1) // 2 + 1)
                            pl = (h - 1) // 2
                            scalar.copy(ao_sb[:, pl * 128:(pl + 1) * 128],
                                        pV[:, (hp % 4) * 128:(hp % 4) * 128 + 128]).then_inc(s_aoev, 1)

        @block.gpsimd
        def _(g_eng):
            for b in range(1, NBLK + 1):
                bi = b - 1
                if dbg is not None:
                    qv = dbg["q_dump"].ap().rearrange("(c p) t -> p c t", p=128)
                    av = dbg["ao_dump"].ap().rearrange("(c p) t -> p c t", p=128)
                    g_eng.wait_ge(s_qev, bi + 1)
                    for qq in range(2):
                        g_eng.dma_start(qv[0:64, qq * 8:(qq + 1) * 8, bi * 128:(bi + 1) * 128],
                                        qe_sb.ap().rearrange("p (c t) -> p c t", c=16)[0:64, qq * 8:(qq + 1) * 8, :]).then_inc(s_dbgst, 16)
                    for qq in range(2):
                        g_eng.dma_start(qv[64:128, qq * 8:(qq + 1) * 8, bi * 128:(bi + 1) * 128],
                                        qo_sb.ap().rearrange("p (c t) -> p c t", c=16)[64:128, qq * 8:(qq + 1) * 8, :]).then_inc(s_dbgst, 16)
                    g_eng.wait_ge(s_nrm4, 8 * bi + 1)
                    g_eng.dma_start(dbg["pr_dump"].ap()[:, bi * 1024:(bi + 1) * 1024],
                                    probs_sb[:, 0:1024]).then_inc(s_dbgst, 16)
                    g_eng.wait_ge(s_aoev, 16 * bi + 16)
                    asv = ao_sb.ap().rearrange("p (c t) -> p c t", c=16)
                    for qq in range(4):
                        g_eng.dma_start(av[:, qq * 4:(qq + 1) * 4, bi * 128:(bi + 1) * 128],
                                        asv[:, qq * 4:(qq + 1) * 4, :]).then_inc(s_dbgst, 16)
                g_eng.wait_ge(s_x2ev, bi + 1)
                dst = x2v[:, :, bi * 128:(bi + 1) * 128]
                src = x2_sb.ap().rearrange("p (c t) -> p c t", c=16)
                for q in range(4):
                    g_eng.dma_start(dst[:, q * 4:(q + 1) * 4, :], src[:, q * 4:(q + 1) * 4, :]).then_inc(s_x2st0, 16)
            g_eng.wait_ge(s_x2st0, 64 * NBLK)
            if dbg is not None:
                g_eng.wait_ge(s_dbgst, 144 * NBLK)


def _build(taps=(), upto=99, race_detect=True):
    nc = bass.Bass("TRN2", target_bir_lowering=False, detect_race_conditions=race_detect)
    xT = nc.dram_tensor("xT", [D, T_ALL], F32, kind="ExternalInput")
    gwT = nc.dram_tensor("gwT", [D, IF], F32, kind="ExternalInput")
    uwT = nc.dram_tensor("uwT", [D, IF], F32, kind="ExternalInput")
    dwT = nc.dram_tensor("dwT", [IF, D], F32, kind="ExternalInput")
    gs = nc.dram_tensor("gs", [128, 44], F32, kind="ExternalInput")
    us = nc.dram_tensor("us", [128, 44], F32, kind="ExternalInput")
    ds = nc.dram_tensor("ds", [128, 16], F32, kind="ExternalInput")
    wqT = nc.dram_tensor("wqT", [D, D], F32, kind="ExternalInput")
    wkT = nc.dram_tensor("wkT", [D, D], F32, kind="ExternalInput")
    wvT = nc.dram_tensor("wvT", [D, D], F32, kind="ExternalInput")
    woT = nc.dram_tensor("woT", [D, D], F32, kind="ExternalInput")
    maskA = nc.dram_tensor("maskA", [128, 256], F32, kind="ExternalInput")
    maskB = nc.dram_tensor("maskB", [128, 256], F32, kind="ExternalInput")
    ident = nc.dram_tensor("ident", [128, 128], BF16, kind="ExternalInput")

    def scratch(name, shape, dt):
        kind = "ExternalOutput" if name in taps else "Internal"
        return nc.dram_tensor(name, shape, dt, kind=kind)

    h_dram = scratch("h_dram", [IF, T_ALL], BF16)
    x1_dram = scratch("x1_dram", [D, T_ALL], F32)
    k_dram = scratch("k_dram", [D, T_ALL], BF16)
    ve_dram = scratch("ve_dram", [T_ALL, D], BF16)
    vo_dram = scratch("vo_dram", [T_ALL, D], BF16)
    xn2_dram = scratch("xn2_dram", [D, T_ALL], BF16)
    x2_dram = scratch("x2_dram", [D, T_OWN], F32)
    outT = nc.dram_tensor("outT", [D, T_OWN], F32, kind="ExternalOutput")

    x_view3 = xT.ap().rearrange("(c p) t -> p c t", p=128)
    x1_view3 = x1_dram.ap().rearrange("(c p) t -> p c t", p=128)
    x2_view3 = x2_dram.ap().rearrange("(c p) t -> p c t", p=128)

    with (
        nc.sbuf_tensor("gs_sb", [128, 44], F32) as gs_sb,
        nc.sbuf_tensor("us_sb", [128, 44], F32) as us_sb,
        nc.sbuf_tensor("ds2_sb", [128, 16], F32) as ds2_sb,
        nc.sbuf_tensor("maskA_sb", [128, 256], F32) as maskA_sb,
        nc.sbuf_tensor("maskB_sb", [128, 256], F32) as maskB_sb,
        nc.sbuf_tensor("ident_sb", [128, 128], BF16) as ident_sb,
        nc.sbuf_tensor("ones128", [128, 128], BF16) as ones128,
        nc.sbuf_tensor("eps_col", [128, 1], F32) as eps_col,
    ):
        consts = dict(gs_sb=gs_sb, us_sb=us_sb, ds2_sb=ds2_sb, maskA_sb=maskA_sb,
                      maskB_sb=maskB_sb, ident_sb=ident_sb, ones128=ones128,
                      eps_col=eps_col)
        with (
            nc.semaphore("p0ld") as p0ld,
            nc.Block() as block,
        ):
            @block.sync
            def _(sync):
                sync.dma_start(out=gs_sb[:], in_=gs.ap()).then_inc(p0ld, 16)
                sync.dma_start(out=us_sb[:], in_=us.ap()).then_inc(p0ld, 16)
                sync.dma_start(out=ds2_sb[:], in_=ds.ap()).then_inc(p0ld, 16)
                sync.dma_start(out=maskA_sb[:], in_=maskA.ap()).then_inc(p0ld, 16)
                sync.dma_start(out=maskB_sb[:], in_=maskB.ap()).then_inc(p0ld, 16)
                sync.dma_start(out=ident_sb[:], in_=ident.ap()).then_inc(p0ld, 16)

            @block.vector
            def _(vector):
                vector.wait_ge(p0ld, 96)
                vector.tensor_scalar_mul(ds2_sb[:], ds2_sb[:], 0.5)
                vector.memset(ones128[:], 1.0)
                vector.memset(eps_col[:], EPS)

        if upto < 1:
            return nc
        with nc.sbuf_tensor("xn1", [128, NCH * T_ALL], BF16) as xn1_sb:
            with (
                nc.sbuf_tensor("p1_xst", [128, 2 * 4096], F32) as p1_xst,
                nc.sbuf_tensor("p1_tmp", [128, 4096], F32) as p1_tmp,
                nc.sbuf_tensor("p1_xc", [128, 4096], BF16) as p1_xc,
                nc.sbuf_tensor("p1_xsq", [128, 4096], BF16) as p1_xsq,
                nc.sbuf_tensor("p1_rows", [128, 1024], F32) as p1_rows,
                nc.psum_tensor("p1_pstat", [128, 512], F32) as p1_pstat,
            ):
                bt = dict(xst=[p1_xst.ap()[:, 0:4096], p1_xst.ap()[:, 4096:8192]],
                          tmp=p1_tmp, xc=p1_xc, xsq=p1_xsq, rows=p1_rows,
                          pstat=p1_pstat)
                _ln_phase(nc, bt, x_view3, xn1_sb, T_ALL, LN_STEPS, consts, "ln1")
            if "xn1_dump" in taps:
                xn1_dump = nc.dram_tensor("xn1_dump", [128, NCH * T_ALL], BF16, kind="ExternalOutput")
                with nc.semaphore("dmp") as sdmp:
                    _clear_sems(nc, [sdmp])
                with nc.semaphore("dmp2") as sdmp, nc.Block() as block:
                    @block.gpsimd
                    def _(g_eng):
                        for c in range(NCH):
                            g_eng.dma_start(xn1_dump.ap()[:, c * T_ALL:(c + 1) * T_ALL],
                                            xn1_sb[:, c * T_ALL:(c + 1) * T_ALL]).then_inc(sdmp, 16)
                        g_eng.wait_ge(sdmp, 16 * NCH)
            if upto < 2:
                return nc
            _gateup_phase(nc, xn1_sb, T_ALL, GROUPS, gwT, uwT, h_dram, consts, "g1")

        if upto < 3:
            return nc
        _down_phase(nc, dwT, h_dram, x_view3, x1_dram, GROUPS, consts, "d1")

        if upto < 4:
            return nc
        with (
            nc.sbuf_tensor("p3_xst", [128, 2 * 4096], F32) as p3_xst,
            nc.sbuf_tensor("p3_tmp", [128, 4096], F32) as p3_tmp,
            nc.sbuf_tensor("p3_xc", [128, 4096], BF16) as p3_xc,
            nc.sbuf_tensor("p3_xsq", [128, 4096], BF16) as p3_xsq,
            nc.sbuf_tensor("p3_rows", [128, 1024], F32) as p3_rows,
            nc.sbuf_tensor("p3_xout", [128, 2 * 4096], BF16) as p3_xout,
            nc.psum_tensor("p3_pstat", [128, 512], F32) as p3_pstat,
        ):
            bt = dict(xst=[p3_xst.ap()[:, 0:4096], p3_xst.ap()[:, 4096:8192]],
                      tmp=p3_tmp, xc=p3_xc, xsq=p3_xsq, rows=p3_rows,
                      pstat=p3_pstat)
            _ln_phase(nc, bt, x1_view3, p3_xout, T_ALL, LN_STEPS, consts, "ln2",
                      dram_out=xn2_dram)
        if upto < 5:
            return nc
        _kv_phase(nc, wkT, wvT, xn2_dram, k_dram, ve_dram, vo_dram, consts, "kv")

        if upto < 6:
            return nc
        dbg = None
        if "q_dump" in taps:
            dbg = {"q_dump": nc.dram_tensor("q_dump", [D, T_OWN], BF16, kind="ExternalOutput"),
                   "ao_dump": nc.dram_tensor("ao_dump", [D, T_OWN], BF16, kind="ExternalOutput"),
                   "pr_dump": nc.dram_tensor("pr_dump", [128, NBLK * 1024], BF16, kind="ExternalOutput")}
        _attn_phase(nc, wqT, woT, xn2_dram, k_dram, ve_dram, vo_dram, x1_dram,
                    x2_dram, consts, "at", dbg=dbg)

        if upto < 7:
            return nc
        with nc.sbuf_tensor("xn3", [128, NCH * T_OWN], BF16) as xn3_sb:
            with (
                nc.sbuf_tensor("p5_xst", [128, 2 * 4096], F32) as p5_xst,
                nc.sbuf_tensor("p5_tmp", [128, 4096], F32) as p5_tmp,
                nc.sbuf_tensor("p5_xc", [128, 4096], BF16) as p5_xc,
                nc.sbuf_tensor("p5_xsq", [128, 4096], BF16) as p5_xsq,
                nc.sbuf_tensor("p5_rows", [128, 1024], F32) as p5_rows,
                nc.psum_tensor("p5_pstat", [128, 512], F32) as p5_pstat,
            ):
                bt = dict(xst=[p5_xst.ap()[:, 0:4096], p5_xst.ap()[:, 4096:8192]],
                          tmp=p5_tmp, xc=p5_xc, xsq=p5_xsq, rows=p5_rows,
                          pstat=p5_pstat)
                _ln_phase(nc, bt, x2_view3, xn3_sb, T_OWN, LN_STEPS_OWN, consts, "ln3")
            _gateup_phase(nc, xn3_sb, T_OWN, GROUPS_OWN, gwT, uwT, h_dram, consts,
                          "g2", col_off=HALO)

        if upto < 8:
            return nc
        _down_phase(nc, dwT, h_dram, x2_view3, outT, GROUPS_OWN, consts, "d2",
                    h_col_off=HALO, out_col_off=0)

    return nc



def _prep_maps(L0_attn_x, L0_attn_Wq, L0_attn_Wk, L0_attn_Wv, L0_attn_Wo,
               L0_ffn_gate_w, L0_ffn_up_w, L0_ffn_down_w,
               L0_ffn_gate_s, L0_ffn_up_s, L0_ffn_down_s):
    x = np.asarray(L0_attn_x, np.float32)
    common = {
        "gwT": np.ascontiguousarray(np.asarray(L0_ffn_gate_w, np.float32).T),
        "uwT": np.ascontiguousarray(np.asarray(L0_ffn_up_w, np.float32).T),
        "dwT": np.ascontiguousarray(np.asarray(L0_ffn_down_w, np.float32).T),
        "gs": np.ascontiguousarray(np.asarray(L0_ffn_gate_s, np.float32).reshape(44, 128).T),
        "us": np.ascontiguousarray(np.asarray(L0_ffn_up_s, np.float32).reshape(44, 128).T),
        "ds": np.ascontiguousarray(np.asarray(L0_ffn_down_s, np.float32).reshape(16, 128).T),
        "wqT": np.ascontiguousarray(np.asarray(L0_attn_Wq, np.float32).T),
        "wkT": np.ascontiguousarray(np.asarray(L0_attn_Wk, np.float32).T),
        "wvT": np.ascontiguousarray(np.asarray(L0_attn_Wv, np.float32).T),
        "woT": np.ascontiguousarray(np.asarray(L0_attn_Wo, np.float32).T),
        "ident": np.eye(128, dtype=ml_dtypes.bfloat16),
    }
    r = np.arange(128)[:, None]
    c = np.arange(256)[None, :]
    validB = (c >= r) & (c <= r + 128)
    maskB = np.where(validB, np.float32(0.0), np.float32(NEG))
    maskA0 = np.where(validB & (c >= 128), np.float32(0.0), np.float32(NEG))
    in_maps = []
    for core in range(8):
        b, hf = core // 2, core % 2
        t0 = hf * T_OWN
        own = x[b, t0:t0 + T_OWN]
        halo = x[b, t0 - HALO:t0] if hf == 1 else np.zeros((HALO, D), np.float32)
        xT_c = np.ascontiguousarray(np.concatenate([halo, own], 0).T)
        m = dict(common)
        m["xT"] = xT_c
        m["maskA"] = maskB if hf == 1 else maskA0
        m["maskB"] = maskB
        in_maps.append(m)
    return in_maps


def kernel(**inputs):
    in_maps = _prep_maps(**inputs)
    nc = _build()
    res = run_bass_kernel_spmd(nc, in_maps, core_ids=list(range(8)))
    out = np.empty((4, 4096, D), np.float32)
    for core in range(8):
        b, hf = core // 2, core % 2
        out[b, hf * T_OWN:(hf + 1) * T_OWN] = res.results[core]["outT"].T
    return out
